# version 08
# baseline (speedup 1.0000x reference)
import numpy as np
from contextlib import ExitStack

import concourse.bass as bass
import concourse.mybir as mybir
from concourse.bass_utils import run_bass_kernel_spmd

F32 = mybir.dt.float32
BF16 = mybir.dt.bfloat16
I32 = mybir.dt.int32
AF = mybir.ActivationFunctionType
ALU = mybir.AluOpType
AXX = mybir.AxisListType.X

MAGIC = 12582912.0
NCORE = 8
T = 2048
NCH = 8
EPS0 = 1e-5


class S:
    def __init__(self, nc, es, name):
        self.h = es.enter_context(nc.semaphore(name))
        self.n = 0

    def inc(self, inst, k=1):
        inst.then_inc(self.h, k)
        self.n += k
        return self.n


class Build:
    def __init__(self, layers=(0, 1), dump=None):
        self.layers = layers
        self.dump = dump
        self.nc = bass.Bass("TRN2", target_bir_lowering=False)
        self.es = ExitStack()

    def sb(self, name, shape, dt=F32, stack=None):
        return (stack or self.es).enter_context(self.nc.sbuf_tensor(name, shape, dt))

    def psum(self, name, shape, dt=F32, stack=None):
        return (stack or self.es).enter_context(self.nc.psum_tensor(name, shape, dt))

    def V(self, eng, inst, sem=None):
        s = sem if sem is not None else self.dv1
        s.inc(inst)
        eng.wait_ge(s.h, s.n)
        return inst

    def barrier(self):
        sems = [self.pe1, self.pe2, self.dv1, self.dv2, self.ac1, self.sy, self.gp, self.cc,
                self.gpA, self.gpB, self.syA, self.syB]
        for emit in (self.block.tensor, self.block.vector, self.block.scalar,
                     self.block.sync, self.block.gpsimd):
            def _f(eng, sems=sems):
                for s in sems:
                    if s.n > 0:
                        eng.wait_ge(s.h, s.n)
            emit(_f)

    def build(self):
        nc = self.nc
        es = self.es
        dram = {}

        def din(name, shape, dt=F32):
            dram[name] = nc.dram_tensor(name, shape, dt, kind="ExternalInput").ap()
            return dram[name]

        l0 = 0 in self.layers
        l1 = 1 in self.layers
        din("xT", [128, NCH * T])
        if l0:
            din("kT", [128, 2 * 4096]); din("V", [128, 32 * 256])
            din("lng", [128, 8]); din("lnb", [128, 8])
            din("W1A", [128, 8 * 512]); din("W1B", [64, 8 * 4096])
            din("W2A", [128, 8 * 32 * 64]); din("W2B", [128, 4 * 1024])
            din("ew", [1, 2048]); din("ei", [1, 2048], I32)
        if l1:
            din("wqT", [128, 8 * 1024]); din("wkT", [128, 8 * 1024])
            din("wvT", [128, 8 * 1024]); din("woT", [128, 8 * 1024])
            din("bq", [128, 8]); din("bo", [128, 8])
            din("bkr", [1, 1024]); din("bvr", [1, 1024])
            din("W1f", [128, 8 * 4096]); din("W2f", [128, 32 * 1024])
            din("b1", [128, 32]); din("b2", [128, 8])
        out = nc.dram_tensor("outT", [128, NCH * T], F32, kind="ExternalOutput").ap()
        dbg = {}
        if self.dump:
            dbg["dXN"] = nc.dram_tensor("dXN", [128, NCH * T], BF16, kind="ExternalOutput").ap()
        if l1:
            cc_in = nc.dram_tensor("cc_in", [64, 16 * 130], F32, kind="Internal").ap()
            cc_out = nc.dram_tensor("cc_out", [64, 16 * 130], F32, kind="Internal").ap()
            w1t_dram = nc.dram_tensor("w1t_dram", [128, 8 * 4096], BF16, kind="Internal").ap()
            w2t_dram = nc.dram_tensor("w2t_dram", [128, 32 * 1024], BF16, kind="Internal").ap()

        with es:
            ACC = self.sb("ACC", [128, NCH * T], F32)
            XN = self.sb("XN", [128, NCH * T], BF16)
            self._ones = self.sb("ones", [128, 1], F32)
            self._onesr = self.sb("onesr", [1, 128], F32)
            self._bcp2 = self.sb("bcp2", [2, 128], F32)
            smalls = []
            lng = lnb = bq = bo = b1 = b2 = None
            if l0:
                lng = self.sb("lng_t", [128, 8]); lnb = self.sb("lnb_t", [128, 8])
                smalls += [("lng", lng), ("lnb", lnb)]
            if l1:
                bq = self.sb("bq_t", [128, 8]); bo = self.sb("bo_t", [128, 8])
                b1 = self.sb("b1_t", [128, 32]); b2 = self.sb("b2_t", [128, 8])
                smalls += [("bq", bq), ("bo", bo), ("b1", b1), ("b2", b2)]

            self.pe1 = S(nc, es, "pe1"); self.pe2 = S(nc, es, "pe2")
            self.dv1 = S(nc, es, "dv1"); self.dv2 = S(nc, es, "dv2")
            self.ac1 = S(nc, es, "ac1")
            self.sy = S(nc, es, "sy"); self.gp = S(nc, es, "gp")
            self.cc = S(nc, es, "cc"); self.st = S(nc, es, "st")
            self.gpA = S(nc, es, "gpA"); self.gpB = S(nc, es, "gpB")
            self.syA = S(nc, es, "syA"); self.syB = S(nc, es, "syB")
            sy, gp, st = self.sy, self.gp, self.st

            Xv = [ACC[:, T * i:T * (i + 1)] for i in range(NCH)]
            XNv = [XN[:, T * i:T * (i + 1)] for i in range(NCH)]

            blk_cm = nc.Block()
            self.block = block = blk_cm.__enter__()
            try:
                def p0_sync(sync):
                    for j in range(4):
                        sy.inc(sync.dma_start(out=ACC[:, 4096 * j:4096 * (j + 1)],
                                              in_=dram["xT"][:, 4096 * j:4096 * (j + 1)]), 16)
                block.sync(p0_sync)

                def p0_gp(g):
                    for nm, t_ in smalls:
                        gp.inc(g.dma_start(out=t_[:], in_=dram[nm][:]), 16)
                block.gpsimd(p0_gp)

                def p0_v(v):
                    v.memset(self._ones[:], 1.0)
                    self.dv1.inc(v.memset(self._onesr[:], 1.0))
                block.vector(p0_v)

                bcp_np = np.zeros((2, 128), dtype=np.float32)
                bcp_np[0, 0:64] = 1.0
                bcp_np[1, 64:128] = 1.0
                bcp_const = nc.inline_tensor(bcp_np, name="bcp_const").ap()

                def p0_gp2(g):
                    gp.inc(g.dma_start(out=self._bcp2[:], in_=bcp_const[:]), 16)
                block.gpsimd(p0_gp2)

                self.barrier()

                if l0:
                    with ExitStack() as l0scope:
                        self._keep_stack = l0scope
                        self.ln_phase(Xv, XNv, fold=True)
                        self.barrier()
                        if self.dump == "p1":
                            self.finish(out, ACC, XN, dbg); return nc
                        self.l0_attn(Xv, XNv, dram, lng, lnb)
                        self.barrier()
                    if self.dump == "p3":
                        self.finish(out, ACC, XN, dbg); return nc
                    self.moe(Xv, XNv, dram)
                    self.barrier()
                if not l1 or self.dump == "p4":
                    self.finish(out, ACC, XN, dbg); return nc

                self.ternarize(dram, w1t_dram, w2t_dram)
                if self.dump == "p4b":
                    dW1 = nc.dram_tensor("dW1t", [128, 8 * 4096], BF16, kind="ExternalOutput").ap()
                    dW2 = nc.dram_tensor("dW2t", [128, 32 * 1024], BF16, kind="ExternalOutput").ap()
                    def f_tdump(sync):
                        for s_ in [self.pe1, self.pe2, self.dv1, self.dv2, self.ac1,
                                   self.sy, self.gp, self.cc,
                                   self.gpA, self.gpB, self.syA, self.syB]:
                            if s_.n > 0:
                                sync.wait_ge(s_.h, s_.n)
                        self.st.inc(sync.dma_start(out=dW1[:], in_=w1t_dram[:]), 16)
                        self.st.inc(sync.dma_start(out=dW2[:], in_=w2t_dram[:]), 16)
                        sync.wait_ge(self.st.h, self.st.n)
                    block.sync(f_tdump)
                    self.finish(out, ACC, XN, dbg); return nc
                self.ln_phase(Xv, XNv, fold=False)
                self.barrier()
                if self.dump == "p5":
                    self.finish(out, ACC, XN, dbg); return nc
                self.l1_attn(Xv, XNv, dram, bq, bo, cc_in, cc_out)
                self.barrier()
                if self.dump == "p67":
                    self.finish(out, ACC, XN, dbg); return nc
                self.l1_ffn(Xv, XNv, dram, b1, b2, w1t_dram, w2t_dram, out)
            finally:
                blk_cm.__exit__(None, None, None)
        return nc

    def finish(self, out, ACC, XN, dbg, tern=None):
        st = self.st
        sems = [self.pe1, self.pe2, self.dv1, self.dv2, self.ac1, self.sy, self.gp, self.cc,
                self.gpA, self.gpB, self.syA, self.syB]

        def f(sync):
            for s_ in sems:
                if s_.n > 0:
                    sync.wait_ge(s_.h, s_.n)
            for j in range(4):
                st.inc(sync.dma_start(out=out[:, 4096 * j:4096 * (j + 1)],
                                      in_=ACC[:, 4096 * j:4096 * (j + 1)]), 16)
            if dbg:
                st.inc(sync.dma_start(out=dbg["dXN"][:], in_=XN[:]), 16)
            sync.wait_ge(st.h, st.n)
        self.block.sync(f)
        if tern is not None and dbg:
            pass

    def ln_phase(self, Xv, XNv, fold):
        block = self.block
        pe1, pe2, dv1, ac1 = self.pe1, self.pe2, self.dv1, self.ac1
        ones, onesr = self._ones, self._onesr
        sfx = "_f" if fold else "_n"
        if fold:
            self._r_var = self.sb("r_var_keep", [1, T], stack=self._keep_stack)
            self._r_rstd = self.sb("r_rstd_keep", [1, T], stack=self._keep_stack)
        with ExitStack() as ph:
            r_mean = self.sb("r_mean" + sfx, [1, T], stack=ph)
            r_var = self._r_var if fold else self.sb("r_var" + sfx, [1, T], stack=ph)
            r_rstd = self._r_rstd if fold else self.sb("r_rstd" + sfx, [1, T], stack=ph)
            r_a = self.sb("r_a" + sfx, [1, T], stack=ph)
            r_tmp = self.sb("r_tmp" + sfx, [1, T], stack=ph)
            SQ = self.sb("SQ" + sfx, [128, 2 * 512], stack=ph)
            N1A = self.sb("N1A" + sfx, [128, 8 * 512], stack=ph)
            pb_sum = [self.psum("pb_sum%s%d" % (sfx, k), [1, 512], stack=ph) for k in range(2)]
            pb_sq = [self.psum("pb_sq%s%d" % (sfx, k), [1, 512], stack=ph) for k in range(2)]
            pb_r = [self.psum("pb_r%s%d" % (sfx, k), [128, 512], stack=ph) for k in range(2)]
            pb_a = [self.psum("pb_a%s%d" % (sfx, k), [128, 512], stack=ph) for k in range(2)]

            rows_done = {}; norm_done = {}; stats_done = {}
            for tt in range(4):
                sl = slice(512 * tt, 512 * (tt + 1))
                par = tt % 2
                ac_b = ac1.n; pe_b = pe1.n

                def f_act(sc, sl=sl, ac_b=ac_b, pe_b=pe_b, tt=tt):
                    for i in range(NCH):
                        if tt > 0 and i < 2:
                            sc.wait_ge(pe1.h, pe_b)
                        if i >= 2:
                            sc.wait_ge(pe1.h, pe_b + i - 1)
                        ac1.inc(sc.activation(out=SQ[:, 512 * (i % 2):512 * (i % 2 + 1)],
                                              in_=Xv[i][:, sl], func=AF.Square))
                block.scalar(f_act)

                def f_pe(t, sl=sl, par=par, ac_b=ac_b, tt=tt):
                    if tt >= 2:
                        t.wait_ge(dv1.h, rows_done[tt - 2])
                    for i in range(NCH):
                        t.matmul(pb_sum[par][:], ones[:], Xv[i][:, sl],
                                 start=(i == 0), stop=(i == NCH - 1))
                        t.wait_ge(ac1.h, ac_b + i + 1)
                        pe1.inc(t.matmul(pb_sq[par][:], ones[:],
                                         SQ[:, 512 * (i % 2):512 * (i % 2 + 1)],
                                         start=(i == 0), stop=(i == NCH - 1)))
                block.tensor(f_pe)
                stats_done[tt] = pe1.n

                def f_rows(v, sl=sl, par=par, pe_s=stats_done[tt]):
                    v.wait_ge(pe1.h, pe_s)
                    self.V(v, v.tensor_scalar(out=r_mean[:, sl], in0=pb_sum[par][:],
                                              scalar1=1.0 / 1024, scalar2=None, op0=ALU.mult))
                    self.V(v, v.tensor_scalar(out=r_var[:, sl], in0=pb_sq[par][:],
                                              scalar1=1.0 / 1024, scalar2=None, op0=ALU.mult))
                    self.V(v, v.tensor_tensor(out=r_tmp[:, sl], in0=r_mean[:, sl],
                                              in1=r_mean[:, sl], op=ALU.mult))
                    self.V(v, v.tensor_tensor(out=r_var[:, sl], in0=r_var[:, sl],
                                              in1=r_tmp[:, sl], op=ALU.subtract))
                    self.V(v, v.tensor_scalar(out=r_tmp[:, sl], in0=r_var[:, sl],
                                              scalar1=EPS0, scalar2=None, op0=ALU.add))
                block.vector(f_rows)
                rows_done[tt] = dv1.n

                def f_sqrt(sc, sl=sl, dv_r1=rows_done[tt]):
                    sc.wait_ge(dv1.h, dv_r1)
                    self.V(sc, sc.activation(out=r_tmp[:, sl], in_=r_tmp[:, sl], func=AF.Ln),
                           sem=ac1)
                    ac1.inc(sc.activation(out=r_rstd[:, sl], in_=r_tmp[:, sl], func=AF.Exp,
                                          scale=-0.5))
                block.scalar(f_sqrt)
                ac_r1 = ac1.n

                def f_rows2(v, sl=sl, ac_r1=ac_r1):
                    v.wait_ge(ac1.h, ac_r1)
                    self.V(v, v.scalar_tensor_tensor(out=r_a[:, sl], in0=r_mean[:, sl],
                                                     scalar=-1.0, in1=r_rstd[:, sl],
                                                     op0=ALU.mult, op1=ALU.mult))
                block.vector(f_rows2)
                dv_r2 = dv1.n

                def f_bc(t, sl=sl, par=par, dv_r2=dv_r2, tt=tt):
                    t.wait_ge(dv1.h, dv_r2)
                    if tt >= 2:
                        t.wait_ge(dv1.h, norm_done[tt - 2])
                    t.matmul(pb_r[par][:], onesr[:], r_rstd[:, sl], start=True, stop=True)
                    pe2.inc(t.matmul(pb_a[par][:], onesr[:], r_a[:, sl], start=True, stop=True))
                block.tensor(f_bc)
                pe2_r = pe2.n

                def f_norm(v, sl=sl, par=par, pe2_r=pe2_r):
                    v.wait_ge(pe2.h, pe2_r)
                    for i in range(NCH):
                        inst = v.tensor_tensor(out=N1A[:, 512 * i:512 * (i + 1)],
                                               in0=Xv[i][:, sl], in1=pb_r[par][:], op=ALU.mult)
                        if i == NCH - 1:
                            dv1.inc(inst); v.wait_ge(dv1.h, dv1.n)
                    for i in range(NCH):
                        inst = v.tensor_tensor(out=XNv[i][:, sl], in0=N1A[:, 512 * i:512 * (i + 1)],
                                               in1=pb_a[par][:], op=ALU.add)
                        if i == NCH - 1:
                            dv1.inc(inst); v.wait_ge(dv1.h, dv1.n)
                    if fold:
                        for i in range(NCH):
                            inst = v.tensor_tensor(out=Xv[i][:, sl], in0=Xv[i][:, sl],
                                                   in1=XNv[i][:, sl], op=ALU.add)
                            if i == NCH - 1:
                                dv1.inc(inst); v.wait_ge(dv1.h, dv1.n)
                block.vector(f_norm)
                norm_done[tt] = dv1.n

    def l0_attn(self, Xv, XNv, dram, lng, lnb):
        block = self.block
        pe1, pe2, dv1, dv2, ac1, gp = self.pe1, self.pe2, self.dv1, self.dv2, self.ac1, self.gp
        onesr = self._onesr
        with ExitStack() as ph:
            X2 = self.sb("X2", [128, NCH * T], BF16, stack=ph)
            X2v = [X2[:, T * i:T * (i + 1)] for i in range(NCH)]
            r_rstd2 = self.sb("r_rstd2", [1, T], stack=ph)
            r_t2 = self.sb("r_t2", [1, T], stack=ph)
            N1 = self.sb("N1b", [128, 512], stack=ph)
            kTt = self.sb("kTt", [128, 2 * 4096], BF16, stack=ph)
            Vt = self.sb("Vt", [128, 32 * 256], BF16, stack=ph)
            RB = self.sb("RB", [128, 4 * 512], stack=ph)
            PT = self.sb("PT", [128, 4 * 512], BF16, stack=ph)

            def f_load(g):
                gp.inc(g.dma_start(out=kTt[:], in_=dram["kT"][:]), 16)
                gp.inc(g.dma_start(out=Vt[:], in_=dram["V"][:]), 16)
            block.gpsimd(f_load)
            gp_w = gp.n

            def f_rows(v):
                self.V(v, v.scalar_tensor_tensor(out=r_t2[:], in0=self._r_rstd[:], scalar=1.0,
                                                 in1=self._r_rstd[:], op0=ALU.mult, op1=ALU.mult))
                self.V(v, v.tensor_tensor(out=r_t2[:], in0=self._r_var[:], in1=r_t2[:], op=ALU.mult))
                self.V(v, v.tensor_scalar(out=r_t2[:], in0=r_t2[:], scalar1=EPS0,
                                          scalar2=None, op0=ALU.add))
            block.vector(f_rows)
            dv_r = dv1.n

            def f_sq(sc, dv_r=dv_r):
                sc.wait_ge(dv1.h, dv_r)
                self.V(sc, sc.activation(out=r_t2[:], in_=r_t2[:], func=AF.Ln), sem=ac1)
                ac1.inc(sc.activation(out=r_rstd2[:], in_=r_t2[:], func=AF.Exp, scale=-0.5))
            block.scalar(f_sq)
            ac_r = ac1.n
            dvn0 = dv1.n

            x2scope = ExitStack()
            pb_r = self.psum("pb_r2", [128, 512], stack=x2scope)
            for tt in range(4):
                sl = slice(512 * tt, 512 * (tt + 1))
                dvn = dv1.n
                def f_bc(t, sl=sl, dvn=dvn, ac_r=ac_r):
                    t.wait_ge(dv1.h, dvn)
                    t.wait_ge(ac1.h, ac_r)
                    pe2.inc(t.matmul(pb_r[:], onesr[:], r_rstd2[:, sl], start=True, stop=True))
                block.tensor(f_bc)
                pe2_r = pe2.n
                def f_norm(v, sl=sl, pe2_r=pe2_r):
                    v.wait_ge(pe2.h, pe2_r)
                    for i in range(NCH):
                        self.V(v, v.tensor_tensor(out=N1[:], in0=XNv[i][:, sl], in1=pb_r[:], op=ALU.mult))
                        self.V(v, v.tensor_scalar(out=X2v[i][:, sl], in0=N1[:],
                                                  scalar1=lng[:, i:i + 1], scalar2=lnb[:, i:i + 1],
                                                  op0=ALU.mult, op1=ALU.add))
                block.vector(f_norm)

            x2scope.close()
            ps_s = [self.psum("ps_s%d" % k, [128, 512], stack=ph) for k in range(4)]
            ps_o = [self.psum("ps_o%d" % k, [128, 512], stack=ph) for k in range(4)]

            dv2_0 = dv2.n
            it = 0
            for h in range(4):
                for tt in range(4):
                    sl = slice(512 * tt, 512 * (tt + 1))
                    oset = (it % 2) * 2
                    po = (ps_o[oset], ps_o[oset + 1])
                    pe_b = pe1.n; pe2_b = pe2.n; dv_b = dv1.n; ac_b = ac1.n

                    def f_pe(t, h=h, sl=sl, po=po, dv_b=dv_b, ac_b=ac_b, it=it):
                        if it == 0:
                            t.wait_ge(gp.h, gp_w)
                            t.wait_ge(dv1.h, dv_b)
                        elif it >= 2:
                            t.wait_ge(dv2.h, dv2_0 + it - 1)
                        for s in range(32):
                            if s >= 4:
                                t.wait_ge(dv1.h, dv_b + s - 3)
                            t.matmul(ps_s[s % 4][:], kTt[:, 128 * s:128 * (s + 1)],
                                     X2v[2 * h][:, sl], start=True, stop=False)
                            pe1.inc(t.matmul(ps_s[s % 4][:],
                                             kTt[:, 4096 + 128 * s:4096 + 128 * (s + 1)],
                                             X2v[2 * h + 1][:, sl], start=False, stop=True))
                            if s >= 3:
                                sp = s - 3
                                t.wait_ge(ac1.h, ac_b + sp + 1)
                                t.matmul(po[0][:], Vt[:, 256 * sp:256 * sp + 128],
                                         PT[:, 512 * (sp % 4):512 * (sp % 4 + 1)],
                                         start=(sp == 0), stop=False)
                                pe2.inc(t.matmul(po[1][:], Vt[:, 256 * sp + 128:256 * (sp + 1)],
                                                 PT[:, 512 * (sp % 4):512 * (sp % 4 + 1)],
                                                 start=(sp == 0), stop=False))
                        for sp in (29, 30, 31):
                            t.wait_ge(ac1.h, ac_b + sp + 1)
                            t.matmul(po[0][:], Vt[:, 256 * sp:256 * sp + 128],
                                     PT[:, 512 * (sp % 4):512 * (sp % 4 + 1)],
                                     start=False, stop=(sp == 31))
                            pe2.inc(t.matmul(po[1][:], Vt[:, 256 * sp + 128:256 * (sp + 1)],
                                             PT[:, 512 * (sp % 4):512 * (sp % 4 + 1)],
                                             start=False, stop=(sp == 31)))
                    block.tensor(f_pe)

                    def f_dv(v, pe_b=pe_b, ac_b=ac_b):
                        for s in range(32):
                            v.wait_ge(pe1.h, pe_b + s + 1)
                            if s >= 4:
                                v.wait_ge(ac1.h, ac_b + s - 3)
                            dv1.inc(v.tensor_scalar(out=RB[:, 512 * (s % 4):512 * (s % 4 + 1)],
                                                    in0=ps_s[s % 4][:], scalar1=0.0,
                                                    scalar2=None, op0=ALU.max))
                    block.vector(f_dv)

                    def f_ac(sc, dv_b=dv_b, pe2_b=pe2_b):
                        for s in range(32):
                            sc.wait_ge(dv1.h, dv_b + s + 1)
                            if s >= 4:
                                sc.wait_ge(pe2.h, pe2_b + s - 3)
                            ac1.inc(sc.activation(out=PT[:, 512 * (s % 4):512 * (s % 4 + 1)],
                                                  in_=RB[:, 512 * (s % 4):512 * (s % 4 + 1)],
                                                  func=AF.Square))
                    block.scalar(f_ac)

                    def f_evac(v, h=h, sl=sl, po=po, pe2_b=pe2_b):
                        v.wait_ge(pe2.h, pe2_b + 32)
                        v.tensor_tensor(out=Xv[2 * h][:, sl], in0=Xv[2 * h][:, sl],
                                        in1=po[0][:], op=ALU.add)
                        dv2.inc(v.tensor_tensor(out=Xv[2 * h + 1][:, sl], in0=Xv[2 * h + 1][:, sl],
                                                in1=po[1][:], op=ALU.add))
                    block.vector(f_evac)
                    it += 1

    def moe(self, Xv, XNv, dram):
        block = self.block
        pe1, pe2, dv1, dv2, ac1, gp = self.pe1, self.pe2, self.dv1, self.dv2, self.ac1, self.gp
        onesr = self._onesr
        with ExitStack() as ph:
            W1At = self.sb("W1At", [128, 8 * 512], BF16, stack=ph)
            W1Bt = [self.sb("W1Bt%d" % k, [64, 4096], BF16, stack=ph) for k in range(2)]
            W2Bt = self.sb("W2Bt", [128, 4 * 1024], BF16, stack=ph)
            W2At = [self.sb("W2At%d" % k, [128, 32 * 64], BF16, stack=ph) for k in range(2)]
            Zt = self.sb("Zt", [64, 8 * 1024], BF16, stack=ph)
            Ht = self.sb("Ht", [128, 4 * 512], BF16, stack=ph)
            Gt = self.sb("Gt", [128, 4 * 1024], BF16, stack=ph)
            GS = self.sb("GS", [64, 1024], stack=ph)
            ewt = self.sb("ewt", [1, 2048], stack=ph)
            eit = self.sb("eit", [1, 2048], stack=ph)
            cfm = self.sb("cfm", [1, 2048], stack=ph)
            cfr = self.sb("cfr", [1, 1024], stack=ph)

            gpA, gpB = self.gpA, self.gpB

            def f_load(g):
                gp.inc(g.dma_start(out=W1At[:], in_=dram["W1A"][:]), 16)
                gp.inc(g.dma_start(out=W2Bt[:], in_=dram["W2B"][:]), 16)
                gp.inc(g.dma_start(out=ewt[:], in_=dram["ew"][:]), 16)
                gp.inc(g.dma_start(out=eit[:], in_=dram["ei"][:]), 16)
                gpA.inc(g.dma_start(out=W1Bt[0][:], in_=dram["W1B"][:, 0:4096]), 16)
                gpA.inc(g.dma_start(out=W2At[0][:], in_=dram["W2A"][:, 0:2048]), 16)
            block.gpsimd(f_load)
            gp_b = gp.n
            gpw_ready = {0: (gpA, gpA.n)}

            with ExitStack() as zp:
                pz = [self.psum("pz%d" % k, [128, 512], stack=zp) for k in range(2)]
                dvz0 = dv2.n
                for n, (rc, th) in enumerate([(rc, th) for rc in range(4) for th in range(2)]):
                    buf = n % 2
                    def f_pe(t, rc=rc, th=th, buf=buf, n=n):
                        if n == 0:
                            t.wait_ge(gp.h, gp_b)
                        elif n >= 2:
                            t.wait_ge(dv2.h, dvz0 + n - 1)
                        for i in range(NCH):
                            mm = t.matmul(pz[buf][:],
                                          W1At[:, 512 * i + 128 * rc:512 * i + 128 * (rc + 1)],
                                          XNv[i][:, 512 * th:512 * (th + 1)],
                                          start=(i == 0), stop=(i == NCH - 1))
                        pe1.inc(mm)
                    block.tensor(f_pe)
                    pe_e = pe1.n
                    def f_ev(v, rc=rc, th=th, buf=buf, pe_e=pe_e):
                        v.wait_ge(pe1.h, pe_e)
                        c0 = 1024 * (2 * rc) + 512 * th
                        c1 = 1024 * (2 * rc + 1) + 512 * th
                        v.tensor_copy(out=Zt[:, c0:c0 + 512], in_=pz[buf][0:64, :])
                        dv2.inc(v.tensor_copy(out=Zt[:, c1:c1 + 512], in_=pz[buf][64:128, :]))
                    block.vector(f_ev)

            with ExitStack() as ep:
                ph_ = [self.psum("ph%d" % k, [128, 512], stack=ep) for k in range(4)]
                pg = [self.psum("pg%d" % k, [64, 512], stack=ep) for k in range(2)]
                pc = [self.psum("pc%d" % k, [64, 512], stack=ep) for k in range(2)]
                for e in range(8):
                    if e < 7:
                        dv2_s = dv2.n
                        par = gpB if (e + 1) % 2 else gpA
                        def f_pre(g, e=e, dv2_s=dv2_s, par=par):
                            g.wait_ge(dv2.h, dv2_s)
                            par.inc(g.dma_start(out=W1Bt[(e + 1) % 2][:],
                                                in_=dram["W1B"][:, 4096 * (e + 1):4096 * (e + 2)]), 16)
                            par.inc(g.dma_start(out=W2At[(e + 1) % 2][:],
                                                in_=dram["W2A"][:, 2048 * (e + 1):2048 * (e + 2)]), 16)
                        block.gpsimd(f_pre)
                        gpw_ready[e + 1] = (par, par.n)

                    def f_coef(v, e=e):
                        self.V(v, v.tensor_scalar(out=cfm[:], in0=eit[:], scalar1=float(e),
                                                  scalar2=None, op0=ALU.is_equal))
                        self.V(v, v.tensor_tensor(out=cfm[:], in0=cfm[:], in1=ewt[:], op=ALU.mult))
                        cv = cfm.rearrange("a (t k) -> a t k", k=2)
                        cfrv = cfr.rearrange("a (t k) -> a t k", k=1)
                        self.V(v, v.tensor_tensor(out=cfrv[:, :, :], in0=cv[:, :, 0:1],
                                                  in1=cv[:, :, 1:2], op=ALU.add))
                    block.vector(f_coef)
                    dv_cf = dv1.n

                    gpw_sem, gpw_n = gpw_ready[e]
                    dv2_e = dv2.n
                    for th in range(2):
                        pe_b = pe1.n; ac_b = ac1.n; pe2_s = pe2.n

                        def f_pe(t, e=e, th=th, ac_b=ac_b, gpw_sem=gpw_sem, gpw_n=gpw_n, dv2_e=dv2_e):
                            if th == 0:
                                t.wait_ge(gpw_sem.h, gpw_n)
                                t.wait_ge(dv2.h, dv2_e)

                            def g_mm(f):
                                t.wait_ge(ac1.h, ac_b + f + 1)
                                pe2.inc(t.matmul(pg[th][:],
                                                 W2At[e % 2][:, 64 * f:64 * (f + 1)],
                                                 Ht[:, 512 * (f % 4):512 * (f % 4 + 1)],
                                                 start=(f == 0), stop=(f == 31)))
                            for f in range(32):
                                t.wait_ge(ac1.h, ac_b + max(0, f - 3))
                                pe1.inc(t.matmul(ph_[f % 4][:],
                                                 W1Bt[e % 2][:, 128 * f:128 * (f + 1)],
                                                 Zt[:, 1024 * e + 512 * th:1024 * e + 512 * (th + 1)],
                                                 start=True, stop=True))
                                if f >= 3:
                                    g_mm(f - 3)
                            g_mm(29); g_mm(30); g_mm(31)
                        block.tensor(f_pe)

                        def f_gelu(sc, pe_b=pe_b, pe2_s=pe2_s):
                            for f in range(32):
                                sc.wait_ge(pe1.h, pe_b + f + 1)
                                sc.wait_ge(pe2.h, pe2_s + max(0, f - 3))
                                ac1.inc(sc.activation(out=Ht[:, 512 * (f % 4):512 * (f % 4 + 1)],
                                                      in_=ph_[f % 4][:], func=AF.Gelu))
                        block.scalar(f_gelu)

                    pe2_gdone = pe2.n
                    def f_bc(t, dv_cf=dv_cf, dv2_e=dv2_e):
                        t.wait_ge(dv1.h, dv_cf)
                        t.wait_ge(dv2.h, dv2_e)
                        for th in range(2):
                            pe1.inc(t.matmul(pc[th][:], onesr[:, 0:64],
                                             cfr[:, 512 * th:512 * (th + 1)],
                                             start=True, stop=True))
                    block.tensor(f_bc)
                    pe_bc = pe1.n

                    def f_gs(sc, pe2_gdone=pe2_gdone):
                        sc.wait_ge(pe2.h, pe2_gdone)
                        for th in range(2):
                            ac1.inc(sc.activation(out=GS[:, 512 * th:512 * (th + 1)],
                                                  in_=pg[th][:], func=AF.Copy))
                    block.scalar(f_gs)
                    ac_gs = ac1.n

                    def f_g(v, e=e, ac_gs=ac_gs, pe_bc=pe_bc):
                        v.wait_ge(ac1.h, ac_gs)
                        v.wait_ge(pe1.h, pe_bc)
                        c = e // 2
                        r0 = 64 * (e % 2)
                        for th in range(2):
                            dv2.inc(v.tensor_tensor(
                                out=Gt[r0:r0 + 64, 1024 * c + 512 * th:1024 * c + 512 * (th + 1)],
                                in0=GS[:, 512 * th:512 * (th + 1)], in1=pc[th][:], op=ALU.mult))
                    block.vector(f_g)

            with ExitStack() as yp:
                py = [self.psum("py%d" % k, [128, 512], stack=yp) for k in range(2)]
                dvg = dv2.n
                for n, (i, th) in enumerate([(i, th) for i in range(NCH) for th in range(2)]):
                    buf = n % 2
                    def f_pe(t, i=i, th=th, buf=buf, n=n, dvg=dvg):
                        if n == 0:
                            t.wait_ge(dv2.h, dvg)
                        elif n >= 2:
                            t.wait_ge(dv2.h, dvg + n - 1)
                        for rc in range(4):
                            mm = t.matmul(py[buf][:],
                                          W2Bt[:, 1024 * rc + 128 * i:1024 * rc + 128 * (i + 1)],
                                          Gt[:, 1024 * rc + 512 * th:1024 * rc + 512 * (th + 1)],
                                          start=(rc == 0), stop=(rc == 3))
                        pe1.inc(mm)
                    block.tensor(f_pe)
                    pe_y = pe1.n
                    def f_ev(v, i=i, th=th, buf=buf, pe_y=pe_y):
                        v.wait_ge(pe1.h, pe_y)
                        dv2.inc(v.tensor_tensor(out=Xv[i][:, 512 * th:512 * (th + 1)],
                                                in0=Xv[i][:, 512 * th:512 * (th + 1)],
                                                in1=py[buf][:], op=ALU.add))
                    block.vector(f_ev)

    def ternarize(self, dram, w1t_dram, w2t_dram):
        block = self.block
        pe1, dv1, ac1, sy, gp = self.pe1, self.dv1, self.ac1, self.sy, self.gp
        ones, onesr = self._ones, self._onesr
        with ExitStack() as ph:
            WS = [self.sb("WS%d" % k, [128, 4096], stack=ph) for k in range(4)]
            T1 = self.sb("T1", [128, 4096], stack=ph)
            TB = [self.sb("TB%d" % k, [128, 4096], BF16, stack=ph) for k in range(2)]
            parts = self.sb("parts", [128, 8], stack=ph)
            accp = self.sb("accp", [128, 1], stack=ph)
            sc1 = self.sb("sc1", [1, 1], stack=ph)
            rS = self.sb("rS", [128, 1], stack=ph)
            p1 = self.psum("pt1", [1, 512], stack=ph)
            p2 = self.psum("pt2", [128, 512], stack=ph)

            syp = [self.syA, self.syB, self.gpA, self.gpB]
            gpp = [self.gpA, self.gpB]
            for (src, dst) in ((dram["W1f"], w1t_dram), (dram["W2f"], w2t_dram)):
                nch = 8
                dv_b = dv1.n
                ld_n = {}
                def f_ld(s, src=src, dv_b=dv_b, ld_n=ld_n):
                    for c in range(nch):
                        if c >= 4:
                            s.wait_ge(dv1.h, dv_b + c - 3)
                        syp[c % 4].inc(s.dma_start(out=WS[c % 4][:],
                                                   in_=src[:, 4096 * c:4096 * (c + 1)]), 16)
                        ld_n[c] = syp[c % 4].n
                block.sync(f_ld)

                def f_red(v, ld_n=ld_n):
                    for c in range(nch):
                        v.wait_ge(syp[c % 4].h, ld_n[c])
                        dv1.inc(v.tensor_reduce(out=parts[:, c:c + 1], in_=WS[c % 4][:],
                                                axis=AXX, op=ALU.add, apply_absolute_value=True))
                    v.wait_ge(dv1.h, dv1.n)
                    self.V(v, v.tensor_reduce(out=accp[:], in_=parts[:], axis=AXX, op=ALU.add))
                block.vector(f_red)
                dv_red = dv1.n

                def f_pe(t, dv_red=dv_red):
                    t.wait_ge(dv1.h, dv_red)
                    pe1.inc(t.matmul(p1[:, 0:1], accp[:], ones[:], start=True, stop=True))
                block.tensor(f_pe)
                pe_s = pe1.n

                def f_sc(v, pe_s=pe_s):
                    v.wait_ge(pe1.h, pe_s)
                    self.V(v, v.tensor_scalar(out=sc1[:], in0=p1[0:1, 0:1], scalar1=1.0 / (nch * 4096 * 128),
                                              scalar2=1e-8, op0=ALU.mult, op1=ALU.max))
                    self.V(v, v.reciprocal(out=sc1[:], in_=sc1[:]))
                block.vector(f_sc)
                dv_sc = dv1.n

                def f_bc(t, dv_sc=dv_sc):
                    t.wait_ge(dv1.h, dv_sc)
                    pe1.inc(t.matmul(p2[:, 0:1], onesr[:], sc1[:], start=True, stop=True))
                block.tensor(f_bc)
                pe_s2 = pe1.n

                def f_cp(v, pe_s2=pe_s2):
                    v.wait_ge(pe1.h, pe_s2)
                    self.V(v, v.tensor_copy(out=rS[:], in_=p2[:, 0:1]))
                block.vector(f_cp)
                dv_cp = dv1.n

                ac_b = ac1.n; dv_b5 = dv1.n
                ld2_n = {}
                def f_ld2(s, src=src, ac_b=ac_b, dv_cp=dv_cp, ld2_n=ld2_n):
                    for c in range(nch):
                        if c < 4:
                            s.wait_ge(dv1.h, dv_cp)
                        else:
                            s.wait_ge(ac1.h, ac_b + c - 3)
                        syp[c % 4].inc(s.dma_start(out=WS[c % 4][:],
                                                   in_=src[:, 4096 * c:4096 * (c + 1)]), 16)
                        ld2_n[c] = syp[c % 4].n
                block.sync(f_ld2)

                def f_act(sc, dv_cp=dv_cp, dv_b5=dv_b5, ld2_n=ld2_n):
                    sc.wait_ge(dv1.h, dv_cp)
                    for c in range(nch):
                        sc.wait_ge(syp[c % 4].h, ld2_n[c])
                        if c >= 1:
                            sc.wait_ge(dv1.h, dv_b5 + 2 * c)
                        ac1.inc(sc.activation(out=T1[:], in_=WS[c % 4][:], func=AF.Copy,
                                              scale=rS[:, 0:1], bias=MAGIC))
                block.scalar(f_act)

                st_n = {}
                gpp_base = (gpp[0].n, gpp[1].n)
                def f_tern(v, ac_b=ac_b, gpp_base=gpp_base):
                    for c in range(nch):
                        v.wait_ge(ac1.h, ac_b + c + 1)
                        if c >= 2:
                            v.wait_ge(gpp[c % 2].h, gpp_base[c % 2] + 16 * (c // 2))
                        self.V(v, v.tensor_scalar(out=T1[:], in0=T1[:], scalar1=MAGIC, scalar2=-1.0,
                                                  op0=ALU.subtract, op1=ALU.max))
                        self.V(v, v.tensor_scalar(out=TB[c % 2][:], in0=T1[:], scalar1=1.0,
                                                  scalar2=None, op0=ALU.min))
                block.vector(f_tern)

                def f_st(g, dst=dst, dv_b5=dv_b5):
                    for c in range(nch):
                        g.wait_ge(dv1.h, dv_b5 + 2 * (c + 1))
                        gpp[c % 2].inc(g.dma_start(out=dst[:, 4096 * c:4096 * (c + 1)],
                                                   in_=TB[c % 2][:]), 16)
                block.gpsimd(f_st)
                self.barrier()

    def l1_attn(self, Xv, XNv, dram, bq, bo, cc_in, cc_out):
        block = self.block
        pe1, pe2, dv1, dv2, ac1, gp, cc, sy = (self.pe1, self.pe2, self.dv1, self.dv2,
                                               self.ac1, self.gp, self.cc, self.sy)
        onesr, bcp2 = self._onesr, self._bcp2
        with ExitStack() as ph:
            wkB = self.sb("wkB", [128, 8 * 1024], BF16, stack=ph)
            wvB = self.sb("wvB", [128, 8 * 1024], BF16, stack=ph)
            bkB = self.sb("bkB", [128, 1024], stack=ph)
            bvB = self.sb("bvB", [128, 1024], stack=ph)
            bkr = self.sb("bkr_t", [1, 1024], stack=ph)
            bvr = self.sb("bvr_t", [1, 1024], stack=ph)
            KS1 = self.sb("KS1", [128, 2 * 512], stack=ph)
            KS2 = self.sb("KS2", [128, 2 * 512], stack=ph)
            KS3 = self.sb("KS3", [128, 2 * 512], stack=ph)
            KVbuf = self.sb("KVbuf", [64, 16 * 130], stack=ph)
            KFr = self.sb("KFr", [128, 3 * 1024], BF16, stack=ph)
            VPr = self.sb("VPr", [128, 3 * 16 * 65], BF16, stack=ph)
            vpv = VPr.rearrange("p (r h d) -> p r h d", r=3, h=16)

            def f_load(g):
                gp.inc(g.dma_start(out=wkB[:], in_=dram["wkT"][:]), 16)
                gp.inc(g.dma_start(out=wvB[:], in_=dram["wvT"][:]), 16)
            block.gpsimd(f_load)
            gp_w = gp.n

            def f_load2(s):
                sy.inc(s.dma_start(out=bkr[:], in_=dram["bkr"][:]), 16)
                sy.inc(s.dma_start(out=bvr[:], in_=dram["bvr"][:]), 16)
            block.sync(f_load2)
            sy_w = sy.n

            with ExitStack() as pp:
                pbb = self.psum("pbb", [128, 512], stack=pp)
                pe_b = pe1.n; dv_b = dv1.n

                def f_pebc(t, sy_w=sy_w, dv_b=dv_b):
                    t.wait_ge(sy.h, sy_w)
                    for j in range(4):
                        src = bkr if j < 2 else bvr
                        if j >= 1:
                            t.wait_ge(dv1.h, dv_b + j)
                        pe1.inc(t.matmul(pbb[:], onesr[:], src[:, 512 * (j % 2):512 * (j % 2 + 1)],
                                         start=True, stop=True))
                block.tensor(f_pebc)

                def f_dvbc(v, pe_b=pe_b):
                    for j in range(4):
                        dstt = bkB if j < 2 else bvB
                        v.wait_ge(pe1.h, pe_b + j + 1)
                        dv1.inc(v.tensor_copy(out=dstt[:, 512 * (j % 2):512 * (j % 2 + 1)],
                                              in_=pbb[:]))
                block.vector(f_dvbc)

                def f_ones(v):
                    dv1.inc(v.memset(vpv[:, :, :, 64:65], 1.0))
                block.vector(f_ones)

            with ExitStack() as kp:
                pk = [self.psum("pk%d" % k, [128, 512], stack=kp) for k in range(2)]
                kvp = [self.psum("kvp%d" % k, [64, 390 if k < 5 else 130], stack=kp)
                       for k in range(6)]
                pe_mom = {}
                ks1_done = {}
                vev_done = {}
                chain_done = {}

                def emit_moments(tc):
                    r = tc % 3
                    dv_tc = chain_done[tc]
                    def f_mom(t, tc=tc, r=r, dv_tc=dv_tc):
                        t.wait_ge(dv1.h, dv_tc)
                        slot = tc // 8
                        for h in range(16):
                            bank, off = h // 3, 130 * (h % 3)
                            mm = t.matmul(kvp[bank][:, off + 65 * slot:off + 65 * (slot + 1)],
                                          KFr[:, 1024 * r + 64 * h:1024 * r + 64 * (h + 1)],
                                          VPr[:, 1040 * r + 65 * h:1040 * r + 65 * (h + 1)],
                                          start=(tc == 0 and h % 3 == 0),
                                          stop=(tc == 15 and h % 3 == (2 if bank < 5 else 0)),
                                          skip_group_check=True)
                            if h == 15:
                                pe1.inc(mm)
                    block.tensor(f_mom)
                    pe_mom[tc] = pe1.n

                for tc in range(16):
                    r = tc % 3
                    pe_war = pe_mom.get(tc - 3, 0)
                    for jh in range(2):
                        def f_pe(t, tc=tc, jh=jh, first=(tc == 0 and jh == 0)):
                            if first:
                                t.wait_ge(gp.h, gp_w)
                                t.wait_ge(dv1.h, dv1.n)
                            prev = ks1_done.get((tc, jh - 1), ks1_done.get((tc - 1, 1), 0))
                            if prev:
                                t.wait_ge(dv1.h, prev)
                            for i in range(NCH):
                                mm = t.matmul(pk[0][:],
                                              XNv[i][:, 128 * tc:128 * (tc + 1)],
                                              wkB[:, 1024 * i + 512 * jh:1024 * i + 512 * (jh + 1)],
                                              start=(i == 0), stop=(i == NCH - 1))
                            pe1.inc(mm)
                        block.tensor(f_pe)
                        pe_kv = pe1.n
                        def f_ev(v, jh=jh, pe_kv=pe_kv):
                            v.wait_ge(pe1.h, pe_kv)
                            self.V(v, v.tensor_tensor(out=KS1[:, 512 * jh:512 * (jh + 1)],
                                                      in0=pk[0][:],
                                                      in1=bkB[:, 512 * jh:512 * (jh + 1)], op=ALU.add))
                            self.V(v, v.tensor_scalar(out=KS2[:, 512 * jh:512 * (jh + 1)],
                                                      in0=KS1[:, 512 * jh:512 * (jh + 1)],
                                                      scalar1=0.0, scalar2=None, op0=ALU.min))
                        block.vector(f_ev)
                        ks1_done[(tc, jh)] = dv1.n - 1
                        dv_k1 = dv1.n
                        def f_exp(sc, jh=jh, dv_k1=dv_k1):
                            sc.wait_ge(dv1.h, dv_k1)
                            ac1.inc(sc.activation(out=KS3[:, 512 * jh:512 * (jh + 1)],
                                                  in_=KS2[:, 512 * jh:512 * (jh + 1)], func=AF.Exp))
                        block.scalar(f_exp)
                        ac_k = ac1.n
                        def f_ev2(v, r=r, jh=jh, ac_k=ac_k, pe_war=pe_war):
                            v.wait_ge(ac1.h, ac_k)
                            if pe_war:
                                v.wait_ge(pe1.h, pe_war)
                            self.V(v, v.tensor_scalar(out=KS2[:, 512 * jh:512 * (jh + 1)],
                                                      in0=KS1[:, 512 * jh:512 * (jh + 1)],
                                                      scalar1=0.0, scalar2=None, op0=ALU.max))
                            c0 = 1024 * r + 512 * jh
                            self.V(v, v.tensor_tensor(out=KFr[:, c0:c0 + 512],
                                                      in0=KS3[:, 512 * jh:512 * (jh + 1)],
                                                      in1=KS2[:, 512 * jh:512 * (jh + 1)], op=ALU.add))
                        block.vector(f_ev2)
                    for jh in range(2):
                        def f_pe(t, tc=tc, jh=jh):
                            prev = vev_done.get((tc, jh - 1), vev_done.get((tc - 1, 1), 0))
                            if prev:
                                t.wait_ge(dv1.h, prev)
                            for i in range(NCH):
                                mm = t.matmul(pk[1][:],
                                              XNv[i][:, 128 * tc:128 * (tc + 1)],
                                              wvB[:, 1024 * i + 512 * jh:1024 * i + 512 * (jh + 1)],
                                              start=(i == 0), stop=(i == NCH - 1))
                            pe1.inc(mm)
                        block.tensor(f_pe)
                        pe_kv = pe1.n
                        def f_ev(v, r=r, jh=jh, pe_kv=pe_kv, pe_war=pe_war):
                            v.wait_ge(pe1.h, max(pe_kv, pe_war))
                            pv = pk[1][:].rearrange("p (one h d) -> p one h d", one=1, h=8)
                            bv = bvB[:, 512 * jh:512 * (jh + 1)].rearrange(
                                "p (one h d) -> p one h d", one=1, h=8)
                            dv1.inc(v.tensor_tensor(
                                out=vpv[:, r:r + 1, 8 * jh:8 * (jh + 1), 0:64],
                                in0=pv, in1=bv, op=ALU.add))
                        block.vector(f_ev)
                        vev_done[(tc, jh)] = dv1.n
                    chain_done[tc] = dv1.n
                    if tc >= 1:
                        emit_moments(tc - 1)
                emit_moments(15)

                pe_all = pe1.n
                def f_evm(v, pe_all=pe_all):
                    v.wait_ge(pe1.h, pe_all)
                    for b in range(6):
                        n_h = 3 if b < 5 else 1
                        dv1.inc(v.tensor_copy(out=KVbuf[:, 390 * b:390 * b + 130 * n_h],
                                              in_=kvp[b][:]))
                block.vector(f_evm)

            dv_kv = dv1.n
            def f_cc(g, dv_kv=dv_kv):
                g.wait_ge(dv1.h, dv_kv)
                gp.inc(g.dma_start(out=cc_in[:], in_=KVbuf[:]), 16)
                g.wait_ge(gp.h, gp.n)
                cc.inc(g.collective_compute("AllReduce", ALU.add,
                                            [[0, 1, 2, 3], [4, 5, 6, 7]],
                                            [cc_in.opt()], [cc_out.opt()]))
            block.gpsimd(f_cc)

        with ExitStack() as ph:
            woB = self.sb("woB", [128, 8 * 1024], BF16, stack=ph)
            QF = self.sb("QF", [128, 8 * 2048], BF16, stack=ph)
            arena = self.sb("arena", [128, 16 * 130], BF16, stack=ph)
            BD = self.sb("BD", [128, 16 * 128], BF16, stack=ph)
            KS = self.sb("KSs", [128, 16 * 2], BF16, stack=ph)
            ar5 = arena.rearrange("p (pair slot half j) -> p pair slot half j",
                                  pair=8, slot=2, half=2)

            def f_load(g):
                gp.inc(g.dma_start(out=woB[:], in_=dram["woT"][:]), 16)
            block.gpsimd(f_load)

            def f_ze(v):
                dv1.inc(v.memset(arena[:], 0.0))
            block.vector(f_ze)
            dv_ze = dv1.n
            cc_n = cc.n

            def f_ar(g, dv_ze=dv_ze, cc_n=cc_n):
                g.wait_ge(cc.h, cc_n)
                g.wait_ge(dv1.h, dv_ze)
                src5 = cc_out.rearrange("d (h2 par slot j) -> d h2 par slot j",
                                        h2=8, par=2, slot=2)
                self.gpB.inc(g.dma_start(out=ar5[0:64, :, :, 0:1, :].opt(),
                                         in_=src5[:, :, 0:1, :, :].opt()), 16)
                self.gpB.inc(g.dma_start(out=ar5[64:128, :, :, 1:2, :].opt(),
                                         in_=src5[:, :, 1:2, :, :].opt()), 16)
            block.gpsimd(f_ar)
            gp_ar = self.gpB.n

            with ExitStack() as qp:
                wqB = self.sb("wqB", [128, 8 * 1024], BF16, stack=qp)
                QS1 = self.sb("QS1", [128, 2 * 512], stack=qp)
                QS2 = self.sb("QS2", [128, 2 * 512], stack=qp)
                QS3 = self.sb("QS3", [128, 2 * 512], stack=qp)
                def f_loadq(g):
                    gp.inc(g.dma_start(out=wqB[:], in_=dram["wqT"][:]), 16)
                block.gpsimd(f_loadq)
                gp_w2 = gp.n
                pq = [self.psum("pq%d" % k, [128, 512], stack=qp) for k in range(2)]
                qs_done = {}; exp_done = {}; qf_done = {}
                for n, (ic, tt) in enumerate([(ic, tt) for ic in range(NCH) for tt in range(4)]):
                    b = n % 2
                    bs = slice(512 * b, 512 * (b + 1))
                    def f_pe(t, ic=ic, tt=tt, b=b, n=n):
                        if n == 0:
                            t.wait_ge(gp.h, gp_w2)
                        elif n >= 2:
                            t.wait_ge(dv1.h, qs_done[n - 2])
                        for i in range(NCH):
                            mm = t.matmul(pq[b][:],
                                          wqB[:, 1024 * i + 128 * ic:1024 * i + 128 * (ic + 1)],
                                          XNv[i][:, 512 * tt:512 * (tt + 1)],
                                          start=(i == 0), stop=(i == NCH - 1))
                        pe1.inc(mm)
                    block.tensor(f_pe)
                    pe_q = pe1.n
                    def f_ev(v, ic=ic, b=b, bs=bs, n=n, pe_q=pe_q):
                        v.wait_ge(pe1.h, pe_q)
                        if n >= 2:
                            v.wait_ge(ac1.h, exp_done[n - 2])
                            v.wait_ge(dv1.h, qf_done[n - 2])
                        v.tensor_scalar(out=QS1[:, bs], in0=pq[b][:], scalar1=bq[:, ic:ic + 1],
                                        scalar2=0.0, op0=ALU.add, op1=ALU.min)
                        dv1.inc(v.tensor_scalar(out=QS2[:, bs], in0=pq[b][:], scalar1=bq[:, ic:ic + 1],
                                                scalar2=0.0, op0=ALU.add, op1=ALU.max))
                    block.vector(f_ev)
                    qs_done[n] = dv1.n
                    def f_exp(sc, b=b, bs=bs, n=n, dv_q=qs_done[n]):
                        sc.wait_ge(dv1.h, dv_q)
                        if n >= 2:
                            sc.wait_ge(dv1.h, qf_done[n - 2])
                        ac1.inc(sc.activation(out=QS3[:, bs], in_=QS1[:, bs], func=AF.Exp))
                    block.scalar(f_exp)
                    exp_done[n] = ac1.n
                    def f_ev2(v, ic=ic, tt=tt, bs=bs, ac_q=exp_done[n]):
                        v.wait_ge(ac1.h, ac_q)
                        c0 = 2048 * ic + 512 * tt
                        dv1.inc(v.tensor_tensor(out=QF[:, c0:c0 + 512],
                                                in0=QS3[:, bs], in1=QS2[:, bs], op=ALU.add))
                    block.vector(f_ev2)
                    qf_done[n] = dv1.n

            ATT = self.sb("ATT", [128, 8 * 2048], BF16, stack=ph)

            dv_bd0 = dv1.n
            def f_bd(v, gp_ar=gp_ar):
                v.wait_ge(self.gpB.h, gp_ar)
                for p in range(8):
                    for slot in range(2):
                        idx = 2 * p + slot
                        src = ar5[:, p:p + 1, slot:slot + 1, :, :]
                        v.tensor_copy(
                            out=BD[:, 128 * idx:128 * (idx + 1)].rearrange(
                                "p (one one2 half j) -> p one one2 half j", one=1, one2=1, half=2),
                            in_=src[:, :, :, :, 0:64])
                        dv1.inc(v.tensor_copy(
                            out=KS[:, 2 * idx:2 * (idx + 1)].rearrange(
                                "p (one one2 half j) -> p one one2 half j", one=1, one2=1, half=2),
                            in_=src[:, :, :, :, 64:65]))
            block.vector(f_bd)

            with ExitStack() as ap:
                patt = [self.psum("patt%d" % k, [128, 512], stack=ap) for k in range(2)]
                pden = [self.psum("pden%d" % k, [2, 512], stack=ap) for k in range(2)]
                pdb = [self.psum("pdb%d" % k, [128, 512], stack=ap) for k in range(2)]
                dn2 = self.sb("dn2", [2, 2 * 512], stack=ap)
                AS2 = self.sb("AS2", [128, 2 * 512], stack=ap)
                att_done = {}
                ln_done = {}
                db_done = {}
                dv1_mx = {}
                for n, (p, slot, tt2) in enumerate([(p, s_, t_) for p in range(8)
                                                    for s_ in range(2) for t_ in range(2)]):
                    idx = 2 * p + slot
                    col = 1024 * slot + 512 * tt2
                    b = n % 2
                    dv_now = dv1.n
                    def f_pe(t, idx=idx, p=p, col=col, n=n, b=b, dv_now=dv_now):
                        if n == 0:
                            t.wait_ge(dv1.h, dv_now)
                        if n >= 2:
                            t.wait_ge(dv2.h, att_done[n - 2])
                        t.matmul(patt[b][:], BD[:, 128 * idx:128 * (idx + 1)],
                                 QF[:, 2048 * p + col:2048 * p + col + 512],
                                 start=True, stop=True)
                        pe1.inc(t.matmul(pden[b][:], KS[:, 2 * idx:2 * (idx + 1)],
                                         QF[:, 2048 * p + col:2048 * p + col + 512],
                                         start=True, stop=True))
                    block.tensor(f_pe)
                    pe_a = pe1.n
                    def f_dn(v, pe_a=pe_a, b=b, n=n):
                        v.wait_ge(pe1.h, pe_a)
                        if n >= 2:
                            v.wait_ge(ac1.h, ln_done[n - 2])
                            v.wait_ge(pe2.h, db_done[n - 2])
                        dv1.inc(v.tensor_scalar(out=dn2[:, 512 * b:512 * (b + 1)],
                                                in0=pden[b][:], scalar1=1e-6,
                                                scalar2=None, op0=ALU.max))
                    block.vector(f_dn)
                    dv1_mx[n] = dv1.n
                    def f_lnx(sc, b=b, n=n, dvw=dv1_mx[n]):
                        sc.wait_ge(dv1.h, dvw)
                        self.V(sc, sc.activation(out=dn2[:, 512 * b:512 * (b + 1)],
                                                 in_=dn2[:, 512 * b:512 * (b + 1)], func=AF.Ln),
                               sem=ac1)
                        ac1.inc(sc.activation(out=dn2[:, 512 * b:512 * (b + 1)],
                                              in_=dn2[:, 512 * b:512 * (b + 1)], func=AF.Exp,
                                              scale=-1.0))
                    block.scalar(f_lnx)
                    ln_done[n] = ac1.n
                    def f_db(t, b=b, acw=ln_done[n]):
                        t.wait_ge(ac1.h, acw)
                        pe2.inc(t.matmul(pdb[b][:], bcp2[:], dn2[:, 512 * b:512 * (b + 1)],
                                         start=True, stop=True))
                    block.tensor(f_db)
                    pe2_db = pe2.n
                    db_done[n] = pe2.n
                    def f_as(sc, pe_a=pe_a, b=b, n=n):
                        sc.wait_ge(pe1.h, pe_a)
                        if n >= 2:
                            sc.wait_ge(dv2.h, att_done[n - 2])
                        ac1.inc(sc.activation(out=AS2[:, 512 * b:512 * (b + 1)],
                                              in_=patt[b][:], func=AF.Copy))
                    block.scalar(f_as)
                    ac_as = ac1.n
                    def f_att(v, p=p, col=col, b=b, ac_as=ac_as, pe2_db=pe2_db):
                        v.wait_ge(ac1.h, ac_as)
                        v.wait_ge(pe2.h, pe2_db)
                        dv2.inc(v.tensor_tensor(out=ATT[:, 2048 * p + col:2048 * p + col + 512],
                                                in0=AS2[:, 512 * b:512 * (b + 1)],
                                                in1=pdb[b][:], op=ALU.mult))
                    block.vector(f_att)
                    att_done[n] = dv2.n

                po = [self.psum("po%d" % k, [128, 512], stack=ap) for k in range(2)]
                dv2_q = dv2.n
                ev_done = {}
                for n, (ic, tt) in enumerate([(ic, tt) for ic in range(NCH) for tt in range(4)]):
                    buf = n % 2
                    def f_pe(t, ic=ic, tt=tt, buf=buf, n=n):
                        if n == 0:
                            t.wait_ge(dv2.h, dv2_q)
                        elif n >= 2:
                            t.wait_ge(dv2.h, ev_done[n - 2])
                        for j in range(NCH):
                            mm = t.matmul(po[buf][:],
                                          woB[:, 1024 * j + 128 * ic:1024 * j + 128 * (ic + 1)],
                                          ATT[:, 2048 * j + 512 * tt:2048 * j + 512 * (tt + 1)],
                                          start=(j == 0), stop=(j == NCH - 1))
                        pe1.inc(mm)
                    block.tensor(f_pe)
                    pe_o = pe1.n
                    def f_ev(v, ic=ic, tt=tt, buf=buf, pe_o=pe_o):
                        v.wait_ge(pe1.h, pe_o)
                        dv2.inc(v.scalar_tensor_tensor(
                            out=Xv[ic][:, 512 * tt:512 * (tt + 1)],
                            in0=po[buf][:], scalar=bo[:, ic:ic + 1],
                            in1=Xv[ic][:, 512 * tt:512 * (tt + 1)],
                            op0=ALU.add, op1=ALU.add))
                    block.vector(f_ev)
                    ev_done[n] = dv2.n

    def l1_ffn(self, Xv, XNv, dram, b1, b2, w1t_dram, w2t_dram, out):
        block = self.block
        pe1, pe2, dv1, dv2, ac1, gp, sy, st = (self.pe1, self.pe2, self.dv1, self.dv2,
                                               self.ac1, self.gp, self.sy, self.st)
        with ExitStack() as ph:
            W1tB = self.sb("W1tB", [128, 8 * 4096], BF16, stack=ph)
            HT = self.sb("HT", [128, 32 * 512], BF16, stack=ph)
            W2tB = [self.sb("W2tB%d" % k, [128, 1024], BF16, stack=ph) for k in range(2)]

            def f_ld(g):
                for j in range(4):
                    gp.inc(g.dma_start(out=W1tB[:, 8192 * j:8192 * (j + 1)],
                                       in_=w1t_dram[:, 8192 * j:8192 * (j + 1)]), 16)
            block.gpsimd(f_ld)
            gp_w1 = gp.n

            for tt in range(4):
                with ExitStack() as tp:
                    phh = [self.psum("phh%d_%d" % (tt, k), [128, 512], stack=tp) for k in range(2)]
                    ac_b = ac1.n; pe_b = pe1.n; dv2_prev = dv2.n
                    def f_pe(t, tt=tt, ac_b=ac_b, dv2_prev=dv2_prev, first=(tt == 0)):
                        if first:
                            t.wait_ge(gp.h, gp_w1)
                        else:
                            t.wait_ge(dv2.h, dv2_prev)
                        for f in range(32):
                            buf = f % 2
                            if f >= 2:
                                t.wait_ge(ac1.h, ac_b + f - 1)
                            for i in range(NCH):
                                mm = t.matmul(phh[buf][:],
                                              W1tB[:, 4096 * i + 128 * f:4096 * i + 128 * (f + 1)],
                                              XNv[i][:, 512 * tt:512 * (tt + 1)],
                                              start=(i == 0), stop=(i == NCH - 1))
                            pe1.inc(mm)
                    block.tensor(f_pe)
                    def f_gelu(sc, pe_b=pe_b):
                        for f in range(32):
                            buf = f % 2
                            sc.wait_ge(pe1.h, pe_b + f + 1)
                            ac1.inc(sc.activation(out=HT[:, 512 * f:512 * (f + 1)],
                                                  in_=phh[buf][:], func=AF.Gelu,
                                                  bias=b1[:, f:f + 1]))
                    block.scalar(f_gelu)
                    ac_done = ac1.n

                with ExitStack() as tp:
                    py8 = [self.psum("py8_%d_%d" % (tt, k), [128, 512], stack=tp) for k in range(8)]
                    gpp = [self.gpA, self.gpB]
                    pe2_b = pe2.n
                    w2_n = {}
                    def f_w2(g, pe2_b=pe2_b, w2_n=w2_n):
                        for f in range(32):
                            if f >= 2:
                                g.wait_ge(pe2.h, pe2_b + f - 1)
                            else:
                                g.wait_ge(pe2.h, pe2_b)
                            gpp[f % 2].inc(g.dma_start(out=W2tB[f % 2][:],
                                                       in_=w2t_dram[:, 1024 * f:1024 * (f + 1)]), 16)
                            w2_n[f] = gpp[f % 2].n
                    block.gpsimd(f_w2)

                    def f_pe2(t, ac_done=ac_done, w2_n=w2_n):
                        t.wait_ge(ac1.h, ac_done)
                        for f in range(32):
                            t.wait_ge(gpp[f % 2].h, w2_n[f])
                            for ic in range(NCH):
                                mm = t.matmul(py8[ic][:], W2tB[f % 2][:, 128 * ic:128 * (ic + 1)],
                                              HT[:, 512 * f:512 * (f + 1)],
                                              start=(f == 0), stop=(f == 31))
                            pe2.inc(mm)
                    block.tensor(f_pe2)
                    pe2_done = pe2.n

                    def f_ev(v, tt=tt, pe2_done=pe2_done):
                        v.wait_ge(pe2.h, pe2_done)
                        for ic in range(NCH):
                            dv2.inc(v.scalar_tensor_tensor(
                                out=Xv[ic][:, 512 * tt:512 * (tt + 1)],
                                in0=py8[ic][:], scalar=b2[:, ic:ic + 1],
                                in1=Xv[ic][:, 512 * tt:512 * (tt + 1)],
                                op0=ALU.add, op1=ALU.add))
                    block.vector(f_ev)
                    dv2_done = dv2.n

                    def f_st(s, tt=tt, dv2_done=dv2_done):
                        s.wait_ge(dv2.h, dv2_done)
                        for ic in range(NCH):
                            st.inc(s.dma_start(out=out[:, T * ic + 512 * tt:T * ic + 512 * (tt + 1)],
                                               in_=Xv[ic][:, 512 * tt:512 * (tt + 1)]), 16)
                    block.sync(f_st)

            def f_fin(s):
                s.wait_ge(st.h, st.n)
            block.sync(f_fin)



def _pack(w):
    R, C = w.shape
    k = R // 128
    return np.ascontiguousarray(w.reshape(k, 128, C).transpose(1, 0, 2).reshape(128, k * C))


def _col_tile(vec):
    k = vec.shape[0] // 128
    return np.ascontiguousarray(vec.reshape(k, 128).T)


def _host_inputs(inputs):
    x = np.ascontiguousarray(inputs["L0_attn_residual"], dtype=np.float32)
    xflat = x.reshape(16384, 1024)
    sh = {}
    sh["kT"] = _pack(np.ascontiguousarray(inputs["L0_attn_w_fc"].reshape(4096, 256).T))
    sh["V"] = _pack(np.ascontiguousarray(inputs["L0_attn_w_proj"].T.reshape(4096, 256)))
    sh["lng"] = _col_tile(inputs["L0_attn_ln_g"])
    sh["lnb"] = _col_tile(inputs["L0_attn_ln_b"])
    sh["W1A"] = _pack(np.ascontiguousarray(inputs["L0_ffn_w1_A"].T))
    sh["W1B"] = np.ascontiguousarray(
        inputs["L0_ffn_w1_B"].reshape(8, 64, 4096).transpose(1, 0, 2).reshape(64, 8 * 4096))
    sh["W2A"] = np.ascontiguousarray(
        inputs["L0_ffn_w2_A"].reshape(8, 64, 32, 128).transpose(3, 0, 2, 1).reshape(128, 8 * 32 * 64))
    sh["W2B"] = _pack(np.ascontiguousarray(inputs["L0_ffn_w2_B"]))
    sh["wqT"] = _pack(np.ascontiguousarray(inputs["L1_attn_wq"].T))
    sh["wkT"] = _pack(np.ascontiguousarray(inputs["L1_attn_wk"].T))
    sh["wvT"] = _pack(np.ascontiguousarray(inputs["L1_attn_wv"].T))
    sh["woT"] = _pack(np.ascontiguousarray(inputs["L1_attn_wo"].T))
    sh["bq"] = _col_tile(inputs["L1_attn_bq"])
    sh["bo"] = _col_tile(inputs["L1_attn_bo"])
    sh["bkr"] = np.ascontiguousarray(inputs["L1_attn_bk"].reshape(1, 1024))
    sh["bvr"] = np.ascontiguousarray(inputs["L1_attn_bv"].reshape(1, 1024))
    sh["W1f"] = _pack(np.ascontiguousarray(inputs["L1_ffn_W1"].T))
    sh["W2f"] = _pack(np.ascontiguousarray(inputs["L1_ffn_W2"].T))
    sh["b1"] = _col_tile(inputs["L1_ffn_b1"])
    sh["b2"] = _col_tile(inputs["L1_ffn_b2"])
    ew = np.ascontiguousarray(inputs["L0_ffn_expert_weights"], dtype=np.float32)
    ei = np.ascontiguousarray(inputs["L0_ffn_expert_indices"], dtype=np.int32)
    in_maps = []
    for c in range(NCORE):
        m = dict(sh)
        idx = np.r_[1024 * c:1024 * (c + 1), 8192 + 1024 * c:8192 + 1024 * (c + 1)]
        m["xT"] = _pack(np.ascontiguousarray(xflat[idx].T))
        m["ew"] = np.ascontiguousarray(ew[1024 * c:1024 * (c + 1)].reshape(1, 2048))
        m["ei"] = np.ascontiguousarray(ei[1024 * c:1024 * (c + 1)].reshape(1, 2048))
        in_maps.append(m)
    return in_maps


def _unpack_out(res, key="outT"):
    o = np.empty((16384, 1024), dtype=np.float32)
    for c in range(NCORE):
        arr = np.asarray(res[c][key])
        outT = arr.reshape(128, 8, 2048).transpose(1, 0, 2).reshape(1024, 2048)
        idx = np.r_[1024 * c:1024 * (c + 1), 8192 + 1024 * c:8192 + 1024 * (c + 1)]
        o[idx] = outT.T
    return o.reshape(4, 4096, 1024)


_KEYS = {
    (0, 1): None,
    (0,): ["xT", "kT", "V", "lng", "lnb", "W1A", "W1B", "W2A", "W2B", "ew", "ei"],
    (1,): ["xT", "wqT", "wkT", "wvT", "woT", "bq", "bo", "bkr", "bvr",
           "W1f", "W2f", "b1", "b2"],
}


def run_layers(in_maps, layers=(0, 1), dump=None, trace=False):
    nc = Build(layers=layers, dump=dump).build()
    keep = _KEYS[tuple(layers)]
    maps = [{k: v for k, v in m.items() if keep is None or k in keep} for m in in_maps]
    return run_bass_kernel_spmd(nc, maps, core_ids=list(range(NCORE)), trace=trace)


def kernel(**inputs):
    in_maps = _host_inputs(inputs)
    res = run_layers(in_maps, layers=(0, 1))
    return _unpack_out(res.results)


# version 09
# speedup vs baseline: 1.0391x; 1.0391x over previous
import numpy as np
from contextlib import ExitStack

import concourse.bass as bass
import concourse.mybir as mybir
from concourse.bass_utils import run_bass_kernel_spmd

F32 = mybir.dt.float32
BF16 = mybir.dt.bfloat16
I32 = mybir.dt.int32
AF = mybir.ActivationFunctionType
ALU = mybir.AluOpType
AXX = mybir.AxisListType.X

MAGIC = 12582912.0
NCORE = 8
T = 2048
NCH = 8
EPS0 = 1e-5


class S:
    def __init__(self, nc, es, name):
        self.h = es.enter_context(nc.semaphore(name))
        self.n = 0

    def inc(self, inst, k=1):
        inst.then_inc(self.h, k)
        self.n += k
        return self.n


class Build:
    def __init__(self, layers=(0, 1), dump=None):
        self.layers = layers
        self.dump = dump
        self.nc = bass.Bass("TRN2", target_bir_lowering=False)
        self.es = ExitStack()

    def sb(self, name, shape, dt=F32, stack=None):
        return (stack or self.es).enter_context(self.nc.sbuf_tensor(name, shape, dt))

    def psum(self, name, shape, dt=F32, stack=None):
        return (stack or self.es).enter_context(self.nc.psum_tensor(name, shape, dt))

    def V(self, eng, inst, sem=None):
        s = sem if sem is not None else self.dv1
        s.inc(inst)
        eng.wait_ge(s.h, s.n)
        return inst

    def barrier(self):
        sems = [self.pe1, self.pe2, self.dv1, self.dv2, self.ac1, self.sy, self.gp, self.cc,
                self.gpA, self.gpB, self.syA, self.syB, self.syC, self.syD]
        for emit in (self.block.tensor, self.block.vector, self.block.scalar,
                     self.block.sync, self.block.gpsimd):
            def _f(eng, sems=sems):
                for s in sems:
                    if s.n > 0:
                        eng.wait_ge(s.h, s.n)
            emit(_f)

    def build(self):
        nc = self.nc
        es = self.es
        dram = {}

        def din(name, shape, dt=F32):
            dram[name] = nc.dram_tensor(name, shape, dt, kind="ExternalInput").ap()
            return dram[name]

        l0 = 0 in self.layers
        l1 = 1 in self.layers
        din("xT", [128, NCH * T])
        if l0:
            din("kT", [128, 2 * 4096]); din("V", [128, 32 * 256])
            din("lng", [128, 8]); din("lnb", [128, 8])
            din("W1A", [128, 8 * 512]); din("W1B", [64, 8 * 4096])
            din("W2A", [128, 8 * 32 * 64]); din("W2B", [128, 4 * 1024])
            din("ew", [1, 2048]); din("ei", [1, 2048], I32)
        if l1:
            din("wqT", [128, 8 * 1024]); din("wkT", [128, 8 * 1024])
            din("wvT", [128, 8 * 1024]); din("woT", [128, 8 * 1024])
            din("bq", [128, 8]); din("bo", [128, 8])
            din("bkr", [1, 1024]); din("bvr", [1, 1024])
            din("W1f", [128, 8 * 4096]); din("W2f", [128, 32 * 1024])
            din("b1", [128, 32]); din("b2", [128, 8])
        out = nc.dram_tensor("outT", [128, NCH * T], F32, kind="ExternalOutput").ap()
        dbg = {}
        if self.dump:
            dbg["dXN"] = nc.dram_tensor("dXN", [128, NCH * T], BF16, kind="ExternalOutput").ap()
        if l1:
            cc_in = nc.dram_tensor("cc_in", [64, 16 * 130], F32, kind="Internal").ap()
            cc_out = nc.dram_tensor("cc_out", [64, 16 * 130], F32, kind="Internal").ap()
            w1t_dram = nc.dram_tensor("w1t_dram", [128, 8 * 4096], BF16, kind="Internal").ap()
            w2t_dram = nc.dram_tensor("w2t_dram", [128, 32 * 1024], BF16, kind="Internal").ap()

        with es:
            ACC = self.sb("ACC", [128, NCH * T], F32)
            XN = self.sb("XN", [128, NCH * T], BF16)
            self._ones = self.sb("ones", [128, 1], F32)
            self._onesr = self.sb("onesr", [1, 128], F32)
            self._bcp2 = self.sb("bcp2", [2, 128], F32)
            smalls = []
            lng = lnb = bq = bo = b1 = b2 = None
            if l0:
                lng = self.sb("lng_t", [128, 8]); lnb = self.sb("lnb_t", [128, 8])
                smalls += [("lng", lng), ("lnb", lnb)]
            if l1:
                bq = self.sb("bq_t", [128, 8]); bo = self.sb("bo_t", [128, 8])
                b1 = self.sb("b1_t", [128, 32]); b2 = self.sb("b2_t", [128, 8])
                smalls += [("bq", bq), ("bo", bo), ("b1", b1), ("b2", b2)]

            self.pe1 = S(nc, es, "pe1"); self.pe2 = S(nc, es, "pe2")
            self.dv1 = S(nc, es, "dv1"); self.dv2 = S(nc, es, "dv2")
            self.ac1 = S(nc, es, "ac1")
            self.sy = S(nc, es, "sy"); self.gp = S(nc, es, "gp")
            self.cc = S(nc, es, "cc"); self.st = S(nc, es, "st")
            self.gpA = S(nc, es, "gpA"); self.gpB = S(nc, es, "gpB")
            self.syA = S(nc, es, "syA"); self.syB = S(nc, es, "syB")
            self.syC = S(nc, es, "syC"); self.syD = S(nc, es, "syD")
            sy, gp, st = self.sy, self.gp, self.st

            Xv = [ACC[:, T * i:T * (i + 1)] for i in range(NCH)]
            XNv = [XN[:, T * i:T * (i + 1)] for i in range(NCH)]

            blk_cm = nc.Block()
            self.block = block = blk_cm.__enter__()
            try:
                def p0_sync(sync):
                    for j in range(4):
                        sy.inc(sync.dma_start(out=ACC[:, 4096 * j:4096 * (j + 1)],
                                              in_=dram["xT"][:, 4096 * j:4096 * (j + 1)]), 16)
                block.sync(p0_sync)

                def p0_gp(g):
                    for nm, t_ in smalls:
                        gp.inc(g.dma_start(out=t_[:], in_=dram[nm][:]), 16)
                block.gpsimd(p0_gp)

                def p0_v(v):
                    v.memset(self._ones[:], 1.0)
                    self.dv1.inc(v.memset(self._onesr[:], 1.0))
                block.vector(p0_v)

                bcp_np = np.zeros((2, 128), dtype=np.float32)
                bcp_np[0, 0:64] = 1.0
                bcp_np[1, 64:128] = 1.0
                bcp_const = nc.inline_tensor(bcp_np, name="bcp_const").ap()

                def p0_gp2(g):
                    gp.inc(g.dma_start(out=self._bcp2[:], in_=bcp_const[:]), 16)
                block.gpsimd(p0_gp2)

                self.barrier()

                if l0:
                    with ExitStack() as l0scope:
                        self._keep_stack = l0scope
                        self.ln_phase(Xv, XNv, fold=True)
                        self.barrier()
                        if self.dump == "p1":
                            self.finish(out, ACC, XN, dbg); return nc
                        self.l0_attn(Xv, XNv, dram, lng, lnb)
                        self.barrier()
                    if self.dump == "p3":
                        self.finish(out, ACC, XN, dbg); return nc
                    self.moe(Xv, XNv, dram)
                    self.barrier()
                if not l1 or self.dump == "p4":
                    self.finish(out, ACC, XN, dbg); return nc

                self.ternarize(dram, w1t_dram, w2t_dram)
                if self.dump == "p4b":
                    dW1 = nc.dram_tensor("dW1t", [128, 8 * 4096], BF16, kind="ExternalOutput").ap()
                    dW2 = nc.dram_tensor("dW2t", [128, 32 * 1024], BF16, kind="ExternalOutput").ap()
                    def f_tdump(sync):
                        for s_ in [self.pe1, self.pe2, self.dv1, self.dv2, self.ac1,
                                   self.sy, self.gp, self.cc,
                                   self.gpA, self.gpB, self.syA, self.syB, self.syC, self.syD]:
                            if s_.n > 0:
                                sync.wait_ge(s_.h, s_.n)
                        self.st.inc(sync.dma_start(out=dW1[:], in_=w1t_dram[:]), 16)
                        self.st.inc(sync.dma_start(out=dW2[:], in_=w2t_dram[:]), 16)
                        sync.wait_ge(self.st.h, self.st.n)
                    block.sync(f_tdump)
                    self.finish(out, ACC, XN, dbg); return nc
                self.ln_phase(Xv, XNv, fold=False)
                self.barrier()
                if self.dump == "p5":
                    self.finish(out, ACC, XN, dbg); return nc
                self.l1_attn(Xv, XNv, dram, bq, bo, cc_in, cc_out)
                self.barrier()
                if self.dump == "p67":
                    self.finish(out, ACC, XN, dbg); return nc
                self.l1_ffn(Xv, XNv, dram, b1, b2, w1t_dram, w2t_dram, out)
            finally:
                blk_cm.__exit__(None, None, None)
        return nc

    def finish(self, out, ACC, XN, dbg, tern=None):
        st = self.st
        sems = [self.pe1, self.pe2, self.dv1, self.dv2, self.ac1, self.sy, self.gp, self.cc,
                self.gpA, self.gpB, self.syA, self.syB, self.syC, self.syD]

        def f(sync):
            for s_ in sems:
                if s_.n > 0:
                    sync.wait_ge(s_.h, s_.n)
            for j in range(4):
                st.inc(sync.dma_start(out=out[:, 4096 * j:4096 * (j + 1)],
                                      in_=ACC[:, 4096 * j:4096 * (j + 1)]), 16)
            if dbg:
                st.inc(sync.dma_start(out=dbg["dXN"][:], in_=XN[:]), 16)
            sync.wait_ge(st.h, st.n)
        self.block.sync(f)
        if tern is not None and dbg:
            pass

    def ln_phase(self, Xv, XNv, fold):
        block = self.block
        pe1, pe2, dv1, ac1 = self.pe1, self.pe2, self.dv1, self.ac1
        ones, onesr = self._ones, self._onesr
        sfx = "_f" if fold else "_n"
        if fold:
            self._r_var = self.sb("r_var_keep", [1, T], stack=self._keep_stack)
            self._r_rstd = self.sb("r_rstd_keep", [1, T], stack=self._keep_stack)
        with ExitStack() as ph:
            r_mean = self.sb("r_mean" + sfx, [1, T], stack=ph)
            r_var = self._r_var if fold else self.sb("r_var" + sfx, [1, T], stack=ph)
            r_rstd = self._r_rstd if fold else self.sb("r_rstd" + sfx, [1, T], stack=ph)
            r_a = self.sb("r_a" + sfx, [1, T], stack=ph)
            r_tmp = self.sb("r_tmp" + sfx, [1, T], stack=ph)
            SQ = self.sb("SQ" + sfx, [128, 2 * 512], stack=ph)
            N1A = self.sb("N1A" + sfx, [128, 8 * 512], stack=ph)
            pb_sum = [self.psum("pb_sum%s%d" % (sfx, k), [1, 512], stack=ph) for k in range(2)]
            pb_sq = [self.psum("pb_sq%s%d" % (sfx, k), [1, 512], stack=ph) for k in range(2)]
            pb_r = [self.psum("pb_r%s%d" % (sfx, k), [128, 512], stack=ph) for k in range(2)]
            pb_a = [self.psum("pb_a%s%d" % (sfx, k), [128, 512], stack=ph) for k in range(2)]

            rows_done = {}; norm_done = {}; stats_done = {}
            for tt in range(4):
                sl = slice(512 * tt, 512 * (tt + 1))
                par = tt % 2
                ac_b = ac1.n; pe_b = pe1.n

                def f_act(sc, sl=sl, ac_b=ac_b, pe_b=pe_b, tt=tt):
                    for i in range(NCH):
                        if tt > 0 and i < 2:
                            sc.wait_ge(pe1.h, pe_b)
                        if i >= 2:
                            sc.wait_ge(pe1.h, pe_b + i - 1)
                        ac1.inc(sc.activation(out=SQ[:, 512 * (i % 2):512 * (i % 2 + 1)],
                                              in_=Xv[i][:, sl], func=AF.Square))
                block.scalar(f_act)

                def f_pe(t, sl=sl, par=par, ac_b=ac_b, tt=tt):
                    if tt >= 2:
                        t.wait_ge(dv1.h, rows_done[tt - 2])
                    for i in range(NCH):
                        t.matmul(pb_sum[par][:], ones[:], Xv[i][:, sl],
                                 start=(i == 0), stop=(i == NCH - 1))
                        t.wait_ge(ac1.h, ac_b + i + 1)
                        pe1.inc(t.matmul(pb_sq[par][:], ones[:],
                                         SQ[:, 512 * (i % 2):512 * (i % 2 + 1)],
                                         start=(i == 0), stop=(i == NCH - 1)))
                block.tensor(f_pe)
                stats_done[tt] = pe1.n

                def f_rows(v, sl=sl, par=par, pe_s=stats_done[tt]):
                    v.wait_ge(pe1.h, pe_s)
                    self.V(v, v.tensor_scalar(out=r_mean[:, sl], in0=pb_sum[par][:],
                                              scalar1=1.0 / 1024, scalar2=None, op0=ALU.mult))
                    self.V(v, v.tensor_scalar(out=r_var[:, sl], in0=pb_sq[par][:],
                                              scalar1=1.0 / 1024, scalar2=None, op0=ALU.mult))
                    self.V(v, v.tensor_tensor(out=r_tmp[:, sl], in0=r_mean[:, sl],
                                              in1=r_mean[:, sl], op=ALU.mult))
                    self.V(v, v.tensor_tensor(out=r_var[:, sl], in0=r_var[:, sl],
                                              in1=r_tmp[:, sl], op=ALU.subtract))
                    self.V(v, v.tensor_scalar(out=r_tmp[:, sl], in0=r_var[:, sl],
                                              scalar1=EPS0, scalar2=None, op0=ALU.add))
                block.vector(f_rows)
                rows_done[tt] = dv1.n

                def f_sqrt(sc, sl=sl, dv_r1=rows_done[tt]):
                    sc.wait_ge(dv1.h, dv_r1)
                    self.V(sc, sc.activation(out=r_tmp[:, sl], in_=r_tmp[:, sl], func=AF.Ln),
                           sem=ac1)
                    ac1.inc(sc.activation(out=r_rstd[:, sl], in_=r_tmp[:, sl], func=AF.Exp,
                                          scale=-0.5))
                block.scalar(f_sqrt)
                ac_r1 = ac1.n

                def f_rows2(v, sl=sl, ac_r1=ac_r1):
                    v.wait_ge(ac1.h, ac_r1)
                    self.V(v, v.scalar_tensor_tensor(out=r_a[:, sl], in0=r_mean[:, sl],
                                                     scalar=-1.0, in1=r_rstd[:, sl],
                                                     op0=ALU.mult, op1=ALU.mult))
                block.vector(f_rows2)
                dv_r2 = dv1.n

                def f_bc(t, sl=sl, par=par, dv_r2=dv_r2, tt=tt):
                    t.wait_ge(dv1.h, dv_r2)
                    if tt >= 2:
                        t.wait_ge(dv1.h, norm_done[tt - 2])
                    t.matmul(pb_r[par][:], onesr[:], r_rstd[:, sl], start=True, stop=True)
                    pe2.inc(t.matmul(pb_a[par][:], onesr[:], r_a[:, sl], start=True, stop=True))
                block.tensor(f_bc)
                pe2_r = pe2.n

                def f_norm(v, sl=sl, par=par, pe2_r=pe2_r):
                    v.wait_ge(pe2.h, pe2_r)
                    for i in range(NCH):
                        inst = v.tensor_tensor(out=N1A[:, 512 * i:512 * (i + 1)],
                                               in0=Xv[i][:, sl], in1=pb_r[par][:], op=ALU.mult)
                        if i == NCH - 1:
                            dv1.inc(inst); v.wait_ge(dv1.h, dv1.n)
                    for i in range(NCH):
                        inst = v.tensor_tensor(out=XNv[i][:, sl], in0=N1A[:, 512 * i:512 * (i + 1)],
                                               in1=pb_a[par][:], op=ALU.add)
                        if i == NCH - 1:
                            dv1.inc(inst); v.wait_ge(dv1.h, dv1.n)
                    if fold:
                        for i in range(NCH):
                            inst = v.tensor_tensor(out=Xv[i][:, sl], in0=Xv[i][:, sl],
                                                   in1=XNv[i][:, sl], op=ALU.add)
                            if i == NCH - 1:
                                dv1.inc(inst); v.wait_ge(dv1.h, dv1.n)
                block.vector(f_norm)
                norm_done[tt] = dv1.n

    def l0_attn(self, Xv, XNv, dram, lng, lnb):
        block = self.block
        pe1, pe2, dv1, dv2, ac1, gp = self.pe1, self.pe2, self.dv1, self.dv2, self.ac1, self.gp
        onesr = self._onesr
        with ExitStack() as ph:
            X2 = self.sb("X2", [128, NCH * T], BF16, stack=ph)
            X2v = [X2[:, T * i:T * (i + 1)] for i in range(NCH)]
            r_rstd2 = self.sb("r_rstd2", [1, T], stack=ph)
            r_t2 = self.sb("r_t2", [1, T], stack=ph)
            N1 = self.sb("N1b", [128, 512], stack=ph)
            kTt = self.sb("kTt", [128, 2 * 4096], BF16, stack=ph)
            Vt = self.sb("Vt", [128, 32 * 256], BF16, stack=ph)
            RB = self.sb("RB", [128, 4 * 512], stack=ph)
            PT = self.sb("PT", [128, 4 * 512], BF16, stack=ph)

            def f_load(g):
                gp.inc(g.dma_start(out=kTt[:], in_=dram["kT"][:]), 16)
                gp.inc(g.dma_start(out=Vt[:], in_=dram["V"][:]), 16)
            block.gpsimd(f_load)
            gp_w = gp.n

            def f_rows(v):
                self.V(v, v.scalar_tensor_tensor(out=r_t2[:], in0=self._r_rstd[:], scalar=1.0,
                                                 in1=self._r_rstd[:], op0=ALU.mult, op1=ALU.mult))
                self.V(v, v.tensor_tensor(out=r_t2[:], in0=self._r_var[:], in1=r_t2[:], op=ALU.mult))
                self.V(v, v.tensor_scalar(out=r_t2[:], in0=r_t2[:], scalar1=EPS0,
                                          scalar2=None, op0=ALU.add))
            block.vector(f_rows)
            dv_r = dv1.n

            def f_sq(sc, dv_r=dv_r):
                sc.wait_ge(dv1.h, dv_r)
                self.V(sc, sc.activation(out=r_t2[:], in_=r_t2[:], func=AF.Ln), sem=ac1)
                ac1.inc(sc.activation(out=r_rstd2[:], in_=r_t2[:], func=AF.Exp, scale=-0.5))
            block.scalar(f_sq)
            ac_r = ac1.n
            dvn0 = dv1.n

            x2scope = ExitStack()
            pb_r = self.psum("pb_r2", [128, 512], stack=x2scope)
            for tt in range(4):
                sl = slice(512 * tt, 512 * (tt + 1))
                dvn = dv1.n
                def f_bc(t, sl=sl, dvn=dvn, ac_r=ac_r):
                    t.wait_ge(dv1.h, dvn)
                    t.wait_ge(ac1.h, ac_r)
                    pe2.inc(t.matmul(pb_r[:], onesr[:], r_rstd2[:, sl], start=True, stop=True))
                block.tensor(f_bc)
                pe2_r = pe2.n
                def f_norm(v, sl=sl, pe2_r=pe2_r):
                    v.wait_ge(pe2.h, pe2_r)
                    for i in range(NCH):
                        self.V(v, v.tensor_tensor(out=N1[:], in0=XNv[i][:, sl], in1=pb_r[:], op=ALU.mult))
                        self.V(v, v.tensor_scalar(out=X2v[i][:, sl], in0=N1[:],
                                                  scalar1=lng[:, i:i + 1], scalar2=lnb[:, i:i + 1],
                                                  op0=ALU.mult, op1=ALU.add))
                block.vector(f_norm)

            x2scope.close()
            ps_s = [self.psum("ps_s%d" % k, [128, 512], stack=ph) for k in range(4)]
            ps_o = [self.psum("ps_o%d" % k, [128, 512], stack=ph) for k in range(4)]

            dv2_0 = dv2.n
            it = 0
            for h in range(4):
                for tt in range(4):
                    sl = slice(512 * tt, 512 * (tt + 1))
                    oset = (it % 2) * 2
                    po = (ps_o[oset], ps_o[oset + 1])
                    pe_b = pe1.n; pe2_b = pe2.n; dv_b = dv1.n; ac_b = ac1.n

                    def f_pe(t, h=h, sl=sl, po=po, dv_b=dv_b, ac_b=ac_b, it=it):
                        if it == 0:
                            t.wait_ge(gp.h, gp_w)
                            t.wait_ge(dv1.h, dv_b)
                        elif it >= 2:
                            t.wait_ge(dv2.h, dv2_0 + it - 1)
                        for s in range(32):
                            if s >= 4:
                                t.wait_ge(dv1.h, dv_b + s - 3)
                            t.matmul(ps_s[s % 4][:], kTt[:, 128 * s:128 * (s + 1)],
                                     X2v[2 * h][:, sl], start=True, stop=False)
                            pe1.inc(t.matmul(ps_s[s % 4][:],
                                             kTt[:, 4096 + 128 * s:4096 + 128 * (s + 1)],
                                             X2v[2 * h + 1][:, sl], start=False, stop=True))
                            if s >= 3:
                                sp = s - 3
                                t.wait_ge(ac1.h, ac_b + sp + 1)
                                t.matmul(po[0][:], Vt[:, 256 * sp:256 * sp + 128],
                                         PT[:, 512 * (sp % 4):512 * (sp % 4 + 1)],
                                         start=(sp == 0), stop=False)
                                pe2.inc(t.matmul(po[1][:], Vt[:, 256 * sp + 128:256 * (sp + 1)],
                                                 PT[:, 512 * (sp % 4):512 * (sp % 4 + 1)],
                                                 start=(sp == 0), stop=False))
                        for sp in (29, 30, 31):
                            t.wait_ge(ac1.h, ac_b + sp + 1)
                            t.matmul(po[0][:], Vt[:, 256 * sp:256 * sp + 128],
                                     PT[:, 512 * (sp % 4):512 * (sp % 4 + 1)],
                                     start=False, stop=(sp == 31))
                            pe2.inc(t.matmul(po[1][:], Vt[:, 256 * sp + 128:256 * (sp + 1)],
                                             PT[:, 512 * (sp % 4):512 * (sp % 4 + 1)],
                                             start=False, stop=(sp == 31)))
                    block.tensor(f_pe)

                    def f_dv(v, pe_b=pe_b, ac_b=ac_b):
                        for s in range(32):
                            v.wait_ge(pe1.h, pe_b + s + 1)
                            if s >= 4:
                                v.wait_ge(ac1.h, ac_b + s - 3)
                            dv1.inc(v.tensor_scalar(out=RB[:, 512 * (s % 4):512 * (s % 4 + 1)],
                                                    in0=ps_s[s % 4][:], scalar1=0.0,
                                                    scalar2=None, op0=ALU.max))
                    block.vector(f_dv)

                    def f_ac(sc, dv_b=dv_b, pe2_b=pe2_b):
                        for s in range(32):
                            sc.wait_ge(dv1.h, dv_b + s + 1)
                            if s >= 4:
                                sc.wait_ge(pe2.h, pe2_b + s - 3)
                            ac1.inc(sc.activation(out=PT[:, 512 * (s % 4):512 * (s % 4 + 1)],
                                                  in_=RB[:, 512 * (s % 4):512 * (s % 4 + 1)],
                                                  func=AF.Square))
                    block.scalar(f_ac)

                    def f_evac(v, h=h, sl=sl, po=po, pe2_b=pe2_b):
                        v.wait_ge(pe2.h, pe2_b + 32)
                        v.tensor_tensor(out=Xv[2 * h][:, sl], in0=Xv[2 * h][:, sl],
                                        in1=po[0][:], op=ALU.add)
                        dv2.inc(v.tensor_tensor(out=Xv[2 * h + 1][:, sl], in0=Xv[2 * h + 1][:, sl],
                                                in1=po[1][:], op=ALU.add))
                    block.vector(f_evac)
                    it += 1

    def moe(self, Xv, XNv, dram):
        block = self.block
        pe1, pe2, dv1, dv2, ac1, gp = self.pe1, self.pe2, self.dv1, self.dv2, self.ac1, self.gp
        onesr = self._onesr
        with ExitStack() as ph:
            W1At = self.sb("W1At", [128, 8 * 512], BF16, stack=ph)
            W1Bt = [self.sb("W1Bt%d" % k, [64, 4096], BF16, stack=ph) for k in range(2)]
            W2Bt = self.sb("W2Bt", [128, 4 * 1024], BF16, stack=ph)
            W2At = [self.sb("W2At%d" % k, [128, 32 * 64], BF16, stack=ph) for k in range(2)]
            Zt = self.sb("Zt", [64, 8 * 1024], BF16, stack=ph)
            Ht = self.sb("Ht", [128, 4 * 512], BF16, stack=ph)
            Gt = self.sb("Gt", [128, 4 * 1024], BF16, stack=ph)
            GS = self.sb("GS", [64, 1024], stack=ph)
            ewt = self.sb("ewt", [1, 2048], stack=ph)
            eit = self.sb("eit", [1, 2048], stack=ph)
            cfm = self.sb("cfm", [1, 2048], stack=ph)
            cfr = self.sb("cfr", [1, 1024], stack=ph)

            gpA, gpB = self.gpA, self.gpB

            def f_load(g):
                gp.inc(g.dma_start(out=W1At[:], in_=dram["W1A"][:]), 16)
                gp.inc(g.dma_start(out=W2Bt[:], in_=dram["W2B"][:]), 16)
                gp.inc(g.dma_start(out=ewt[:], in_=dram["ew"][:]), 16)
                gp.inc(g.dma_start(out=eit[:], in_=dram["ei"][:]), 16)
                gpA.inc(g.dma_start(out=W1Bt[0][:], in_=dram["W1B"][:, 0:4096]), 16)
                gpA.inc(g.dma_start(out=W2At[0][:], in_=dram["W2A"][:, 0:2048]), 16)
            block.gpsimd(f_load)
            gp_b = gp.n
            gpw_ready = {0: (gpA, gpA.n)}

            with ExitStack() as zp:
                pz = [self.psum("pz%d" % k, [128, 512], stack=zp) for k in range(2)]
                dvz0 = dv2.n
                for n, (rc, th) in enumerate([(rc, th) for rc in range(4) for th in range(2)]):
                    buf = n % 2
                    def f_pe(t, rc=rc, th=th, buf=buf, n=n):
                        if n == 0:
                            t.wait_ge(gp.h, gp_b)
                        elif n >= 2:
                            t.wait_ge(dv2.h, dvz0 + n - 1)
                        for i in range(NCH):
                            mm = t.matmul(pz[buf][:],
                                          W1At[:, 512 * i + 128 * rc:512 * i + 128 * (rc + 1)],
                                          XNv[i][:, 512 * th:512 * (th + 1)],
                                          start=(i == 0), stop=(i == NCH - 1))
                        pe1.inc(mm)
                    block.tensor(f_pe)
                    pe_e = pe1.n
                    def f_ev(v, rc=rc, th=th, buf=buf, pe_e=pe_e):
                        v.wait_ge(pe1.h, pe_e)
                        c0 = 1024 * (2 * rc) + 512 * th
                        c1 = 1024 * (2 * rc + 1) + 512 * th
                        v.tensor_copy(out=Zt[:, c0:c0 + 512], in_=pz[buf][0:64, :])
                        dv2.inc(v.tensor_copy(out=Zt[:, c1:c1 + 512], in_=pz[buf][64:128, :]))
                    block.vector(f_ev)

            with ExitStack() as ep:
                ph_ = [self.psum("ph%d" % k, [128, 512], stack=ep) for k in range(4)]
                pg = [self.psum("pg%d" % k, [64, 512], stack=ep) for k in range(2)]
                pc = [self.psum("pc%d" % k, [64, 512], stack=ep) for k in range(2)]
                for e in range(8):
                    if e < 7:
                        dv2_s = dv2.n
                        par = gpB if (e + 1) % 2 else gpA
                        def f_pre(g, e=e, dv2_s=dv2_s, par=par):
                            g.wait_ge(dv2.h, dv2_s)
                            par.inc(g.dma_start(out=W1Bt[(e + 1) % 2][:],
                                                in_=dram["W1B"][:, 4096 * (e + 1):4096 * (e + 2)]), 16)
                            par.inc(g.dma_start(out=W2At[(e + 1) % 2][:],
                                                in_=dram["W2A"][:, 2048 * (e + 1):2048 * (e + 2)]), 16)
                        block.gpsimd(f_pre)
                        gpw_ready[e + 1] = (par, par.n)

                    def f_coef(v, e=e):
                        self.V(v, v.tensor_scalar(out=cfm[:], in0=eit[:], scalar1=float(e),
                                                  scalar2=None, op0=ALU.is_equal))
                        self.V(v, v.tensor_tensor(out=cfm[:], in0=cfm[:], in1=ewt[:], op=ALU.mult))
                        cv = cfm.rearrange("a (t k) -> a t k", k=2)
                        cfrv = cfr.rearrange("a (t k) -> a t k", k=1)
                        self.V(v, v.tensor_tensor(out=cfrv[:, :, :], in0=cv[:, :, 0:1],
                                                  in1=cv[:, :, 1:2], op=ALU.add))
                    block.vector(f_coef)
                    dv_cf = dv1.n

                    gpw_sem, gpw_n = gpw_ready[e]
                    dv2_e = dv2.n
                    for th in range(2):
                        pe_b = pe1.n; ac_b = ac1.n; pe2_s = pe2.n

                        def f_pe(t, e=e, th=th, ac_b=ac_b, gpw_sem=gpw_sem, gpw_n=gpw_n, dv2_e=dv2_e):
                            if th == 0:
                                t.wait_ge(gpw_sem.h, gpw_n)
                                t.wait_ge(dv2.h, dv2_e)

                            def g_mm(f):
                                t.wait_ge(ac1.h, ac_b + f + 1)
                                pe2.inc(t.matmul(pg[th][:],
                                                 W2At[e % 2][:, 64 * f:64 * (f + 1)],
                                                 Ht[:, 512 * (f % 4):512 * (f % 4 + 1)],
                                                 start=(f == 0), stop=(f == 31)))
                            for f in range(32):
                                t.wait_ge(ac1.h, ac_b + max(0, f - 3))
                                pe1.inc(t.matmul(ph_[f % 4][:],
                                                 W1Bt[e % 2][:, 128 * f:128 * (f + 1)],
                                                 Zt[:, 1024 * e + 512 * th:1024 * e + 512 * (th + 1)],
                                                 start=True, stop=True))
                                if f >= 3:
                                    g_mm(f - 3)
                            g_mm(29); g_mm(30); g_mm(31)
                        block.tensor(f_pe)

                        def f_gelu(sc, pe_b=pe_b, pe2_s=pe2_s):
                            for f in range(32):
                                sc.wait_ge(pe1.h, pe_b + f + 1)
                                sc.wait_ge(pe2.h, pe2_s + max(0, f - 3))
                                ac1.inc(sc.activation(out=Ht[:, 512 * (f % 4):512 * (f % 4 + 1)],
                                                      in_=ph_[f % 4][:], func=AF.Gelu))
                        block.scalar(f_gelu)

                    pe2_gdone = pe2.n
                    def f_bc(t, dv_cf=dv_cf, dv2_e=dv2_e):
                        t.wait_ge(dv1.h, dv_cf)
                        t.wait_ge(dv2.h, dv2_e)
                        for th in range(2):
                            pe1.inc(t.matmul(pc[th][:], onesr[:, 0:64],
                                             cfr[:, 512 * th:512 * (th + 1)],
                                             start=True, stop=True))
                    block.tensor(f_bc)
                    pe_bc = pe1.n

                    def f_gs(sc, pe2_gdone=pe2_gdone):
                        sc.wait_ge(pe2.h, pe2_gdone)
                        for th in range(2):
                            ac1.inc(sc.activation(out=GS[:, 512 * th:512 * (th + 1)],
                                                  in_=pg[th][:], func=AF.Copy))
                    block.scalar(f_gs)
                    ac_gs = ac1.n

                    def f_g(v, e=e, ac_gs=ac_gs, pe_bc=pe_bc):
                        v.wait_ge(ac1.h, ac_gs)
                        v.wait_ge(pe1.h, pe_bc)
                        c = e // 2
                        r0 = 64 * (e % 2)
                        for th in range(2):
                            dv2.inc(v.tensor_tensor(
                                out=Gt[r0:r0 + 64, 1024 * c + 512 * th:1024 * c + 512 * (th + 1)],
                                in0=GS[:, 512 * th:512 * (th + 1)], in1=pc[th][:], op=ALU.mult))
                    block.vector(f_g)

            with ExitStack() as yp:
                py = [self.psum("py%d" % k, [128, 512], stack=yp) for k in range(2)]
                dvg = dv2.n
                for n, (i, th) in enumerate([(i, th) for i in range(NCH) for th in range(2)]):
                    buf = n % 2
                    def f_pe(t, i=i, th=th, buf=buf, n=n, dvg=dvg):
                        if n == 0:
                            t.wait_ge(dv2.h, dvg)
                        elif n >= 2:
                            t.wait_ge(dv2.h, dvg + n - 1)
                        for rc in range(4):
                            mm = t.matmul(py[buf][:],
                                          W2Bt[:, 1024 * rc + 128 * i:1024 * rc + 128 * (i + 1)],
                                          Gt[:, 1024 * rc + 512 * th:1024 * rc + 512 * (th + 1)],
                                          start=(rc == 0), stop=(rc == 3))
                        pe1.inc(mm)
                    block.tensor(f_pe)
                    pe_y = pe1.n
                    def f_ev(v, i=i, th=th, buf=buf, pe_y=pe_y):
                        v.wait_ge(pe1.h, pe_y)
                        dv2.inc(v.tensor_tensor(out=Xv[i][:, 512 * th:512 * (th + 1)],
                                                in0=Xv[i][:, 512 * th:512 * (th + 1)],
                                                in1=py[buf][:], op=ALU.add))
                    block.vector(f_ev)

    def ternarize(self, dram, w1t_dram, w2t_dram):
        block = self.block
        pe1, dv1, ac1, sy, gp = self.pe1, self.dv1, self.ac1, self.sy, self.gp
        ones, onesr = self._ones, self._onesr
        with ExitStack() as ph:
            WS = [self.sb("WS%d" % k, [128, 4096], stack=ph) for k in range(4)]
            T1 = self.sb("T1", [128, 4096], stack=ph)
            TB = [self.sb("TB%d" % k, [128, 4096], BF16, stack=ph) for k in range(2)]
            parts = self.sb("parts", [128, 8], stack=ph)
            accp = self.sb("accp", [128, 1], stack=ph)
            sc1 = self.sb("sc1", [1, 1], stack=ph)
            rS = self.sb("rS", [128, 1], stack=ph)
            p1 = self.psum("pt1", [1, 512], stack=ph)
            p2 = self.psum("pt2", [128, 512], stack=ph)

            syp = [self.syA, self.syB, self.syC, self.syD]
            gpp = [self.gpA, self.gpB]
            for (src, dst) in ((dram["W1f"], w1t_dram), (dram["W2f"], w2t_dram)):
                nch = 8
                dv_b = dv1.n
                ld_n = {}
                def f_ld(s, src=src, dv_b=dv_b, ld_n=ld_n):
                    for c in range(nch):
                        if c >= 4:
                            s.wait_ge(dv1.h, dv_b + c - 3)
                        syp[c % 4].inc(s.dma_start(out=WS[c % 4][:],
                                                   in_=src[:, 4096 * c:4096 * (c + 1)]), 16)
                        ld_n[c] = syp[c % 4].n
                block.sync(f_ld)

                def f_red(v, ld_n=ld_n):
                    for c in range(nch):
                        v.wait_ge(syp[c % 4].h, ld_n[c])
                        dv1.inc(v.tensor_reduce(out=parts[:, c:c + 1], in_=WS[c % 4][:],
                                                axis=AXX, op=ALU.add, apply_absolute_value=True))
                    v.wait_ge(dv1.h, dv1.n)
                    self.V(v, v.tensor_reduce(out=accp[:], in_=parts[:], axis=AXX, op=ALU.add))
                block.vector(f_red)
                dv_red = dv1.n

                def f_pe(t, dv_red=dv_red):
                    t.wait_ge(dv1.h, dv_red)
                    pe1.inc(t.matmul(p1[:, 0:1], accp[:], ones[:], start=True, stop=True))
                block.tensor(f_pe)
                pe_s = pe1.n

                def f_sc(v, pe_s=pe_s):
                    v.wait_ge(pe1.h, pe_s)
                    self.V(v, v.tensor_scalar(out=sc1[:], in0=p1[0:1, 0:1], scalar1=1.0 / (nch * 4096 * 128),
                                              scalar2=1e-8, op0=ALU.mult, op1=ALU.max))
                    self.V(v, v.reciprocal(out=sc1[:], in_=sc1[:]))
                block.vector(f_sc)
                dv_sc = dv1.n

                def f_bc(t, dv_sc=dv_sc):
                    t.wait_ge(dv1.h, dv_sc)
                    pe1.inc(t.matmul(p2[:, 0:1], onesr[:], sc1[:], start=True, stop=True))
                block.tensor(f_bc)
                pe_s2 = pe1.n

                def f_cp(v, pe_s2=pe_s2):
                    v.wait_ge(pe1.h, pe_s2)
                    self.V(v, v.tensor_copy(out=rS[:], in_=p2[:, 0:1]))
                block.vector(f_cp)
                dv_cp = dv1.n

                ac_b = ac1.n; dv_b5 = dv1.n
                ld2_n = {}
                def f_ld2(s, src=src, ac_b=ac_b, dv_cp=dv_cp, ld2_n=ld2_n):
                    for c in range(nch):
                        if c < 4:
                            s.wait_ge(dv1.h, dv_cp)
                        else:
                            s.wait_ge(ac1.h, ac_b + c - 3)
                        syp[c % 4].inc(s.dma_start(out=WS[c % 4][:],
                                                   in_=src[:, 4096 * c:4096 * (c + 1)]), 16)
                        ld2_n[c] = syp[c % 4].n
                block.sync(f_ld2)

                def f_act(sc, dv_cp=dv_cp, dv_b5=dv_b5, ld2_n=ld2_n):
                    sc.wait_ge(dv1.h, dv_cp)
                    for c in range(nch):
                        sc.wait_ge(syp[c % 4].h, ld2_n[c])
                        if c >= 1:
                            sc.wait_ge(dv1.h, dv_b5 + 2 * c)
                        ac1.inc(sc.activation(out=T1[:], in_=WS[c % 4][:], func=AF.Copy,
                                              scale=rS[:, 0:1], bias=MAGIC))
                block.scalar(f_act)

                st_n = {}
                gpp_base = (gpp[0].n, gpp[1].n)
                def f_tern(v, ac_b=ac_b, gpp_base=gpp_base):
                    for c in range(nch):
                        v.wait_ge(ac1.h, ac_b + c + 1)
                        if c >= 2:
                            v.wait_ge(gpp[c % 2].h, gpp_base[c % 2] + 16 * (c // 2))
                        self.V(v, v.tensor_scalar(out=T1[:], in0=T1[:], scalar1=MAGIC, scalar2=-1.0,
                                                  op0=ALU.subtract, op1=ALU.max))
                        self.V(v, v.tensor_scalar(out=TB[c % 2][:], in0=T1[:], scalar1=1.0,
                                                  scalar2=None, op0=ALU.min))
                block.vector(f_tern)

                def f_st(g, dst=dst, dv_b5=dv_b5):
                    for c in range(nch):
                        g.wait_ge(dv1.h, dv_b5 + 2 * (c + 1))
                        gpp[c % 2].inc(g.dma_start(out=dst[:, 4096 * c:4096 * (c + 1)],
                                                   in_=TB[c % 2][:]), 16)
                block.gpsimd(f_st)
                self.barrier()

    def l1_attn(self, Xv, XNv, dram, bq, bo, cc_in, cc_out):
        block = self.block
        pe1, pe2, dv1, dv2, ac1, gp, cc, sy = (self.pe1, self.pe2, self.dv1, self.dv2,
                                               self.ac1, self.gp, self.cc, self.sy)
        onesr, bcp2 = self._onesr, self._bcp2
        with ExitStack() as ph:
            wkB = self.sb("wkB", [128, 8 * 1024], BF16, stack=ph)
            wvB = self.sb("wvB", [128, 8 * 1024], BF16, stack=ph)
            bkB = self.sb("bkB", [128, 1024], stack=ph)
            bvB = self.sb("bvB", [128, 1024], stack=ph)
            bkr = self.sb("bkr_t", [1, 1024], stack=ph)
            bvr = self.sb("bvr_t", [1, 1024], stack=ph)
            KS1 = self.sb("KS1", [128, 2 * 512], stack=ph)
            KS2 = self.sb("KS2", [128, 2 * 512], stack=ph)
            KS3 = self.sb("KS3", [128, 2 * 512], stack=ph)
            KVbuf = self.sb("KVbuf", [64, 16 * 130], stack=ph)
            KFr = self.sb("KFr", [128, 3 * 1024], BF16, stack=ph)
            VPr = self.sb("VPr", [128, 3 * 16 * 65], BF16, stack=ph)
            vpv = VPr.rearrange("p (r h d) -> p r h d", r=3, h=16)

            def f_load(g):
                gp.inc(g.dma_start(out=wkB[:], in_=dram["wkT"][:]), 16)
                gp.inc(g.dma_start(out=wvB[:], in_=dram["wvT"][:]), 16)
            block.gpsimd(f_load)
            gp_w = gp.n

            def f_load2(s):
                sy.inc(s.dma_start(out=bkr[:], in_=dram["bkr"][:]), 16)
                sy.inc(s.dma_start(out=bvr[:], in_=dram["bvr"][:]), 16)
            block.sync(f_load2)
            sy_w = sy.n

            with ExitStack() as pp:
                pbb = self.psum("pbb", [128, 512], stack=pp)
                pe_b = pe1.n; dv_b = dv1.n

                def f_pebc(t, sy_w=sy_w, dv_b=dv_b):
                    t.wait_ge(sy.h, sy_w)
                    for j in range(4):
                        src = bkr if j < 2 else bvr
                        if j >= 1:
                            t.wait_ge(dv1.h, dv_b + j)
                        pe1.inc(t.matmul(pbb[:], onesr[:], src[:, 512 * (j % 2):512 * (j % 2 + 1)],
                                         start=True, stop=True))
                block.tensor(f_pebc)

                def f_dvbc(v, pe_b=pe_b):
                    for j in range(4):
                        dstt = bkB if j < 2 else bvB
                        v.wait_ge(pe1.h, pe_b + j + 1)
                        dv1.inc(v.tensor_copy(out=dstt[:, 512 * (j % 2):512 * (j % 2 + 1)],
                                              in_=pbb[:]))
                block.vector(f_dvbc)

                def f_ones(v):
                    dv1.inc(v.memset(vpv[:, :, :, 64:65], 1.0))
                block.vector(f_ones)

            with ExitStack() as kp:
                pk = [self.psum("pk%d" % k, [128, 512], stack=kp) for k in range(2)]
                kvp = [self.psum("kvp%d" % k, [64, 390 if k < 5 else 130], stack=kp)
                       for k in range(6)]
                pe_mom = {}
                ks1_done = {}
                vev_done = {}
                chain_done = {}

                def emit_moments(tc):
                    r = tc % 3
                    dv_tc = chain_done[tc]
                    def f_mom(t, tc=tc, r=r, dv_tc=dv_tc):
                        t.wait_ge(dv1.h, dv_tc)
                        slot = tc // 8
                        for h in range(16):
                            bank, off = h // 3, 130 * (h % 3)
                            mm = t.matmul(kvp[bank][:, off + 65 * slot:off + 65 * (slot + 1)],
                                          KFr[:, 1024 * r + 64 * h:1024 * r + 64 * (h + 1)],
                                          VPr[:, 1040 * r + 65 * h:1040 * r + 65 * (h + 1)],
                                          start=(tc == 0 and h % 3 == 0),
                                          stop=(tc == 15 and h % 3 == (2 if bank < 5 else 0)),
                                          skip_group_check=True)
                            if h == 15:
                                pe1.inc(mm)
                    block.tensor(f_mom)
                    pe_mom[tc] = pe1.n

                for tc in range(16):
                    r = tc % 3
                    pe_war = pe_mom.get(tc - 3, 0)
                    for jh in range(2):
                        def f_pe(t, tc=tc, jh=jh, first=(tc == 0 and jh == 0)):
                            if first:
                                t.wait_ge(gp.h, gp_w)
                                t.wait_ge(dv1.h, dv1.n)
                            prev = ks1_done.get((tc, jh - 1), ks1_done.get((tc - 1, 1), 0))
                            if prev:
                                t.wait_ge(dv1.h, prev)
                            for i in range(NCH):
                                mm = t.matmul(pk[0][:],
                                              XNv[i][:, 128 * tc:128 * (tc + 1)],
                                              wkB[:, 1024 * i + 512 * jh:1024 * i + 512 * (jh + 1)],
                                              start=(i == 0), stop=(i == NCH - 1))
                            pe1.inc(mm)
                        block.tensor(f_pe)
                        pe_kv = pe1.n
                        def f_ev(v, jh=jh, pe_kv=pe_kv):
                            v.wait_ge(pe1.h, pe_kv)
                            self.V(v, v.tensor_tensor(out=KS1[:, 512 * jh:512 * (jh + 1)],
                                                      in0=pk[0][:],
                                                      in1=bkB[:, 512 * jh:512 * (jh + 1)], op=ALU.add))
                            self.V(v, v.tensor_scalar(out=KS2[:, 512 * jh:512 * (jh + 1)],
                                                      in0=KS1[:, 512 * jh:512 * (jh + 1)],
                                                      scalar1=0.0, scalar2=None, op0=ALU.min))
                        block.vector(f_ev)
                        ks1_done[(tc, jh)] = dv1.n - 1
                        dv_k1 = dv1.n
                        def f_exp(sc, jh=jh, dv_k1=dv_k1):
                            sc.wait_ge(dv1.h, dv_k1)
                            ac1.inc(sc.activation(out=KS3[:, 512 * jh:512 * (jh + 1)],
                                                  in_=KS2[:, 512 * jh:512 * (jh + 1)], func=AF.Exp))
                        block.scalar(f_exp)
                        ac_k = ac1.n
                        def f_ev2(v, r=r, jh=jh, ac_k=ac_k, pe_war=pe_war):
                            v.wait_ge(ac1.h, ac_k)
                            if pe_war:
                                v.wait_ge(pe1.h, pe_war)
                            self.V(v, v.tensor_scalar(out=KS2[:, 512 * jh:512 * (jh + 1)],
                                                      in0=KS1[:, 512 * jh:512 * (jh + 1)],
                                                      scalar1=0.0, scalar2=None, op0=ALU.max))
                            c0 = 1024 * r + 512 * jh
                            self.V(v, v.tensor_tensor(out=KFr[:, c0:c0 + 512],
                                                      in0=KS3[:, 512 * jh:512 * (jh + 1)],
                                                      in1=KS2[:, 512 * jh:512 * (jh + 1)], op=ALU.add))
                        block.vector(f_ev2)
                    for jh in range(2):
                        def f_pe(t, tc=tc, jh=jh):
                            prev = vev_done.get((tc, jh - 1), vev_done.get((tc - 1, 1), 0))
                            if prev:
                                t.wait_ge(dv1.h, prev)
                            for i in range(NCH):
                                mm = t.matmul(pk[1][:],
                                              XNv[i][:, 128 * tc:128 * (tc + 1)],
                                              wvB[:, 1024 * i + 512 * jh:1024 * i + 512 * (jh + 1)],
                                              start=(i == 0), stop=(i == NCH - 1))
                            pe1.inc(mm)
                        block.tensor(f_pe)
                        pe_kv = pe1.n
                        def f_ev(v, r=r, jh=jh, pe_kv=pe_kv, pe_war=pe_war):
                            v.wait_ge(pe1.h, max(pe_kv, pe_war))
                            pv = pk[1][:].rearrange("p (one h d) -> p one h d", one=1, h=8)
                            bv = bvB[:, 512 * jh:512 * (jh + 1)].rearrange(
                                "p (one h d) -> p one h d", one=1, h=8)
                            dv1.inc(v.tensor_tensor(
                                out=vpv[:, r:r + 1, 8 * jh:8 * (jh + 1), 0:64],
                                in0=pv, in1=bv, op=ALU.add))
                        block.vector(f_ev)
                        vev_done[(tc, jh)] = dv1.n
                    chain_done[tc] = dv1.n
                    if tc >= 1:
                        emit_moments(tc - 1)
                emit_moments(15)

                pe_all = pe1.n
                def f_evm(v, pe_all=pe_all):
                    v.wait_ge(pe1.h, pe_all)
                    for b in range(6):
                        n_h = 3 if b < 5 else 1
                        dv1.inc(v.tensor_copy(out=KVbuf[:, 390 * b:390 * b + 130 * n_h],
                                              in_=kvp[b][:]))
                block.vector(f_evm)

            dv_kv = dv1.n
            def f_cc(g, dv_kv=dv_kv):
                g.wait_ge(dv1.h, dv_kv)
                gp.inc(g.dma_start(out=cc_in[:], in_=KVbuf[:]), 16)
                g.wait_ge(gp.h, gp.n)
                cc.inc(g.collective_compute("AllReduce", ALU.add,
                                            [[0, 1, 2, 3], [4, 5, 6, 7]],
                                            [cc_in.opt()], [cc_out.opt()]))
            block.gpsimd(f_cc)

        with ExitStack() as ph:
            woB = self.sb("woB", [128, 8 * 1024], BF16, stack=ph)
            QF = self.sb("QF", [128, 8 * 2048], BF16, stack=ph)
            arena = self.sb("arena", [128, 16 * 130], BF16, stack=ph)
            BD = self.sb("BD", [128, 16 * 128], BF16, stack=ph)
            KS = self.sb("KSs", [128, 16 * 2], BF16, stack=ph)
            ar5 = arena.rearrange("p (pair slot half j) -> p pair slot half j",
                                  pair=8, slot=2, half=2)

            def f_load(g):
                gp.inc(g.dma_start(out=woB[:], in_=dram["woT"][:]), 16)
            block.gpsimd(f_load)

            def f_ze(v):
                dv1.inc(v.memset(arena[:], 0.0))
            block.vector(f_ze)
            dv_ze = dv1.n
            cc_n = cc.n

            def f_ar(g, dv_ze=dv_ze, cc_n=cc_n):
                g.wait_ge(cc.h, cc_n)
                g.wait_ge(dv1.h, dv_ze)
                src5 = cc_out.rearrange("d (h2 par slot j) -> d h2 par slot j",
                                        h2=8, par=2, slot=2)
                self.gpB.inc(g.dma_start(out=ar5[0:64, :, :, 0:1, :].opt(),
                                         in_=src5[:, :, 0:1, :, :].opt()), 16)
                self.gpB.inc(g.dma_start(out=ar5[64:128, :, :, 1:2, :].opt(),
                                         in_=src5[:, :, 1:2, :, :].opt()), 16)
            block.gpsimd(f_ar)
            gp_ar = self.gpB.n

            with ExitStack() as qp:
                wqB = self.sb("wqB", [128, 8 * 1024], BF16, stack=qp)
                QS1 = self.sb("QS1", [128, 2 * 512], stack=qp)
                QS2 = self.sb("QS2", [128, 2 * 512], stack=qp)
                QS3 = self.sb("QS3", [128, 2 * 512], stack=qp)
                def f_loadq(g):
                    gp.inc(g.dma_start(out=wqB[:], in_=dram["wqT"][:]), 16)
                block.gpsimd(f_loadq)
                gp_w2 = gp.n
                pq = [self.psum("pq%d" % k, [128, 512], stack=qp) for k in range(2)]
                qs_done = {}; exp_done = {}; qf_done = {}
                for n, (ic, tt) in enumerate([(ic, tt) for ic in range(NCH) for tt in range(4)]):
                    b = n % 2
                    bs = slice(512 * b, 512 * (b + 1))
                    def f_pe(t, ic=ic, tt=tt, b=b, n=n):
                        if n == 0:
                            t.wait_ge(gp.h, gp_w2)
                        elif n >= 2:
                            t.wait_ge(dv1.h, qs_done[n - 2])
                        for i in range(NCH):
                            mm = t.matmul(pq[b][:],
                                          wqB[:, 1024 * i + 128 * ic:1024 * i + 128 * (ic + 1)],
                                          XNv[i][:, 512 * tt:512 * (tt + 1)],
                                          start=(i == 0), stop=(i == NCH - 1))
                        pe1.inc(mm)
                    block.tensor(f_pe)
                    pe_q = pe1.n
                    def f_ev(v, ic=ic, b=b, bs=bs, n=n, pe_q=pe_q):
                        v.wait_ge(pe1.h, pe_q)
                        if n >= 2:
                            v.wait_ge(ac1.h, exp_done[n - 2])
                            v.wait_ge(dv1.h, qf_done[n - 2])
                        v.tensor_scalar(out=QS1[:, bs], in0=pq[b][:], scalar1=bq[:, ic:ic + 1],
                                        scalar2=0.0, op0=ALU.add, op1=ALU.min)
                        dv1.inc(v.tensor_scalar(out=QS2[:, bs], in0=pq[b][:], scalar1=bq[:, ic:ic + 1],
                                                scalar2=0.0, op0=ALU.add, op1=ALU.max))
                    block.vector(f_ev)
                    qs_done[n] = dv1.n
                    def f_exp(sc, b=b, bs=bs, n=n, dv_q=qs_done[n]):
                        sc.wait_ge(dv1.h, dv_q)
                        if n >= 2:
                            sc.wait_ge(dv1.h, qf_done[n - 2])
                        ac1.inc(sc.activation(out=QS3[:, bs], in_=QS1[:, bs], func=AF.Exp))
                    block.scalar(f_exp)
                    exp_done[n] = ac1.n
                    def f_ev2(v, ic=ic, tt=tt, bs=bs, ac_q=exp_done[n]):
                        v.wait_ge(ac1.h, ac_q)
                        c0 = 2048 * ic + 512 * tt
                        dv1.inc(v.tensor_tensor(out=QF[:, c0:c0 + 512],
                                                in0=QS3[:, bs], in1=QS2[:, bs], op=ALU.add))
                    block.vector(f_ev2)
                    qf_done[n] = dv1.n

            ATT = self.sb("ATT", [128, 8 * 2048], BF16, stack=ph)

            dv_bd0 = dv1.n
            def f_bd(v, gp_ar=gp_ar):
                v.wait_ge(self.gpB.h, gp_ar)
                for p in range(8):
                    for slot in range(2):
                        idx = 2 * p + slot
                        src = ar5[:, p:p + 1, slot:slot + 1, :, :]
                        v.tensor_copy(
                            out=BD[:, 128 * idx:128 * (idx + 1)].rearrange(
                                "p (one one2 half j) -> p one one2 half j", one=1, one2=1, half=2),
                            in_=src[:, :, :, :, 0:64])
                        dv1.inc(v.tensor_copy(
                            out=KS[:, 2 * idx:2 * (idx + 1)].rearrange(
                                "p (one one2 half j) -> p one one2 half j", one=1, one2=1, half=2),
                            in_=src[:, :, :, :, 64:65]))
            block.vector(f_bd)

            with ExitStack() as ap:
                patt = [self.psum("patt%d" % k, [128, 512], stack=ap) for k in range(2)]
                pden = [self.psum("pden%d" % k, [2, 512], stack=ap) for k in range(2)]
                pdb = [self.psum("pdb%d" % k, [128, 512], stack=ap) for k in range(2)]
                dn2 = self.sb("dn2", [2, 2 * 512], stack=ap)
                AS2 = self.sb("AS2", [128, 2 * 512], stack=ap)
                att_done = {}
                ln_done = {}
                db_done = {}
                dv1_mx = {}
                for n, (p, slot, tt2) in enumerate([(p, s_, t_) for p in range(8)
                                                    for s_ in range(2) for t_ in range(2)]):
                    idx = 2 * p + slot
                    col = 1024 * slot + 512 * tt2
                    b = n % 2
                    dv_now = dv1.n
                    def f_pe(t, idx=idx, p=p, col=col, n=n, b=b, dv_now=dv_now):
                        if n == 0:
                            t.wait_ge(dv1.h, dv_now)
                        if n >= 2:
                            t.wait_ge(dv2.h, att_done[n - 2])
                        t.matmul(patt[b][:], BD[:, 128 * idx:128 * (idx + 1)],
                                 QF[:, 2048 * p + col:2048 * p + col + 512],
                                 start=True, stop=True)
                        pe1.inc(t.matmul(pden[b][:], KS[:, 2 * idx:2 * (idx + 1)],
                                         QF[:, 2048 * p + col:2048 * p + col + 512],
                                         start=True, stop=True))
                    block.tensor(f_pe)
                    pe_a = pe1.n
                    def f_dn(v, pe_a=pe_a, b=b, n=n):
                        v.wait_ge(pe1.h, pe_a)
                        if n >= 2:
                            v.wait_ge(ac1.h, ln_done[n - 2])
                            v.wait_ge(pe2.h, db_done[n - 2])
                        dv1.inc(v.tensor_scalar(out=dn2[:, 512 * b:512 * (b + 1)],
                                                in0=pden[b][:], scalar1=1e-6,
                                                scalar2=None, op0=ALU.max))
                    block.vector(f_dn)
                    dv1_mx[n] = dv1.n
                    def f_lnx(sc, b=b, n=n, dvw=dv1_mx[n]):
                        sc.wait_ge(dv1.h, dvw)
                        self.V(sc, sc.activation(out=dn2[:, 512 * b:512 * (b + 1)],
                                                 in_=dn2[:, 512 * b:512 * (b + 1)], func=AF.Ln),
                               sem=ac1)
                        ac1.inc(sc.activation(out=dn2[:, 512 * b:512 * (b + 1)],
                                              in_=dn2[:, 512 * b:512 * (b + 1)], func=AF.Exp,
                                              scale=-1.0))
                    block.scalar(f_lnx)
                    ln_done[n] = ac1.n
                    def f_db(t, b=b, acw=ln_done[n]):
                        t.wait_ge(ac1.h, acw)
                        pe2.inc(t.matmul(pdb[b][:], bcp2[:], dn2[:, 512 * b:512 * (b + 1)],
                                         start=True, stop=True))
                    block.tensor(f_db)
                    pe2_db = pe2.n
                    db_done[n] = pe2.n
                    def f_as(sc, pe_a=pe_a, b=b, n=n):
                        sc.wait_ge(pe1.h, pe_a)
                        if n >= 2:
                            sc.wait_ge(dv2.h, att_done[n - 2])
                        ac1.inc(sc.activation(out=AS2[:, 512 * b:512 * (b + 1)],
                                              in_=patt[b][:], func=AF.Copy))
                    block.scalar(f_as)
                    ac_as = ac1.n
                    def f_att(v, p=p, col=col, b=b, ac_as=ac_as, pe2_db=pe2_db):
                        v.wait_ge(ac1.h, ac_as)
                        v.wait_ge(pe2.h, pe2_db)
                        dv2.inc(v.tensor_tensor(out=ATT[:, 2048 * p + col:2048 * p + col + 512],
                                                in0=AS2[:, 512 * b:512 * (b + 1)],
                                                in1=pdb[b][:], op=ALU.mult))
                    block.vector(f_att)
                    att_done[n] = dv2.n

                po = [self.psum("po%d" % k, [128, 512], stack=ap) for k in range(2)]
                dv2_q = dv2.n
                ev_done = {}
                for n, (ic, tt) in enumerate([(ic, tt) for ic in range(NCH) for tt in range(4)]):
                    buf = n % 2
                    def f_pe(t, ic=ic, tt=tt, buf=buf, n=n):
                        if n == 0:
                            t.wait_ge(dv2.h, dv2_q)
                        elif n >= 2:
                            t.wait_ge(dv2.h, ev_done[n - 2])
                        for j in range(NCH):
                            mm = t.matmul(po[buf][:],
                                          woB[:, 1024 * j + 128 * ic:1024 * j + 128 * (ic + 1)],
                                          ATT[:, 2048 * j + 512 * tt:2048 * j + 512 * (tt + 1)],
                                          start=(j == 0), stop=(j == NCH - 1))
                        pe1.inc(mm)
                    block.tensor(f_pe)
                    pe_o = pe1.n
                    def f_ev(v, ic=ic, tt=tt, buf=buf, pe_o=pe_o):
                        v.wait_ge(pe1.h, pe_o)
                        dv2.inc(v.scalar_tensor_tensor(
                            out=Xv[ic][:, 512 * tt:512 * (tt + 1)],
                            in0=po[buf][:], scalar=bo[:, ic:ic + 1],
                            in1=Xv[ic][:, 512 * tt:512 * (tt + 1)],
                            op0=ALU.add, op1=ALU.add))
                    block.vector(f_ev)
                    ev_done[n] = dv2.n

    def l1_ffn(self, Xv, XNv, dram, b1, b2, w1t_dram, w2t_dram, out):
        block = self.block
        pe1, pe2, dv1, dv2, ac1, gp, sy, st = (self.pe1, self.pe2, self.dv1, self.dv2,
                                               self.ac1, self.gp, self.sy, self.st)
        with ExitStack() as ph:
            W1tB = self.sb("W1tB", [128, 8 * 4096], BF16, stack=ph)
            HT = self.sb("HT", [128, 32 * 512], BF16, stack=ph)
            W2tB = [self.sb("W2tB%d" % k, [128, 1024], BF16, stack=ph) for k in range(2)]

            def f_ld(g):
                for j in range(4):
                    gp.inc(g.dma_start(out=W1tB[:, 8192 * j:8192 * (j + 1)],
                                       in_=w1t_dram[:, 8192 * j:8192 * (j + 1)]), 16)
            block.gpsimd(f_ld)
            gp_w1 = gp.n

            for tt in range(4):
                with ExitStack() as tp:
                    phh = [self.psum("phh%d_%d" % (tt, k), [128, 512], stack=tp) for k in range(2)]
                    ac_b = ac1.n; pe_b = pe1.n; dv2_prev = dv2.n
                    def f_pe(t, tt=tt, ac_b=ac_b, dv2_prev=dv2_prev, first=(tt == 0)):
                        if first:
                            t.wait_ge(gp.h, gp_w1)
                        else:
                            t.wait_ge(dv2.h, dv2_prev)
                        for f in range(32):
                            buf = f % 2
                            if f >= 2:
                                t.wait_ge(ac1.h, ac_b + f - 1)
                            for i in range(NCH):
                                mm = t.matmul(phh[buf][:],
                                              W1tB[:, 4096 * i + 128 * f:4096 * i + 128 * (f + 1)],
                                              XNv[i][:, 512 * tt:512 * (tt + 1)],
                                              start=(i == 0), stop=(i == NCH - 1))
                            pe1.inc(mm)
                    block.tensor(f_pe)
                    def f_gelu(sc, pe_b=pe_b):
                        for f in range(32):
                            buf = f % 2
                            sc.wait_ge(pe1.h, pe_b + f + 1)
                            ac1.inc(sc.activation(out=HT[:, 512 * f:512 * (f + 1)],
                                                  in_=phh[buf][:], func=AF.Gelu,
                                                  bias=b1[:, f:f + 1]))
                    block.scalar(f_gelu)
                    ac_done = ac1.n

                with ExitStack() as tp:
                    py8 = [self.psum("py8_%d_%d" % (tt, k), [128, 512], stack=tp) for k in range(8)]
                    gpp = [self.gpA, self.gpB]
                    pe2_b = pe2.n
                    w2_n = {}
                    def f_w2(g, pe2_b=pe2_b, w2_n=w2_n):
                        for f in range(32):
                            if f >= 2:
                                g.wait_ge(pe2.h, pe2_b + f - 1)
                            else:
                                g.wait_ge(pe2.h, pe2_b)
                            gpp[f % 2].inc(g.dma_start(out=W2tB[f % 2][:],
                                                       in_=w2t_dram[:, 1024 * f:1024 * (f + 1)]), 16)
                            w2_n[f] = gpp[f % 2].n
                    block.gpsimd(f_w2)

                    def f_pe2(t, ac_done=ac_done, w2_n=w2_n):
                        t.wait_ge(ac1.h, ac_done)
                        for f in range(32):
                            t.wait_ge(gpp[f % 2].h, w2_n[f])
                            for ic in range(NCH):
                                mm = t.matmul(py8[ic][:], W2tB[f % 2][:, 128 * ic:128 * (ic + 1)],
                                              HT[:, 512 * f:512 * (f + 1)],
                                              start=(f == 0), stop=(f == 31))
                            pe2.inc(mm)
                    block.tensor(f_pe2)
                    pe2_done = pe2.n

                    def f_ev(v, tt=tt, pe2_done=pe2_done):
                        v.wait_ge(pe2.h, pe2_done)
                        for ic in range(NCH):
                            dv2.inc(v.scalar_tensor_tensor(
                                out=Xv[ic][:, 512 * tt:512 * (tt + 1)],
                                in0=py8[ic][:], scalar=b2[:, ic:ic + 1],
                                in1=Xv[ic][:, 512 * tt:512 * (tt + 1)],
                                op0=ALU.add, op1=ALU.add))
                    block.vector(f_ev)
                    dv2_done = dv2.n

                    def f_st(s, tt=tt, dv2_done=dv2_done):
                        s.wait_ge(dv2.h, dv2_done)
                        for ic in range(NCH):
                            st.inc(s.dma_start(out=out[:, T * ic + 512 * tt:T * ic + 512 * (tt + 1)],
                                               in_=Xv[ic][:, 512 * tt:512 * (tt + 1)]), 16)
                    block.sync(f_st)

            def f_fin(s):
                s.wait_ge(st.h, st.n)
            block.sync(f_fin)



def _pack(w):
    R, C = w.shape
    k = R // 128
    return np.ascontiguousarray(w.reshape(k, 128, C).transpose(1, 0, 2).reshape(128, k * C))


def _col_tile(vec):
    k = vec.shape[0] // 128
    return np.ascontiguousarray(vec.reshape(k, 128).T)


def _host_inputs(inputs):
    x = np.ascontiguousarray(inputs["L0_attn_residual"], dtype=np.float32)
    xflat = x.reshape(16384, 1024)
    sh = {}
    sh["kT"] = _pack(np.ascontiguousarray(inputs["L0_attn_w_fc"].reshape(4096, 256).T))
    sh["V"] = _pack(np.ascontiguousarray(inputs["L0_attn_w_proj"].T.reshape(4096, 256)))
    sh["lng"] = _col_tile(inputs["L0_attn_ln_g"])
    sh["lnb"] = _col_tile(inputs["L0_attn_ln_b"])
    sh["W1A"] = _pack(np.ascontiguousarray(inputs["L0_ffn_w1_A"].T))
    sh["W1B"] = np.ascontiguousarray(
        inputs["L0_ffn_w1_B"].reshape(8, 64, 4096).transpose(1, 0, 2).reshape(64, 8 * 4096))
    sh["W2A"] = np.ascontiguousarray(
        inputs["L0_ffn_w2_A"].reshape(8, 64, 32, 128).transpose(3, 0, 2, 1).reshape(128, 8 * 32 * 64))
    sh["W2B"] = _pack(np.ascontiguousarray(inputs["L0_ffn_w2_B"]))
    sh["wqT"] = _pack(np.ascontiguousarray(inputs["L1_attn_wq"].T))
    sh["wkT"] = _pack(np.ascontiguousarray(inputs["L1_attn_wk"].T))
    sh["wvT"] = _pack(np.ascontiguousarray(inputs["L1_attn_wv"].T))
    sh["woT"] = _pack(np.ascontiguousarray(inputs["L1_attn_wo"].T))
    sh["bq"] = _col_tile(inputs["L1_attn_bq"])
    sh["bo"] = _col_tile(inputs["L1_attn_bo"])
    sh["bkr"] = np.ascontiguousarray(inputs["L1_attn_bk"].reshape(1, 1024))
    sh["bvr"] = np.ascontiguousarray(inputs["L1_attn_bv"].reshape(1, 1024))
    sh["W1f"] = _pack(np.ascontiguousarray(inputs["L1_ffn_W1"].T))
    sh["W2f"] = _pack(np.ascontiguousarray(inputs["L1_ffn_W2"].T))
    sh["b1"] = _col_tile(inputs["L1_ffn_b1"])
    sh["b2"] = _col_tile(inputs["L1_ffn_b2"])
    ew = np.ascontiguousarray(inputs["L0_ffn_expert_weights"], dtype=np.float32)
    ei = np.ascontiguousarray(inputs["L0_ffn_expert_indices"], dtype=np.int32)
    in_maps = []
    for c in range(NCORE):
        m = dict(sh)
        idx = np.r_[1024 * c:1024 * (c + 1), 8192 + 1024 * c:8192 + 1024 * (c + 1)]
        m["xT"] = _pack(np.ascontiguousarray(xflat[idx].T))
        m["ew"] = np.ascontiguousarray(ew[1024 * c:1024 * (c + 1)].reshape(1, 2048))
        m["ei"] = np.ascontiguousarray(ei[1024 * c:1024 * (c + 1)].reshape(1, 2048))
        in_maps.append(m)
    return in_maps


def _unpack_out(res, key="outT"):
    o = np.empty((16384, 1024), dtype=np.float32)
    for c in range(NCORE):
        arr = np.asarray(res[c][key])
        outT = arr.reshape(128, 8, 2048).transpose(1, 0, 2).reshape(1024, 2048)
        idx = np.r_[1024 * c:1024 * (c + 1), 8192 + 1024 * c:8192 + 1024 * (c + 1)]
        o[idx] = outT.T
    return o.reshape(4, 4096, 1024)


_KEYS = {
    (0, 1): None,
    (0,): ["xT", "kT", "V", "lng", "lnb", "W1A", "W1B", "W2A", "W2B", "ew", "ei"],
    (1,): ["xT", "wqT", "wkT", "wvT", "woT", "bq", "bo", "bkr", "bvr",
           "W1f", "W2f", "b1", "b2"],
}


def run_layers(in_maps, layers=(0, 1), dump=None, trace=False):
    nc = Build(layers=layers, dump=dump).build()
    keep = _KEYS[tuple(layers)]
    maps = [{k: v for k, v in m.items() if keep is None or k in keep} for m in in_maps]
    return run_bass_kernel_spmd(nc, maps, core_ids=list(range(NCORE)), trace=trace)


def kernel(**inputs):
    in_maps = _host_inputs(inputs)
    res = run_layers(in_maps, layers=(0, 1))
    return _unpack_out(res.results)


# version 10
# speedup vs baseline: 1.0718x; 1.0314x over previous
import numpy as np
from contextlib import ExitStack

import concourse.bass as bass
import concourse.mybir as mybir
from concourse.bass_utils import run_bass_kernel_spmd

F32 = mybir.dt.float32
BF16 = mybir.dt.bfloat16
I32 = mybir.dt.int32
AF = mybir.ActivationFunctionType
ALU = mybir.AluOpType
AXX = mybir.AxisListType.X

MAGIC = 12582912.0
NCORE = 8
T = 2048
NCH = 8
EPS0 = 1e-5


class S:
    def __init__(self, nc, es, name):
        self.h = es.enter_context(nc.semaphore(name))
        self.n = 0

    def inc(self, inst, k=1):
        inst.then_inc(self.h, k)
        self.n += k
        return self.n


class Build:
    def __init__(self, layers=(0, 1), dump=None):
        self.layers = layers
        self.dump = dump
        self.nc = bass.Bass("TRN2", target_bir_lowering=False)
        self.es = ExitStack()

    def sb(self, name, shape, dt=F32, stack=None):
        return (stack or self.es).enter_context(self.nc.sbuf_tensor(name, shape, dt))

    def psum(self, name, shape, dt=F32, stack=None):
        return (stack or self.es).enter_context(self.nc.psum_tensor(name, shape, dt))

    def V(self, eng, inst, sem=None):
        s = sem if sem is not None else self.dv1
        s.inc(inst)
        eng.wait_ge(s.h, s.n)
        return inst

    def barrier(self):
        sems = [self.pe1, self.pe2, self.dv1, self.dv2, self.ac1, self.sy, self.gp, self.cc,
                self.gpA, self.gpB, self.syA, self.syB, self.syC, self.syD]
        for emit in (self.block.tensor, self.block.vector, self.block.scalar,
                     self.block.sync, self.block.gpsimd):
            def _f(eng, sems=sems):
                for s in sems:
                    if s.n > 0:
                        eng.wait_ge(s.h, s.n)
            emit(_f)

    def build(self):
        nc = self.nc
        es = self.es
        dram = {}

        def din(name, shape, dt=F32):
            dram[name] = nc.dram_tensor(name, shape, dt, kind="ExternalInput").ap()
            return dram[name]

        l0 = 0 in self.layers
        l1 = 1 in self.layers
        din("xT", [128, NCH * T])
        if l0:
            din("kT", [128, 2 * 4096]); din("V", [128, 32 * 256])
            din("lng", [128, 8]); din("lnb", [128, 8])
            din("W1A", [128, 8 * 512]); din("W1B", [64, 8 * 4096])
            din("W2A", [128, 8 * 32 * 64]); din("W2B", [128, 4 * 1024])
            din("ew", [1, 2048]); din("ei", [1, 2048], I32)
        if l1:
            din("wqT", [128, 8 * 1024]); din("wkT", [128, 8 * 1024])
            din("wvT", [128, 8 * 1024]); din("woT", [128, 8 * 1024])
            din("bq", [128, 8]); din("bo", [128, 8])
            din("bkr", [1, 1024]); din("bvr", [1, 1024])
            din("W1f", [128, 8 * 4096]); din("W2f", [128, 32 * 1024])
            din("b1", [128, 32]); din("b2", [128, 8])
        out = nc.dram_tensor("outT", [128, NCH * T], F32, kind="ExternalOutput").ap()
        dbg = {}
        if self.dump:
            dbg["dXN"] = nc.dram_tensor("dXN", [128, NCH * T], BF16, kind="ExternalOutput").ap()
        if l1:
            cc_in = nc.dram_tensor("cc_in", [64, 16 * 130], F32, kind="Internal").ap()
            cc_out = nc.dram_tensor("cc_out", [64, 16 * 130], F32, kind="Internal").ap()
            w1t_dram = nc.dram_tensor("w1t_dram", [128, 8 * 4096], BF16, kind="Internal").ap()
            w2t_dram = nc.dram_tensor("w2t_dram", [128, 32 * 1024], BF16, kind="Internal").ap()

        with es:
            ACC = self.sb("ACC", [128, NCH * T], F32)
            XN = self.sb("XN", [128, NCH * T], BF16)
            self._ones = self.sb("ones", [128, 1], F32)
            self._onesr = self.sb("onesr", [1, 128], F32)
            self._bcp2 = self.sb("bcp2", [2, 128], F32)
            smalls = []
            lng = lnb = bq = bo = b1 = b2 = None
            if l0:
                lng = self.sb("lng_t", [128, 8]); lnb = self.sb("lnb_t", [128, 8])
                smalls += [("lng", lng), ("lnb", lnb)]
            if l1:
                bq = self.sb("bq_t", [128, 8]); bo = self.sb("bo_t", [128, 8])
                b1 = self.sb("b1_t", [128, 32]); b2 = self.sb("b2_t", [128, 8])
                smalls += [("bq", bq), ("bo", bo), ("b1", b1), ("b2", b2)]

            self.pe1 = S(nc, es, "pe1"); self.pe2 = S(nc, es, "pe2")
            self.dv1 = S(nc, es, "dv1"); self.dv2 = S(nc, es, "dv2")
            self.ac1 = S(nc, es, "ac1")
            self.sy = S(nc, es, "sy"); self.gp = S(nc, es, "gp")
            self.cc = S(nc, es, "cc"); self.st = S(nc, es, "st")
            self.gpA = S(nc, es, "gpA"); self.gpB = S(nc, es, "gpB")
            self.syA = S(nc, es, "syA"); self.syB = S(nc, es, "syB")
            self.syC = S(nc, es, "syC"); self.syD = S(nc, es, "syD")
            self.gpC = S(nc, es, "gpC"); self.gpD = S(nc, es, "gpD")
            sy, gp, st = self.sy, self.gp, self.st

            Xv = [ACC[:, T * i:T * (i + 1)] for i in range(NCH)]
            XNv = [XN[:, T * i:T * (i + 1)] for i in range(NCH)]

            blk_cm = nc.Block()
            self.block = block = blk_cm.__enter__()
            try:
                def p0_sync(sync):
                    for j in range(4):
                        sy.inc(sync.dma_start(out=ACC[:, 4096 * j:4096 * (j + 1)],
                                              in_=dram["xT"][:, 4096 * j:4096 * (j + 1)]), 16)
                block.sync(p0_sync)

                def p0_gp(g):
                    for nm, t_ in smalls:
                        gp.inc(g.dma_start(out=t_[:], in_=dram[nm][:]), 16)
                block.gpsimd(p0_gp)

                def p0_v(v):
                    v.memset(self._ones[:], 1.0)
                    self.dv1.inc(v.memset(self._onesr[:], 1.0))
                block.vector(p0_v)

                bcp_np = np.zeros((2, 128), dtype=np.float32)
                bcp_np[0, 0:64] = 1.0
                bcp_np[1, 64:128] = 1.0
                bcp_const = nc.inline_tensor(bcp_np, name="bcp_const").ap()

                def p0_gp2(g):
                    gp.inc(g.dma_start(out=self._bcp2[:], in_=bcp_const[:]), 16)
                block.gpsimd(p0_gp2)

                self.barrier()

                if l0:
                    with ExitStack() as l0scope:
                        self._keep_stack = l0scope
                        self.ln_phase(Xv, XNv, fold=True)
                        self.barrier()
                        if self.dump == "p1":
                            self.finish(out, ACC, XN, dbg); return nc
                        self.l0_attn(Xv, XNv, dram, lng, lnb)
                        self.barrier()
                    if self.dump == "p3":
                        self.finish(out, ACC, XN, dbg); return nc
                    self.moe(Xv, XNv, dram)
                    self.barrier()
                if not l1 or self.dump == "p4":
                    self.finish(out, ACC, XN, dbg); return nc

                self.ternarize(dram, w1t_dram, w2t_dram)
                if self.dump == "p4b":
                    dW1 = nc.dram_tensor("dW1t", [128, 8 * 4096], BF16, kind="ExternalOutput").ap()
                    dW2 = nc.dram_tensor("dW2t", [128, 32 * 1024], BF16, kind="ExternalOutput").ap()
                    def f_tdump(sync):
                        for s_ in [self.pe1, self.pe2, self.dv1, self.dv2, self.ac1,
                                   self.sy, self.gp, self.cc,
                                   self.gpA, self.gpB, self.syA, self.syB, self.syC, self.syD]:
                            if s_.n > 0:
                                sync.wait_ge(s_.h, s_.n)
                        self.st.inc(sync.dma_start(out=dW1[:], in_=w1t_dram[:]), 16)
                        self.st.inc(sync.dma_start(out=dW2[:], in_=w2t_dram[:]), 16)
                        sync.wait_ge(self.st.h, self.st.n)
                    block.sync(f_tdump)
                    self.finish(out, ACC, XN, dbg); return nc
                self.ln_phase(Xv, XNv, fold=False)
                self.barrier()
                if self.dump == "p5":
                    self.finish(out, ACC, XN, dbg); return nc
                self.l1_attn(Xv, XNv, dram, bq, bo, cc_in, cc_out)
                self.barrier()
                if self.dump == "p67":
                    self.finish(out, ACC, XN, dbg); return nc
                self.l1_ffn(Xv, XNv, dram, b1, b2, w1t_dram, w2t_dram, out)
            finally:
                blk_cm.__exit__(None, None, None)
        return nc

    def finish(self, out, ACC, XN, dbg, tern=None):
        st = self.st
        sems = [self.pe1, self.pe2, self.dv1, self.dv2, self.ac1, self.sy, self.gp, self.cc,
                self.gpA, self.gpB, self.syA, self.syB, self.syC, self.syD,
                self.gpC, self.gpD]

        def f(sync):
            for s_ in sems:
                if s_.n > 0:
                    sync.wait_ge(s_.h, s_.n)
            for j in range(4):
                st.inc(sync.dma_start(out=out[:, 4096 * j:4096 * (j + 1)],
                                      in_=ACC[:, 4096 * j:4096 * (j + 1)]), 16)
            if dbg:
                st.inc(sync.dma_start(out=dbg["dXN"][:], in_=XN[:]), 16)
            sync.wait_ge(st.h, st.n)
        self.block.sync(f)
        if tern is not None and dbg:
            pass

    def ln_phase(self, Xv, XNv, fold):
        block = self.block
        pe1, pe2, dv1, ac1 = self.pe1, self.pe2, self.dv1, self.ac1
        ones, onesr = self._ones, self._onesr
        sfx = "_f" if fold else "_n"
        if fold:
            self._r_var = self.sb("r_var_keep", [1, T], stack=self._keep_stack)
            self._r_rstd = self.sb("r_rstd_keep", [1, T], stack=self._keep_stack)
        with ExitStack() as ph:
            r_mean = self.sb("r_mean" + sfx, [1, T], stack=ph)
            r_var = self._r_var if fold else self.sb("r_var" + sfx, [1, T], stack=ph)
            r_rstd = self._r_rstd if fold else self.sb("r_rstd" + sfx, [1, T], stack=ph)
            r_a = self.sb("r_a" + sfx, [1, T], stack=ph)
            r_tmp = self.sb("r_tmp" + sfx, [1, T], stack=ph)
            SQ = self.sb("SQ" + sfx, [128, 2 * 512], stack=ph)
            N1A = self.sb("N1A" + sfx, [128, 8 * 512], stack=ph)
            pb_sum = [self.psum("pb_sum%s%d" % (sfx, k), [1, 512], stack=ph) for k in range(2)]
            pb_sq = [self.psum("pb_sq%s%d" % (sfx, k), [1, 512], stack=ph) for k in range(2)]
            pb_r = [self.psum("pb_r%s%d" % (sfx, k), [128, 512], stack=ph) for k in range(2)]
            pb_a = [self.psum("pb_a%s%d" % (sfx, k), [128, 512], stack=ph) for k in range(2)]

            rows_done = {}; norm_done = {}; stats_done = {}
            for tt in range(4):
                sl = slice(512 * tt, 512 * (tt + 1))
                par = tt % 2
                ac_b = ac1.n; pe_b = pe1.n

                def f_act(sc, sl=sl, ac_b=ac_b, pe_b=pe_b, tt=tt):
                    for i in range(NCH):
                        if tt > 0 and i < 2:
                            sc.wait_ge(pe1.h, pe_b)
                        if i >= 2:
                            sc.wait_ge(pe1.h, pe_b + i - 1)
                        ac1.inc(sc.activation(out=SQ[:, 512 * (i % 2):512 * (i % 2 + 1)],
                                              in_=Xv[i][:, sl], func=AF.Square))
                block.scalar(f_act)

                def f_pe(t, sl=sl, par=par, ac_b=ac_b, tt=tt):
                    if tt >= 2:
                        t.wait_ge(dv1.h, rows_done[tt - 2])
                    for i in range(NCH):
                        t.matmul(pb_sum[par][:], ones[:], Xv[i][:, sl],
                                 start=(i == 0), stop=(i == NCH - 1))
                        t.wait_ge(ac1.h, ac_b + i + 1)
                        pe1.inc(t.matmul(pb_sq[par][:], ones[:],
                                         SQ[:, 512 * (i % 2):512 * (i % 2 + 1)],
                                         start=(i == 0), stop=(i == NCH - 1)))
                block.tensor(f_pe)
                stats_done[tt] = pe1.n

                def f_rows(v, sl=sl, par=par, pe_s=stats_done[tt]):
                    v.wait_ge(pe1.h, pe_s)
                    self.V(v, v.tensor_scalar(out=r_mean[:, sl], in0=pb_sum[par][:],
                                              scalar1=1.0 / 1024, scalar2=None, op0=ALU.mult))
                    self.V(v, v.tensor_scalar(out=r_var[:, sl], in0=pb_sq[par][:],
                                              scalar1=1.0 / 1024, scalar2=None, op0=ALU.mult))
                    self.V(v, v.tensor_tensor(out=r_tmp[:, sl], in0=r_mean[:, sl],
                                              in1=r_mean[:, sl], op=ALU.mult))
                    self.V(v, v.tensor_tensor(out=r_var[:, sl], in0=r_var[:, sl],
                                              in1=r_tmp[:, sl], op=ALU.subtract))
                    self.V(v, v.tensor_scalar(out=r_tmp[:, sl], in0=r_var[:, sl],
                                              scalar1=EPS0, scalar2=None, op0=ALU.add))
                block.vector(f_rows)
                rows_done[tt] = dv1.n

                def f_sqrt(sc, sl=sl, dv_r1=rows_done[tt]):
                    sc.wait_ge(dv1.h, dv_r1)
                    self.V(sc, sc.activation(out=r_tmp[:, sl], in_=r_tmp[:, sl], func=AF.Ln),
                           sem=ac1)
                    ac1.inc(sc.activation(out=r_rstd[:, sl], in_=r_tmp[:, sl], func=AF.Exp,
                                          scale=-0.5))
                block.scalar(f_sqrt)
                ac_r1 = ac1.n

                def f_rows2(v, sl=sl, ac_r1=ac_r1):
                    v.wait_ge(ac1.h, ac_r1)
                    self.V(v, v.scalar_tensor_tensor(out=r_a[:, sl], in0=r_mean[:, sl],
                                                     scalar=-1.0, in1=r_rstd[:, sl],
                                                     op0=ALU.mult, op1=ALU.mult))
                block.vector(f_rows2)
                dv_r2 = dv1.n

                def f_bc(t, sl=sl, par=par, dv_r2=dv_r2, tt=tt):
                    t.wait_ge(dv1.h, dv_r2)
                    if tt >= 2:
                        t.wait_ge(dv1.h, norm_done[tt - 2])
                    t.matmul(pb_r[par][:], onesr[:], r_rstd[:, sl], start=True, stop=True)
                    pe2.inc(t.matmul(pb_a[par][:], onesr[:], r_a[:, sl], start=True, stop=True))
                block.tensor(f_bc)
                pe2_r = pe2.n

                def f_norm(v, sl=sl, par=par, pe2_r=pe2_r):
                    v.wait_ge(pe2.h, pe2_r)
                    for i in range(NCH):
                        inst = v.tensor_tensor(out=N1A[:, 512 * i:512 * (i + 1)],
                                               in0=Xv[i][:, sl], in1=pb_r[par][:], op=ALU.mult)
                        if i == NCH - 1:
                            dv1.inc(inst); v.wait_ge(dv1.h, dv1.n)
                    for i in range(NCH):
                        inst = v.tensor_tensor(out=XNv[i][:, sl], in0=N1A[:, 512 * i:512 * (i + 1)],
                                               in1=pb_a[par][:], op=ALU.add)
                        if i == NCH - 1:
                            dv1.inc(inst); v.wait_ge(dv1.h, dv1.n)
                    if fold:
                        for i in range(NCH):
                            inst = v.tensor_tensor(out=Xv[i][:, sl], in0=Xv[i][:, sl],
                                                   in1=XNv[i][:, sl], op=ALU.add)
                            if i == NCH - 1:
                                dv1.inc(inst); v.wait_ge(dv1.h, dv1.n)
                block.vector(f_norm)
                norm_done[tt] = dv1.n

    def l0_attn(self, Xv, XNv, dram, lng, lnb):
        block = self.block
        pe1, pe2, dv1, dv2, ac1, gp = self.pe1, self.pe2, self.dv1, self.dv2, self.ac1, self.gp
        onesr = self._onesr
        with ExitStack() as ph:
            X2 = self.sb("X2", [128, NCH * T], BF16, stack=ph)
            X2v = [X2[:, T * i:T * (i + 1)] for i in range(NCH)]
            r_rstd2 = self.sb("r_rstd2", [1, T], stack=ph)
            r_t2 = self.sb("r_t2", [1, T], stack=ph)
            N1 = self.sb("N1b", [128, 512], stack=ph)
            kTt = self.sb("kTt", [128, 2 * 4096], BF16, stack=ph)
            Vt = self.sb("Vt", [128, 32 * 256], BF16, stack=ph)
            RB = self.sb("RB", [128, 4 * 512], stack=ph)
            PT = self.sb("PT", [128, 4 * 512], BF16, stack=ph)

            def f_load(g):
                gp.inc(g.dma_start(out=kTt[:], in_=dram["kT"][:]), 16)
                gp.inc(g.dma_start(out=Vt[:], in_=dram["V"][:]), 16)
            block.gpsimd(f_load)
            gp_w = gp.n

            def f_rows(v):
                self.V(v, v.scalar_tensor_tensor(out=r_t2[:], in0=self._r_rstd[:], scalar=1.0,
                                                 in1=self._r_rstd[:], op0=ALU.mult, op1=ALU.mult))
                self.V(v, v.tensor_tensor(out=r_t2[:], in0=self._r_var[:], in1=r_t2[:], op=ALU.mult))
                self.V(v, v.tensor_scalar(out=r_t2[:], in0=r_t2[:], scalar1=EPS0,
                                          scalar2=None, op0=ALU.add))
            block.vector(f_rows)
            dv_r = dv1.n

            def f_sq(sc, dv_r=dv_r):
                sc.wait_ge(dv1.h, dv_r)
                self.V(sc, sc.activation(out=r_t2[:], in_=r_t2[:], func=AF.Ln), sem=ac1)
                ac1.inc(sc.activation(out=r_rstd2[:], in_=r_t2[:], func=AF.Exp, scale=-0.5))
            block.scalar(f_sq)
            ac_r = ac1.n
            dvn0 = dv1.n

            x2scope = ExitStack()
            pb_r = self.psum("pb_r2", [128, 512], stack=x2scope)
            for tt in range(4):
                sl = slice(512 * tt, 512 * (tt + 1))
                dvn = dv1.n
                def f_bc(t, sl=sl, dvn=dvn, ac_r=ac_r):
                    t.wait_ge(dv1.h, dvn)
                    t.wait_ge(ac1.h, ac_r)
                    pe2.inc(t.matmul(pb_r[:], onesr[:], r_rstd2[:, sl], start=True, stop=True))
                block.tensor(f_bc)
                pe2_r = pe2.n
                def f_norm(v, sl=sl, pe2_r=pe2_r):
                    v.wait_ge(pe2.h, pe2_r)
                    for i in range(NCH):
                        self.V(v, v.tensor_tensor(out=N1[:], in0=XNv[i][:, sl], in1=pb_r[:], op=ALU.mult))
                        self.V(v, v.tensor_scalar(out=X2v[i][:, sl], in0=N1[:],
                                                  scalar1=lng[:, i:i + 1], scalar2=lnb[:, i:i + 1],
                                                  op0=ALU.mult, op1=ALU.add))
                block.vector(f_norm)

            x2scope.close()
            ps_s = [self.psum("ps_s%d" % k, [128, 512], stack=ph) for k in range(4)]
            ps_o = [self.psum("ps_o%d" % k, [128, 512], stack=ph) for k in range(4)]

            dv2_0 = dv2.n
            it = 0
            for h in range(4):
                for tt in range(4):
                    sl = slice(512 * tt, 512 * (tt + 1))
                    oset = (it % 2) * 2
                    po = (ps_o[oset], ps_o[oset + 1])
                    pe_b = pe1.n; pe2_b = pe2.n; dv_b = dv1.n; ac_b = ac1.n

                    def f_pe(t, h=h, sl=sl, po=po, dv_b=dv_b, ac_b=ac_b, it=it):
                        if it == 0:
                            t.wait_ge(gp.h, gp_w)
                            t.wait_ge(dv1.h, dv_b)
                        elif it >= 2:
                            t.wait_ge(dv2.h, dv2_0 + it - 1)
                        for s in range(32):
                            if s >= 4:
                                t.wait_ge(dv1.h, dv_b + s - 3)
                            t.matmul(ps_s[s % 4][:], kTt[:, 128 * s:128 * (s + 1)],
                                     X2v[2 * h][:, sl], start=True, stop=False)
                            pe1.inc(t.matmul(ps_s[s % 4][:],
                                             kTt[:, 4096 + 128 * s:4096 + 128 * (s + 1)],
                                             X2v[2 * h + 1][:, sl], start=False, stop=True))
                            if s >= 3:
                                sp = s - 3
                                t.wait_ge(ac1.h, ac_b + sp + 1)
                                t.matmul(po[0][:], Vt[:, 256 * sp:256 * sp + 128],
                                         PT[:, 512 * (sp % 4):512 * (sp % 4 + 1)],
                                         start=(sp == 0), stop=False)
                                pe2.inc(t.matmul(po[1][:], Vt[:, 256 * sp + 128:256 * (sp + 1)],
                                                 PT[:, 512 * (sp % 4):512 * (sp % 4 + 1)],
                                                 start=(sp == 0), stop=False))
                        for sp in (29, 30, 31):
                            t.wait_ge(ac1.h, ac_b + sp + 1)
                            t.matmul(po[0][:], Vt[:, 256 * sp:256 * sp + 128],
                                     PT[:, 512 * (sp % 4):512 * (sp % 4 + 1)],
                                     start=False, stop=(sp == 31))
                            pe2.inc(t.matmul(po[1][:], Vt[:, 256 * sp + 128:256 * (sp + 1)],
                                             PT[:, 512 * (sp % 4):512 * (sp % 4 + 1)],
                                             start=False, stop=(sp == 31)))
                    block.tensor(f_pe)

                    def f_dv(v, pe_b=pe_b, ac_b=ac_b):
                        for s in range(32):
                            v.wait_ge(pe1.h, pe_b + s + 1)
                            if s >= 4:
                                v.wait_ge(ac1.h, ac_b + s - 3)
                            dv1.inc(v.tensor_scalar(out=RB[:, 512 * (s % 4):512 * (s % 4 + 1)],
                                                    in0=ps_s[s % 4][:], scalar1=0.0,
                                                    scalar2=None, op0=ALU.max))
                    block.vector(f_dv)

                    def f_ac(sc, dv_b=dv_b, pe2_b=pe2_b):
                        for s in range(32):
                            sc.wait_ge(dv1.h, dv_b + s + 1)
                            if s >= 4:
                                sc.wait_ge(pe2.h, pe2_b + s - 3)
                            ac1.inc(sc.activation(out=PT[:, 512 * (s % 4):512 * (s % 4 + 1)],
                                                  in_=RB[:, 512 * (s % 4):512 * (s % 4 + 1)],
                                                  func=AF.Square))
                    block.scalar(f_ac)

                    def f_evac(v, h=h, sl=sl, po=po, pe2_b=pe2_b):
                        v.wait_ge(pe2.h, pe2_b + 32)
                        v.tensor_tensor(out=Xv[2 * h][:, sl], in0=Xv[2 * h][:, sl],
                                        in1=po[0][:], op=ALU.add)
                        dv2.inc(v.tensor_tensor(out=Xv[2 * h + 1][:, sl], in0=Xv[2 * h + 1][:, sl],
                                                in1=po[1][:], op=ALU.add))
                    block.vector(f_evac)
                    it += 1

    def moe(self, Xv, XNv, dram):
        block = self.block
        pe1, pe2, dv1, dv2, ac1, gp = self.pe1, self.pe2, self.dv1, self.dv2, self.ac1, self.gp
        onesr = self._onesr
        with ExitStack() as ph:
            W1At = self.sb("W1At", [128, 8 * 512], BF16, stack=ph)
            W1Bt = [self.sb("W1Bt%d" % k, [64, 4096], BF16, stack=ph) for k in range(2)]
            W2Bt = self.sb("W2Bt", [128, 4 * 1024], BF16, stack=ph)
            W2At = [self.sb("W2At%d" % k, [128, 32 * 64], BF16, stack=ph) for k in range(2)]
            Zt = self.sb("Zt", [64, 8 * 1024], BF16, stack=ph)
            Ht = self.sb("Ht", [128, 4 * 512], BF16, stack=ph)
            Gt = self.sb("Gt", [128, 4 * 1024], BF16, stack=ph)
            GS = self.sb("GS", [64, 1024], stack=ph)
            ewt = self.sb("ewt", [1, 2048], stack=ph)
            eit = self.sb("eit", [1, 2048], stack=ph)
            cfm = self.sb("cfm", [1, 2048], stack=ph)
            cfr = self.sb("cfr", [1, 1024], stack=ph)

            gpA, gpB = self.gpA, self.gpB

            def f_load(g):
                gp.inc(g.dma_start(out=W1At[:], in_=dram["W1A"][:]), 16)
                gp.inc(g.dma_start(out=W2Bt[:], in_=dram["W2B"][:]), 16)
                gp.inc(g.dma_start(out=ewt[:], in_=dram["ew"][:]), 16)
                gp.inc(g.dma_start(out=eit[:], in_=dram["ei"][:]), 16)
                gpA.inc(g.dma_start(out=W1Bt[0][:], in_=dram["W1B"][:, 0:4096]), 16)
                gpA.inc(g.dma_start(out=W2At[0][:], in_=dram["W2A"][:, 0:2048]), 16)
            block.gpsimd(f_load)
            gp_b = gp.n
            gpw_ready = {0: (gpA, gpA.n)}

            with ExitStack() as zp:
                pz = [self.psum("pz%d" % k, [128, 512], stack=zp) for k in range(2)]
                dvz0 = dv2.n
                for n, (rc, th) in enumerate([(rc, th) for rc in range(4) for th in range(2)]):
                    buf = n % 2
                    def f_pe(t, rc=rc, th=th, buf=buf, n=n):
                        if n == 0:
                            t.wait_ge(gp.h, gp_b)
                        elif n >= 2:
                            t.wait_ge(dv2.h, dvz0 + n - 1)
                        for i in range(NCH):
                            mm = t.matmul(pz[buf][:],
                                          W1At[:, 512 * i + 128 * rc:512 * i + 128 * (rc + 1)],
                                          XNv[i][:, 512 * th:512 * (th + 1)],
                                          start=(i == 0), stop=(i == NCH - 1))
                        pe1.inc(mm)
                    block.tensor(f_pe)
                    pe_e = pe1.n
                    def f_ev(v, rc=rc, th=th, buf=buf, pe_e=pe_e):
                        v.wait_ge(pe1.h, pe_e)
                        c0 = 1024 * (2 * rc) + 512 * th
                        c1 = 1024 * (2 * rc + 1) + 512 * th
                        v.tensor_copy(out=Zt[:, c0:c0 + 512], in_=pz[buf][0:64, :])
                        dv2.inc(v.tensor_copy(out=Zt[:, c1:c1 + 512], in_=pz[buf][64:128, :]))
                    block.vector(f_ev)

            with ExitStack() as ep:
                ph_ = [self.psum("ph%d" % k, [128, 512], stack=ep) for k in range(4)]
                pg = [self.psum("pg%d" % k, [64, 512], stack=ep) for k in range(2)]
                pc = [self.psum("pc%d" % k, [64, 512], stack=ep) for k in range(2)]
                for e in range(8):
                    if e < 7:
                        dv2_s = dv2.n
                        par = gpB if (e + 1) % 2 else gpA
                        def f_pre(g, e=e, dv2_s=dv2_s, par=par):
                            g.wait_ge(dv2.h, dv2_s)
                            par.inc(g.dma_start(out=W1Bt[(e + 1) % 2][:],
                                                in_=dram["W1B"][:, 4096 * (e + 1):4096 * (e + 2)]), 16)
                            par.inc(g.dma_start(out=W2At[(e + 1) % 2][:],
                                                in_=dram["W2A"][:, 2048 * (e + 1):2048 * (e + 2)]), 16)
                        block.gpsimd(f_pre)
                        gpw_ready[e + 1] = (par, par.n)

                    def f_coef(v, e=e):
                        self.V(v, v.tensor_scalar(out=cfm[:], in0=eit[:], scalar1=float(e),
                                                  scalar2=None, op0=ALU.is_equal))
                        self.V(v, v.tensor_tensor(out=cfm[:], in0=cfm[:], in1=ewt[:], op=ALU.mult))
                        cv = cfm.rearrange("a (t k) -> a t k", k=2)
                        cfrv = cfr.rearrange("a (t k) -> a t k", k=1)
                        self.V(v, v.tensor_tensor(out=cfrv[:, :, :], in0=cv[:, :, 0:1],
                                                  in1=cv[:, :, 1:2], op=ALU.add))
                    block.vector(f_coef)
                    dv_cf = dv1.n

                    gpw_sem, gpw_n = gpw_ready[e]
                    dv2_e = dv2.n
                    for th in range(2):
                        pe_b = pe1.n; ac_b = ac1.n; pe2_s = pe2.n

                        def f_pe(t, e=e, th=th, ac_b=ac_b, gpw_sem=gpw_sem, gpw_n=gpw_n, dv2_e=dv2_e):
                            if th == 0:
                                t.wait_ge(gpw_sem.h, gpw_n)
                                t.wait_ge(dv2.h, dv2_e)

                            def g_mm(f):
                                t.wait_ge(ac1.h, ac_b + f + 1)
                                pe2.inc(t.matmul(pg[th][:],
                                                 W2At[e % 2][:, 64 * f:64 * (f + 1)],
                                                 Ht[:, 512 * (f % 4):512 * (f % 4 + 1)],
                                                 start=(f == 0), stop=(f == 31)))
                            for f in range(32):
                                t.wait_ge(ac1.h, ac_b + max(0, f - 3))
                                pe1.inc(t.matmul(ph_[f % 4][:],
                                                 W1Bt[e % 2][:, 128 * f:128 * (f + 1)],
                                                 Zt[:, 1024 * e + 512 * th:1024 * e + 512 * (th + 1)],
                                                 start=True, stop=True))
                                if f >= 3:
                                    g_mm(f - 3)
                            g_mm(29); g_mm(30); g_mm(31)
                        block.tensor(f_pe)

                        def f_gelu(sc, pe_b=pe_b, pe2_s=pe2_s):
                            for f in range(32):
                                sc.wait_ge(pe1.h, pe_b + f + 1)
                                sc.wait_ge(pe2.h, pe2_s + max(0, f - 3))
                                ac1.inc(sc.activation(out=Ht[:, 512 * (f % 4):512 * (f % 4 + 1)],
                                                      in_=ph_[f % 4][:], func=AF.Gelu))
                        block.scalar(f_gelu)

                    pe2_gdone = pe2.n
                    def f_bc(t, dv_cf=dv_cf, dv2_e=dv2_e):
                        t.wait_ge(dv1.h, dv_cf)
                        t.wait_ge(dv2.h, dv2_e)
                        for th in range(2):
                            pe1.inc(t.matmul(pc[th][:], onesr[:, 0:64],
                                             cfr[:, 512 * th:512 * (th + 1)],
                                             start=True, stop=True))
                    block.tensor(f_bc)
                    pe_bc = pe1.n

                    def f_gs(sc, pe2_gdone=pe2_gdone):
                        sc.wait_ge(pe2.h, pe2_gdone)
                        for th in range(2):
                            ac1.inc(sc.activation(out=GS[:, 512 * th:512 * (th + 1)],
                                                  in_=pg[th][:], func=AF.Copy))
                    block.scalar(f_gs)
                    ac_gs = ac1.n

                    def f_g(v, e=e, ac_gs=ac_gs, pe_bc=pe_bc):
                        v.wait_ge(ac1.h, ac_gs)
                        v.wait_ge(pe1.h, pe_bc)
                        c = e // 2
                        r0 = 64 * (e % 2)
                        for th in range(2):
                            dv2.inc(v.tensor_tensor(
                                out=Gt[r0:r0 + 64, 1024 * c + 512 * th:1024 * c + 512 * (th + 1)],
                                in0=GS[:, 512 * th:512 * (th + 1)], in1=pc[th][:], op=ALU.mult))
                    block.vector(f_g)

            with ExitStack() as yp:
                py = [self.psum("py%d" % k, [128, 512], stack=yp) for k in range(2)]
                dvg = dv2.n
                for n, (i, th) in enumerate([(i, th) for i in range(NCH) for th in range(2)]):
                    buf = n % 2
                    def f_pe(t, i=i, th=th, buf=buf, n=n, dvg=dvg):
                        if n == 0:
                            t.wait_ge(dv2.h, dvg)
                        elif n >= 2:
                            t.wait_ge(dv2.h, dvg + n - 1)
                        for rc in range(4):
                            mm = t.matmul(py[buf][:],
                                          W2Bt[:, 1024 * rc + 128 * i:1024 * rc + 128 * (i + 1)],
                                          Gt[:, 1024 * rc + 512 * th:1024 * rc + 512 * (th + 1)],
                                          start=(rc == 0), stop=(rc == 3))
                        pe1.inc(mm)
                    block.tensor(f_pe)
                    pe_y = pe1.n
                    def f_ev(v, i=i, th=th, buf=buf, pe_y=pe_y):
                        v.wait_ge(pe1.h, pe_y)
                        dv2.inc(v.tensor_tensor(out=Xv[i][:, 512 * th:512 * (th + 1)],
                                                in0=Xv[i][:, 512 * th:512 * (th + 1)],
                                                in1=py[buf][:], op=ALU.add))
                    block.vector(f_ev)

    def ternarize(self, dram, w1t_dram, w2t_dram):
        block = self.block
        pe1, dv1, ac1, sy, gp = self.pe1, self.dv1, self.ac1, self.sy, self.gp
        ones, onesr = self._ones, self._onesr
        with ExitStack() as ph:
            WS = [self.sb("WS%d" % k, [128, 4096], stack=ph) for k in range(4)]
            T1 = self.sb("T1", [128, 4096], stack=ph)
            TB = [self.sb("TB%d" % k, [128, 4096], BF16, stack=ph) for k in range(2)]
            parts = self.sb("parts", [128, 8], stack=ph)
            accp = self.sb("accp", [128, 1], stack=ph)
            sc1 = self.sb("sc1", [1, 1], stack=ph)
            rS = self.sb("rS", [128, 1], stack=ph)
            p1 = self.psum("pt1", [1, 512], stack=ph)
            p2 = self.psum("pt2", [128, 512], stack=ph)

            syp = [self.syA, self.syB, self.syC, self.syD]
            gpp = [self.gpA, self.gpB]
            for (src, dst) in ((dram["W1f"], w1t_dram), (dram["W2f"], w2t_dram)):
                nch = 8
                dv_b = dv1.n
                ld_n = {}
                def f_ld(s, src=src, dv_b=dv_b, ld_n=ld_n):
                    for c in range(nch):
                        if c >= 4:
                            s.wait_ge(dv1.h, dv_b + c - 3)
                        syp[c % 4].inc(s.dma_start(out=WS[c % 4][:],
                                                   in_=src[:, 4096 * c:4096 * (c + 1)]), 16)
                        ld_n[c] = syp[c % 4].n
                block.sync(f_ld)

                def f_red(v, ld_n=ld_n):
                    for c in range(nch):
                        v.wait_ge(syp[c % 4].h, ld_n[c])
                        dv1.inc(v.tensor_reduce(out=parts[:, c:c + 1], in_=WS[c % 4][:],
                                                axis=AXX, op=ALU.add, apply_absolute_value=True))
                    v.wait_ge(dv1.h, dv1.n)
                    self.V(v, v.tensor_reduce(out=accp[:], in_=parts[:], axis=AXX, op=ALU.add))
                block.vector(f_red)
                dv_red = dv1.n

                def f_pe(t, dv_red=dv_red):
                    t.wait_ge(dv1.h, dv_red)
                    pe1.inc(t.matmul(p1[:, 0:1], accp[:], ones[:], start=True, stop=True))
                block.tensor(f_pe)
                pe_s = pe1.n

                def f_sc(v, pe_s=pe_s):
                    v.wait_ge(pe1.h, pe_s)
                    self.V(v, v.tensor_scalar(out=sc1[:], in0=p1[0:1, 0:1], scalar1=1.0 / (nch * 4096 * 128),
                                              scalar2=1e-8, op0=ALU.mult, op1=ALU.max))
                    self.V(v, v.reciprocal(out=sc1[:], in_=sc1[:]))
                block.vector(f_sc)
                dv_sc = dv1.n

                def f_bc(t, dv_sc=dv_sc):
                    t.wait_ge(dv1.h, dv_sc)
                    pe1.inc(t.matmul(p2[:, 0:1], onesr[:], sc1[:], start=True, stop=True))
                block.tensor(f_bc)
                pe_s2 = pe1.n

                def f_cp(v, pe_s2=pe_s2):
                    v.wait_ge(pe1.h, pe_s2)
                    self.V(v, v.tensor_copy(out=rS[:], in_=p2[:, 0:1]))
                block.vector(f_cp)
                dv_cp = dv1.n

                ac_b = ac1.n; dv_b5 = dv1.n
                ld2_n = {}
                def f_ld2(s, src=src, ac_b=ac_b, dv_cp=dv_cp, ld2_n=ld2_n):
                    for c in range(nch):
                        if c < 4:
                            s.wait_ge(dv1.h, dv_cp)
                        else:
                            s.wait_ge(ac1.h, ac_b + c - 3)
                        syp[c % 4].inc(s.dma_start(out=WS[c % 4][:],
                                                   in_=src[:, 4096 * c:4096 * (c + 1)]), 16)
                        ld2_n[c] = syp[c % 4].n
                block.sync(f_ld2)

                def f_act(sc, dv_cp=dv_cp, dv_b5=dv_b5, ld2_n=ld2_n):
                    sc.wait_ge(dv1.h, dv_cp)
                    for c in range(nch):
                        sc.wait_ge(syp[c % 4].h, ld2_n[c])
                        if c >= 1:
                            sc.wait_ge(dv1.h, dv_b5 + 2 * c)
                        ac1.inc(sc.activation(out=T1[:], in_=WS[c % 4][:], func=AF.Copy,
                                              scale=rS[:, 0:1], bias=MAGIC))
                block.scalar(f_act)

                st_n = {}
                gpp_base = (gpp[0].n, gpp[1].n)
                def f_tern(v, ac_b=ac_b, gpp_base=gpp_base):
                    for c in range(nch):
                        v.wait_ge(ac1.h, ac_b + c + 1)
                        if c >= 2:
                            v.wait_ge(gpp[c % 2].h, gpp_base[c % 2] + 16 * (c // 2))
                        self.V(v, v.tensor_scalar(out=T1[:], in0=T1[:], scalar1=MAGIC, scalar2=-1.0,
                                                  op0=ALU.subtract, op1=ALU.max))
                        self.V(v, v.tensor_scalar(out=TB[c % 2][:], in0=T1[:], scalar1=1.0,
                                                  scalar2=None, op0=ALU.min))
                block.vector(f_tern)

                def f_st(g, dst=dst, dv_b5=dv_b5):
                    for c in range(nch):
                        g.wait_ge(dv1.h, dv_b5 + 2 * (c + 1))
                        gpp[c % 2].inc(g.dma_start(out=dst[:, 4096 * c:4096 * (c + 1)],
                                                   in_=TB[c % 2][:]), 16)
                block.gpsimd(f_st)
                self.barrier()

    def l1_attn(self, Xv, XNv, dram, bq, bo, cc_in, cc_out):
        block = self.block
        pe1, pe2, dv1, dv2, ac1, gp, cc, sy = (self.pe1, self.pe2, self.dv1, self.dv2,
                                               self.ac1, self.gp, self.cc, self.sy)
        onesr, bcp2 = self._onesr, self._bcp2
        with ExitStack() as ph:
            wkB = self.sb("wkB", [128, 8 * 1024], BF16, stack=ph)
            wvB = self.sb("wvB", [128, 8 * 1024], BF16, stack=ph)
            bkB = self.sb("bkB", [128, 1024], stack=ph)
            bvB = self.sb("bvB", [128, 1024], stack=ph)
            bkr = self.sb("bkr_t", [1, 1024], stack=ph)
            bvr = self.sb("bvr_t", [1, 1024], stack=ph)
            KS1 = self.sb("KS1", [128, 2 * 512], stack=ph)
            KS2 = self.sb("KS2", [128, 2 * 512], stack=ph)
            KS3 = self.sb("KS3", [128, 2 * 512], stack=ph)
            KVbuf = self.sb("KVbuf", [64, 16 * 130], stack=ph)
            KFr = self.sb("KFr", [128, 3 * 1024], BF16, stack=ph)
            VPr = self.sb("VPr", [128, 3 * 16 * 65], BF16, stack=ph)
            vpv = VPr.rearrange("p (r h d) -> p r h d", r=3, h=16)

            def f_load(g):
                gp.inc(g.dma_start(out=wkB[:], in_=dram["wkT"][:]), 16)
                gp.inc(g.dma_start(out=wvB[:], in_=dram["wvT"][:]), 16)
            block.gpsimd(f_load)
            gp_w = gp.n

            def f_load2(s):
                sy.inc(s.dma_start(out=bkr[:], in_=dram["bkr"][:]), 16)
                sy.inc(s.dma_start(out=bvr[:], in_=dram["bvr"][:]), 16)
            block.sync(f_load2)
            sy_w = sy.n

            with ExitStack() as pp:
                pbb = self.psum("pbb", [128, 512], stack=pp)
                pe_b = pe1.n; dv_b = dv1.n

                def f_pebc(t, sy_w=sy_w, dv_b=dv_b):
                    t.wait_ge(sy.h, sy_w)
                    for j in range(4):
                        src = bkr if j < 2 else bvr
                        if j >= 1:
                            t.wait_ge(dv1.h, dv_b + j)
                        pe1.inc(t.matmul(pbb[:], onesr[:], src[:, 512 * (j % 2):512 * (j % 2 + 1)],
                                         start=True, stop=True))
                block.tensor(f_pebc)

                def f_dvbc(v, pe_b=pe_b):
                    for j in range(4):
                        dstt = bkB if j < 2 else bvB
                        v.wait_ge(pe1.h, pe_b + j + 1)
                        dv1.inc(v.tensor_copy(out=dstt[:, 512 * (j % 2):512 * (j % 2 + 1)],
                                              in_=pbb[:]))
                block.vector(f_dvbc)

                def f_ones(v):
                    dv1.inc(v.memset(vpv[:, :, :, 64:65], 1.0))
                block.vector(f_ones)

            with ExitStack() as kp:
                pk = [self.psum("pk%d" % k, [128, 512], stack=kp) for k in range(2)]
                kvp = [self.psum("kvp%d" % k, [64, 390 if k < 5 else 130], stack=kp)
                       for k in range(6)]
                pe_mom = {}
                ks1_done = {}
                vev_done = {}
                chain_done = {}

                def emit_moments(tc):
                    r = tc % 3
                    dv_tc = chain_done[tc]
                    def f_mom(t, tc=tc, r=r, dv_tc=dv_tc):
                        t.wait_ge(dv1.h, dv_tc)
                        slot = tc // 8
                        for h in range(16):
                            bank, off = h // 3, 130 * (h % 3)
                            mm = t.matmul(kvp[bank][:, off + 65 * slot:off + 65 * (slot + 1)],
                                          KFr[:, 1024 * r + 64 * h:1024 * r + 64 * (h + 1)],
                                          VPr[:, 1040 * r + 65 * h:1040 * r + 65 * (h + 1)],
                                          start=(tc == 0 and h % 3 == 0),
                                          stop=(tc == 15 and h % 3 == (2 if bank < 5 else 0)),
                                          skip_group_check=True)
                            if h == 15:
                                pe1.inc(mm)
                    block.tensor(f_mom)
                    pe_mom[tc] = pe1.n

                for tc in range(16):
                    r = tc % 3
                    pe_war = pe_mom.get(tc - 3, 0)
                    for jh in range(2):
                        def f_pe(t, tc=tc, jh=jh, first=(tc == 0 and jh == 0)):
                            if first:
                                t.wait_ge(gp.h, gp_w)
                                t.wait_ge(dv1.h, dv1.n)
                            prev = ks1_done.get((tc, jh - 1), ks1_done.get((tc - 1, 1), 0))
                            if prev:
                                t.wait_ge(dv1.h, prev)
                            for i in range(NCH):
                                mm = t.matmul(pk[0][:],
                                              XNv[i][:, 128 * tc:128 * (tc + 1)],
                                              wkB[:, 1024 * i + 512 * jh:1024 * i + 512 * (jh + 1)],
                                              start=(i == 0), stop=(i == NCH - 1))
                            pe1.inc(mm)
                        block.tensor(f_pe)
                        pe_kv = pe1.n
                        def f_ev(v, jh=jh, pe_kv=pe_kv):
                            v.wait_ge(pe1.h, pe_kv)
                            self.V(v, v.tensor_tensor(out=KS1[:, 512 * jh:512 * (jh + 1)],
                                                      in0=pk[0][:],
                                                      in1=bkB[:, 512 * jh:512 * (jh + 1)], op=ALU.add))
                            self.V(v, v.tensor_scalar(out=KS2[:, 512 * jh:512 * (jh + 1)],
                                                      in0=KS1[:, 512 * jh:512 * (jh + 1)],
                                                      scalar1=0.0, scalar2=None, op0=ALU.min))
                        block.vector(f_ev)
                        ks1_done[(tc, jh)] = dv1.n - 1
                        dv_k1 = dv1.n
                        def f_exp(sc, jh=jh, dv_k1=dv_k1):
                            sc.wait_ge(dv1.h, dv_k1)
                            ac1.inc(sc.activation(out=KS3[:, 512 * jh:512 * (jh + 1)],
                                                  in_=KS2[:, 512 * jh:512 * (jh + 1)], func=AF.Exp))
                        block.scalar(f_exp)
                        ac_k = ac1.n
                        def f_ev2(v, r=r, jh=jh, ac_k=ac_k, pe_war=pe_war):
                            v.wait_ge(ac1.h, ac_k)
                            if pe_war:
                                v.wait_ge(pe1.h, pe_war)
                            self.V(v, v.tensor_scalar(out=KS2[:, 512 * jh:512 * (jh + 1)],
                                                      in0=KS1[:, 512 * jh:512 * (jh + 1)],
                                                      scalar1=0.0, scalar2=None, op0=ALU.max))
                            c0 = 1024 * r + 512 * jh
                            self.V(v, v.tensor_tensor(out=KFr[:, c0:c0 + 512],
                                                      in0=KS3[:, 512 * jh:512 * (jh + 1)],
                                                      in1=KS2[:, 512 * jh:512 * (jh + 1)], op=ALU.add))
                        block.vector(f_ev2)
                    for jh in range(2):
                        def f_pe(t, tc=tc, jh=jh):
                            prev = vev_done.get((tc, jh - 1), vev_done.get((tc - 1, 1), 0))
                            if prev:
                                t.wait_ge(dv1.h, prev)
                            for i in range(NCH):
                                mm = t.matmul(pk[1][:],
                                              XNv[i][:, 128 * tc:128 * (tc + 1)],
                                              wvB[:, 1024 * i + 512 * jh:1024 * i + 512 * (jh + 1)],
                                              start=(i == 0), stop=(i == NCH - 1))
                            pe1.inc(mm)
                        block.tensor(f_pe)
                        pe_kv = pe1.n
                        def f_ev(v, r=r, jh=jh, pe_kv=pe_kv, pe_war=pe_war):
                            v.wait_ge(pe1.h, max(pe_kv, pe_war))
                            pv = pk[1][:].rearrange("p (one h d) -> p one h d", one=1, h=8)
                            bv = bvB[:, 512 * jh:512 * (jh + 1)].rearrange(
                                "p (one h d) -> p one h d", one=1, h=8)
                            dv1.inc(v.tensor_tensor(
                                out=vpv[:, r:r + 1, 8 * jh:8 * (jh + 1), 0:64],
                                in0=pv, in1=bv, op=ALU.add))
                        block.vector(f_ev)
                        vev_done[(tc, jh)] = dv1.n
                    chain_done[tc] = dv1.n
                    if tc >= 1:
                        emit_moments(tc - 1)
                emit_moments(15)

                pe_all = pe1.n
                def f_evm(v, pe_all=pe_all):
                    v.wait_ge(pe1.h, pe_all)
                    for b in range(6):
                        n_h = 3 if b < 5 else 1
                        dv1.inc(v.tensor_copy(out=KVbuf[:, 390 * b:390 * b + 130 * n_h],
                                              in_=kvp[b][:]))
                block.vector(f_evm)

            dv_kv = dv1.n
            def f_cc(g, dv_kv=dv_kv):
                g.wait_ge(dv1.h, dv_kv)
                gp.inc(g.dma_start(out=cc_in[:], in_=KVbuf[:]), 16)
                g.wait_ge(gp.h, gp.n)
                cc.inc(g.collective_compute("AllReduce", ALU.add,
                                            [[0, 1, 2, 3], [4, 5, 6, 7]],
                                            [cc_in.opt()], [cc_out.opt()]))
            block.gpsimd(f_cc)

        with ExitStack() as ph:
            woB = self.sb("woB", [128, 8 * 1024], BF16, stack=ph)
            QF = self.sb("QF", [128, 8 * 2048], BF16, stack=ph)
            arena = self.sb("arena", [128, 16 * 130], BF16, stack=ph)
            BD = self.sb("BD", [128, 16 * 128], BF16, stack=ph)
            KS = self.sb("KSs", [128, 16 * 2], BF16, stack=ph)
            ar5 = arena.rearrange("p (pair slot half j) -> p pair slot half j",
                                  pair=8, slot=2, half=2)

            def f_load(g):
                gp.inc(g.dma_start(out=woB[:], in_=dram["woT"][:]), 16)
            block.gpsimd(f_load)

            def f_ze(v):
                dv1.inc(v.memset(arena[:], 0.0))
            block.vector(f_ze)
            dv_ze = dv1.n
            cc_n = cc.n

            def f_ar(g, dv_ze=dv_ze, cc_n=cc_n):
                g.wait_ge(cc.h, cc_n)
                g.wait_ge(dv1.h, dv_ze)
                src5 = cc_out.rearrange("d (h2 par slot j) -> d h2 par slot j",
                                        h2=8, par=2, slot=2)
                self.gpB.inc(g.dma_start(out=ar5[0:64, :, :, 0:1, :].opt(),
                                         in_=src5[:, :, 0:1, :, :].opt()), 16)
                self.gpB.inc(g.dma_start(out=ar5[64:128, :, :, 1:2, :].opt(),
                                         in_=src5[:, :, 1:2, :, :].opt()), 16)
            block.gpsimd(f_ar)
            gp_ar = self.gpB.n

            with ExitStack() as qp:
                wqB = self.sb("wqB", [128, 8 * 1024], BF16, stack=qp)
                QS1 = self.sb("QS1", [128, 2 * 512], stack=qp)
                QS2 = self.sb("QS2", [128, 2 * 512], stack=qp)
                QS3 = self.sb("QS3", [128, 2 * 512], stack=qp)
                def f_loadq(g):
                    gp.inc(g.dma_start(out=wqB[:], in_=dram["wqT"][:]), 16)
                block.gpsimd(f_loadq)
                gp_w2 = gp.n
                pq = [self.psum("pq%d" % k, [128, 512], stack=qp) for k in range(2)]
                qs_done = {}; exp_done = {}; qf_done = {}
                for n, (ic, tt) in enumerate([(ic, tt) for ic in range(NCH) for tt in range(4)]):
                    b = n % 2
                    bs = slice(512 * b, 512 * (b + 1))
                    def f_pe(t, ic=ic, tt=tt, b=b, n=n):
                        if n == 0:
                            t.wait_ge(gp.h, gp_w2)
                        elif n >= 2:
                            t.wait_ge(dv1.h, qs_done[n - 2])
                        for i in range(NCH):
                            mm = t.matmul(pq[b][:],
                                          wqB[:, 1024 * i + 128 * ic:1024 * i + 128 * (ic + 1)],
                                          XNv[i][:, 512 * tt:512 * (tt + 1)],
                                          start=(i == 0), stop=(i == NCH - 1))
                        pe1.inc(mm)
                    block.tensor(f_pe)
                    pe_q = pe1.n
                    def f_ev(v, ic=ic, b=b, bs=bs, n=n, pe_q=pe_q):
                        v.wait_ge(pe1.h, pe_q)
                        if n >= 2:
                            v.wait_ge(ac1.h, exp_done[n - 2])
                            v.wait_ge(dv1.h, qf_done[n - 2])
                        v.tensor_scalar(out=QS1[:, bs], in0=pq[b][:], scalar1=bq[:, ic:ic + 1],
                                        scalar2=0.0, op0=ALU.add, op1=ALU.min)
                        dv1.inc(v.tensor_scalar(out=QS2[:, bs], in0=pq[b][:], scalar1=bq[:, ic:ic + 1],
                                                scalar2=0.0, op0=ALU.add, op1=ALU.max))
                    block.vector(f_ev)
                    qs_done[n] = dv1.n
                    def f_exp(sc, b=b, bs=bs, n=n, dv_q=qs_done[n]):
                        sc.wait_ge(dv1.h, dv_q)
                        if n >= 2:
                            sc.wait_ge(dv1.h, qf_done[n - 2])
                        ac1.inc(sc.activation(out=QS3[:, bs], in_=QS1[:, bs], func=AF.Exp))
                    block.scalar(f_exp)
                    exp_done[n] = ac1.n
                    def f_ev2(v, ic=ic, tt=tt, bs=bs, ac_q=exp_done[n]):
                        v.wait_ge(ac1.h, ac_q)
                        c0 = 2048 * ic + 512 * tt
                        dv1.inc(v.tensor_tensor(out=QF[:, c0:c0 + 512],
                                                in0=QS3[:, bs], in1=QS2[:, bs], op=ALU.add))
                    block.vector(f_ev2)
                    qf_done[n] = dv1.n

            ATT = self.sb("ATT", [128, 8 * 2048], BF16, stack=ph)

            dv_bd0 = dv1.n
            def f_bd(v, gp_ar=gp_ar):
                v.wait_ge(self.gpB.h, gp_ar)
                for p in range(8):
                    for slot in range(2):
                        idx = 2 * p + slot
                        src = ar5[:, p:p + 1, slot:slot + 1, :, :]
                        v.tensor_copy(
                            out=BD[:, 128 * idx:128 * (idx + 1)].rearrange(
                                "p (one one2 half j) -> p one one2 half j", one=1, one2=1, half=2),
                            in_=src[:, :, :, :, 0:64])
                        dv1.inc(v.tensor_copy(
                            out=KS[:, 2 * idx:2 * (idx + 1)].rearrange(
                                "p (one one2 half j) -> p one one2 half j", one=1, one2=1, half=2),
                            in_=src[:, :, :, :, 64:65]))
            block.vector(f_bd)

            with ExitStack() as ap:
                patt = [self.psum("patt%d" % k, [128, 512], stack=ap) for k in range(2)]
                pden = [self.psum("pden%d" % k, [2, 512], stack=ap) for k in range(2)]
                pdb = [self.psum("pdb%d" % k, [128, 512], stack=ap) for k in range(2)]
                dn2 = self.sb("dn2", [2, 2 * 512], stack=ap)
                AS2 = self.sb("AS2", [128, 2 * 512], stack=ap)
                att_done = {}
                ln_done = {}
                db_done = {}
                dv1_mx = {}
                for n, (p, slot, tt2) in enumerate([(p, s_, t_) for p in range(8)
                                                    for s_ in range(2) for t_ in range(2)]):
                    idx = 2 * p + slot
                    col = 1024 * slot + 512 * tt2
                    b = n % 2
                    dv_now = dv1.n
                    def f_pe(t, idx=idx, p=p, col=col, n=n, b=b, dv_now=dv_now):
                        if n == 0:
                            t.wait_ge(dv1.h, dv_now)
                        if n >= 2:
                            t.wait_ge(dv2.h, att_done[n - 2])
                        t.matmul(patt[b][:], BD[:, 128 * idx:128 * (idx + 1)],
                                 QF[:, 2048 * p + col:2048 * p + col + 512],
                                 start=True, stop=True)
                        pe1.inc(t.matmul(pden[b][:], KS[:, 2 * idx:2 * (idx + 1)],
                                         QF[:, 2048 * p + col:2048 * p + col + 512],
                                         start=True, stop=True))
                    block.tensor(f_pe)
                    pe_a = pe1.n
                    def f_dn(v, pe_a=pe_a, b=b, n=n):
                        v.wait_ge(pe1.h, pe_a)
                        if n >= 2:
                            v.wait_ge(ac1.h, ln_done[n - 2])
                            v.wait_ge(pe2.h, db_done[n - 2])
                        dv1.inc(v.tensor_scalar(out=dn2[:, 512 * b:512 * (b + 1)],
                                                in0=pden[b][:], scalar1=1e-6,
                                                scalar2=None, op0=ALU.max))
                    block.vector(f_dn)
                    dv1_mx[n] = dv1.n
                    def f_lnx(sc, b=b, n=n, dvw=dv1_mx[n]):
                        sc.wait_ge(dv1.h, dvw)
                        self.V(sc, sc.activation(out=dn2[:, 512 * b:512 * (b + 1)],
                                                 in_=dn2[:, 512 * b:512 * (b + 1)], func=AF.Ln),
                               sem=ac1)
                        ac1.inc(sc.activation(out=dn2[:, 512 * b:512 * (b + 1)],
                                              in_=dn2[:, 512 * b:512 * (b + 1)], func=AF.Exp,
                                              scale=-1.0))
                    block.scalar(f_lnx)
                    ln_done[n] = ac1.n
                    def f_db(t, b=b, acw=ln_done[n]):
                        t.wait_ge(ac1.h, acw)
                        pe2.inc(t.matmul(pdb[b][:], bcp2[:], dn2[:, 512 * b:512 * (b + 1)],
                                         start=True, stop=True))
                    block.tensor(f_db)
                    pe2_db = pe2.n
                    db_done[n] = pe2.n
                    def f_as(sc, pe_a=pe_a, b=b, n=n):
                        sc.wait_ge(pe1.h, pe_a)
                        if n >= 2:
                            sc.wait_ge(dv2.h, att_done[n - 2])
                        ac1.inc(sc.activation(out=AS2[:, 512 * b:512 * (b + 1)],
                                              in_=patt[b][:], func=AF.Copy))
                    block.scalar(f_as)
                    ac_as = ac1.n
                    def f_att(v, p=p, col=col, b=b, ac_as=ac_as, pe2_db=pe2_db):
                        v.wait_ge(ac1.h, ac_as)
                        v.wait_ge(pe2.h, pe2_db)
                        dv2.inc(v.tensor_tensor(out=ATT[:, 2048 * p + col:2048 * p + col + 512],
                                                in0=AS2[:, 512 * b:512 * (b + 1)],
                                                in1=pdb[b][:], op=ALU.mult))
                    block.vector(f_att)
                    att_done[n] = dv2.n

                po = [self.psum("po%d" % k, [128, 512], stack=ap) for k in range(2)]
                dv2_q = dv2.n
                ev_done = {}
                for n, (ic, tt) in enumerate([(ic, tt) for ic in range(NCH) for tt in range(4)]):
                    buf = n % 2
                    def f_pe(t, ic=ic, tt=tt, buf=buf, n=n):
                        if n == 0:
                            t.wait_ge(dv2.h, dv2_q)
                        elif n >= 2:
                            t.wait_ge(dv2.h, ev_done[n - 2])
                        for j in range(NCH):
                            mm = t.matmul(po[buf][:],
                                          woB[:, 1024 * j + 128 * ic:1024 * j + 128 * (ic + 1)],
                                          ATT[:, 2048 * j + 512 * tt:2048 * j + 512 * (tt + 1)],
                                          start=(j == 0), stop=(j == NCH - 1))
                        pe1.inc(mm)
                    block.tensor(f_pe)
                    pe_o = pe1.n
                    def f_ev(v, ic=ic, tt=tt, buf=buf, pe_o=pe_o):
                        v.wait_ge(pe1.h, pe_o)
                        dv2.inc(v.scalar_tensor_tensor(
                            out=Xv[ic][:, 512 * tt:512 * (tt + 1)],
                            in0=po[buf][:], scalar=bo[:, ic:ic + 1],
                            in1=Xv[ic][:, 512 * tt:512 * (tt + 1)],
                            op0=ALU.add, op1=ALU.add))
                    block.vector(f_ev)
                    ev_done[n] = dv2.n

    def l1_ffn(self, Xv, XNv, dram, b1, b2, w1t_dram, w2t_dram, out):
        block = self.block
        pe1, pe2, dv1, dv2, ac1, gp, sy, st = (self.pe1, self.pe2, self.dv1, self.dv2,
                                               self.ac1, self.gp, self.sy, self.st)
        with ExitStack() as ph:
            W1tB = self.sb("W1tB", [128, 8 * 4096], BF16, stack=ph)
            HT = self.sb("HT", [128, 32 * 512], BF16, stack=ph)
            W2tB = [self.sb("W2tB%d" % k, [128, 1024], BF16, stack=ph) for k in range(4)]

            def f_ld(g):
                for j in range(4):
                    gp.inc(g.dma_start(out=W1tB[:, 8192 * j:8192 * (j + 1)],
                                       in_=w1t_dram[:, 8192 * j:8192 * (j + 1)]), 16)
            block.gpsimd(f_ld)
            gp_w1 = gp.n

            for tt in range(4):
                with ExitStack() as tp:
                    phh = [self.psum("phh%d_%d" % (tt, k), [128, 512], stack=tp) for k in range(2)]
                    ac_b = ac1.n; pe_b = pe1.n; dv2_prev = dv2.n
                    def f_pe(t, tt=tt, ac_b=ac_b, dv2_prev=dv2_prev, first=(tt == 0)):
                        if first:
                            t.wait_ge(gp.h, gp_w1)
                        else:
                            t.wait_ge(dv2.h, dv2_prev)
                        for f in range(32):
                            buf = f % 2
                            if f >= 2:
                                t.wait_ge(ac1.h, ac_b + f - 1)
                            for i in range(NCH):
                                mm = t.matmul(phh[buf][:],
                                              W1tB[:, 4096 * i + 128 * f:4096 * i + 128 * (f + 1)],
                                              XNv[i][:, 512 * tt:512 * (tt + 1)],
                                              start=(i == 0), stop=(i == NCH - 1))
                            pe1.inc(mm)
                    block.tensor(f_pe)
                    def f_gelu(sc, pe_b=pe_b):
                        for f in range(32):
                            buf = f % 2
                            sc.wait_ge(pe1.h, pe_b + f + 1)
                            ac1.inc(sc.activation(out=HT[:, 512 * f:512 * (f + 1)],
                                                  in_=phh[buf][:], func=AF.Gelu,
                                                  bias=b1[:, f:f + 1]))
                    block.scalar(f_gelu)
                    ac_done = ac1.n

                with ExitStack() as tp:
                    py8 = [self.psum("py8_%d_%d" % (tt, k), [128, 512], stack=tp) for k in range(8)]
                    gpp = [self.gpA, self.gpB, self.gpC, self.gpD]
                    pe2_b = pe2.n
                    w2_n = {}
                    def f_w2(g, pe2_b=pe2_b, w2_n=w2_n):
                        for f in range(32):
                            if f >= 4:
                                g.wait_ge(pe2.h, pe2_b + f - 3)
                            else:
                                g.wait_ge(pe2.h, pe2_b)
                            gpp[f % 4].inc(g.dma_start(out=W2tB[f % 4][:],
                                                       in_=w2t_dram[:, 1024 * f:1024 * (f + 1)]), 16)
                            w2_n[f] = gpp[f % 4].n
                    block.gpsimd(f_w2)

                    def f_pe2(t, ac_done=ac_done, w2_n=w2_n):
                        t.wait_ge(ac1.h, ac_done)
                        for f in range(32):
                            t.wait_ge(gpp[f % 4].h, w2_n[f])
                            for ic in range(NCH):
                                mm = t.matmul(py8[ic][:], W2tB[f % 4][:, 128 * ic:128 * (ic + 1)],
                                              HT[:, 512 * f:512 * (f + 1)],
                                              start=(f == 0), stop=(f == 31))
                            pe2.inc(mm)
                    block.tensor(f_pe2)
                    pe2_done = pe2.n

                    def f_ev(v, tt=tt, pe2_done=pe2_done):
                        v.wait_ge(pe2.h, pe2_done)
                        for ic in range(NCH):
                            dv2.inc(v.scalar_tensor_tensor(
                                out=Xv[ic][:, 512 * tt:512 * (tt + 1)],
                                in0=py8[ic][:], scalar=b2[:, ic:ic + 1],
                                in1=Xv[ic][:, 512 * tt:512 * (tt + 1)],
                                op0=ALU.add, op1=ALU.add))
                    block.vector(f_ev)
                    dv2_done = dv2.n

                    def f_st(s, tt=tt, dv2_done=dv2_done):
                        s.wait_ge(dv2.h, dv2_done)
                        for ic in range(NCH):
                            st.inc(s.dma_start(out=out[:, T * ic + 512 * tt:T * ic + 512 * (tt + 1)],
                                               in_=Xv[ic][:, 512 * tt:512 * (tt + 1)]), 16)
                    block.sync(f_st)

            def f_fin(s):
                s.wait_ge(st.h, st.n)
            block.sync(f_fin)



def _pack(w):
    R, C = w.shape
    k = R // 128
    return np.ascontiguousarray(w.reshape(k, 128, C).transpose(1, 0, 2).reshape(128, k * C))


def _col_tile(vec):
    k = vec.shape[0] // 128
    return np.ascontiguousarray(vec.reshape(k, 128).T)


def _host_inputs(inputs):
    x = np.ascontiguousarray(inputs["L0_attn_residual"], dtype=np.float32)
    xflat = x.reshape(16384, 1024)
    sh = {}
    sh["kT"] = _pack(np.ascontiguousarray(inputs["L0_attn_w_fc"].reshape(4096, 256).T))
    sh["V"] = _pack(np.ascontiguousarray(inputs["L0_attn_w_proj"].T.reshape(4096, 256)))
    sh["lng"] = _col_tile(inputs["L0_attn_ln_g"])
    sh["lnb"] = _col_tile(inputs["L0_attn_ln_b"])
    sh["W1A"] = _pack(np.ascontiguousarray(inputs["L0_ffn_w1_A"].T))
    sh["W1B"] = np.ascontiguousarray(
        inputs["L0_ffn_w1_B"].reshape(8, 64, 4096).transpose(1, 0, 2).reshape(64, 8 * 4096))
    sh["W2A"] = np.ascontiguousarray(
        inputs["L0_ffn_w2_A"].reshape(8, 64, 32, 128).transpose(3, 0, 2, 1).reshape(128, 8 * 32 * 64))
    sh["W2B"] = _pack(np.ascontiguousarray(inputs["L0_ffn_w2_B"]))
    sh["wqT"] = _pack(np.ascontiguousarray(inputs["L1_attn_wq"].T))
    sh["wkT"] = _pack(np.ascontiguousarray(inputs["L1_attn_wk"].T))
    sh["wvT"] = _pack(np.ascontiguousarray(inputs["L1_attn_wv"].T))
    sh["woT"] = _pack(np.ascontiguousarray(inputs["L1_attn_wo"].T))
    sh["bq"] = _col_tile(inputs["L1_attn_bq"])
    sh["bo"] = _col_tile(inputs["L1_attn_bo"])
    sh["bkr"] = np.ascontiguousarray(inputs["L1_attn_bk"].reshape(1, 1024))
    sh["bvr"] = np.ascontiguousarray(inputs["L1_attn_bv"].reshape(1, 1024))
    sh["W1f"] = _pack(np.ascontiguousarray(inputs["L1_ffn_W1"].T))
    sh["W2f"] = _pack(np.ascontiguousarray(inputs["L1_ffn_W2"].T))
    sh["b1"] = _col_tile(inputs["L1_ffn_b1"])
    sh["b2"] = _col_tile(inputs["L1_ffn_b2"])
    ew = np.ascontiguousarray(inputs["L0_ffn_expert_weights"], dtype=np.float32)
    ei = np.ascontiguousarray(inputs["L0_ffn_expert_indices"], dtype=np.int32)
    in_maps = []
    for c in range(NCORE):
        m = dict(sh)
        idx = np.r_[1024 * c:1024 * (c + 1), 8192 + 1024 * c:8192 + 1024 * (c + 1)]
        m["xT"] = _pack(np.ascontiguousarray(xflat[idx].T))
        m["ew"] = np.ascontiguousarray(ew[1024 * c:1024 * (c + 1)].reshape(1, 2048))
        m["ei"] = np.ascontiguousarray(ei[1024 * c:1024 * (c + 1)].reshape(1, 2048))
        in_maps.append(m)
    return in_maps


def _unpack_out(res, key="outT"):
    o = np.empty((16384, 1024), dtype=np.float32)
    for c in range(NCORE):
        arr = np.asarray(res[c][key])
        outT = arr.reshape(128, 8, 2048).transpose(1, 0, 2).reshape(1024, 2048)
        idx = np.r_[1024 * c:1024 * (c + 1), 8192 + 1024 * c:8192 + 1024 * (c + 1)]
        o[idx] = outT.T
    return o.reshape(4, 4096, 1024)


_KEYS = {
    (0, 1): None,
    (0,): ["xT", "kT", "V", "lng", "lnb", "W1A", "W1B", "W2A", "W2B", "ew", "ei"],
    (1,): ["xT", "wqT", "wkT", "wvT", "woT", "bq", "bo", "bkr", "bvr",
           "W1f", "W2f", "b1", "b2"],
}


def run_layers(in_maps, layers=(0, 1), dump=None, trace=False):
    nc = Build(layers=layers, dump=dump).build()
    keep = _KEYS[tuple(layers)]
    maps = [{k: v for k, v in m.items() if keep is None or k in keep} for m in in_maps]
    return run_bass_kernel_spmd(nc, maps, core_ids=list(range(NCORE)), trace=trace)


def kernel(**inputs):
    in_maps = _host_inputs(inputs)
    res = run_layers(in_maps, layers=(0, 1))
    return _unpack_out(res.results)


# version 11
# speedup vs baseline: 1.0949x; 1.0216x over previous
import numpy as np
from contextlib import ExitStack

import concourse.bass as bass
import concourse.mybir as mybir
from concourse.bass_utils import run_bass_kernel_spmd

F32 = mybir.dt.float32
BF16 = mybir.dt.bfloat16
I32 = mybir.dt.int32
AF = mybir.ActivationFunctionType
ALU = mybir.AluOpType
AXX = mybir.AxisListType.X

MAGIC = 12582912.0
NCORE = 8
T = 2048
NCH = 8
EPS0 = 1e-5


class S:
    def __init__(self, nc, es, name):
        self.h = es.enter_context(nc.semaphore(name))
        self.n = 0

    def inc(self, inst, k=1):
        inst.then_inc(self.h, k)
        self.n += k
        return self.n


class Build:
    def __init__(self, layers=(0, 1), dump=None):
        self.layers = layers
        self.dump = dump
        self.nc = bass.Bass("TRN2", target_bir_lowering=False)
        self.es = ExitStack()

    def sb(self, name, shape, dt=F32, stack=None):
        return (stack or self.es).enter_context(self.nc.sbuf_tensor(name, shape, dt))

    def psum(self, name, shape, dt=F32, stack=None):
        return (stack or self.es).enter_context(self.nc.psum_tensor(name, shape, dt))

    def V(self, eng, inst, sem=None):
        s = sem if sem is not None else self.dv1
        s.inc(inst)
        eng.wait_ge(s.h, s.n)
        return inst

    def barrier(self):
        sems = [self.pe1, self.pe2, self.dv1, self.dv2, self.ac1, self.sy, self.gp, self.cc,
                self.gpA, self.gpB, self.syA, self.syB, self.syC, self.syD]
        for emit in (self.block.tensor, self.block.vector, self.block.scalar,
                     self.block.sync, self.block.gpsimd):
            def _f(eng, sems=sems):
                for s in sems:
                    if s.n > 0:
                        eng.wait_ge(s.h, s.n)
            emit(_f)

    def build(self):
        nc = self.nc
        es = self.es
        dram = {}

        def din(name, shape, dt=F32):
            dram[name] = nc.dram_tensor(name, shape, dt, kind="ExternalInput").ap()
            return dram[name]

        l0 = 0 in self.layers
        l1 = 1 in self.layers
        din("xT", [128, NCH * T])
        if l0:
            din("kT", [128, 2 * 4096]); din("V", [128, 32 * 256])
            din("lng", [128, 8]); din("lnb", [128, 8])
            din("W1A", [128, 8 * 512]); din("W1B", [64, 8 * 4096])
            din("W2A", [128, 8 * 32 * 64]); din("W2B", [128, 4 * 1024])
            din("ew", [1, 2048]); din("ei", [1, 2048], I32)
        if l1:
            din("wqT", [128, 8 * 1024]); din("wkT", [128, 8 * 1024])
            din("wvT", [128, 8 * 1024]); din("woT", [128, 8 * 1024])
            din("bq", [128, 8]); din("bo", [128, 8])
            din("bkr", [1, 1024]); din("bvr", [1, 1024])
            din("W1f", [128, 8 * 4096]); din("W2f", [128, 32 * 1024])
            din("b1", [128, 32]); din("b2", [128, 8])
        out = nc.dram_tensor("outT", [128, NCH * T], F32, kind="ExternalOutput").ap()
        dbg = {}
        if self.dump:
            dbg["dXN"] = nc.dram_tensor("dXN", [128, NCH * T], BF16, kind="ExternalOutput").ap()
        if l1:
            cc_in = nc.dram_tensor("cc_in", [64, 16 * 130], F32, kind="Internal").ap()
            cc_out = nc.dram_tensor("cc_out", [64, 16 * 130], F32, kind="Internal").ap()
            w1t_dram = nc.dram_tensor("w1t_dram", [128, 8 * 4096], BF16, kind="Internal").ap()
            w2t_dram = nc.dram_tensor("w2t_dram", [128, 32 * 1024], BF16, kind="Internal").ap()

        with es:
            ACC = self.sb("ACC", [128, NCH * T], F32)
            XN = self.sb("XN", [128, NCH * T], BF16)
            self._ones = self.sb("ones", [128, 1], F32)
            self._onesr = self.sb("onesr", [1, 128], F32)
            self._bcp2 = self.sb("bcp2", [2, 128], F32)
            smalls = []
            lng = lnb = bq = bo = b1 = b2 = None
            if l0:
                lng = self.sb("lng_t", [128, 8]); lnb = self.sb("lnb_t", [128, 8])
                smalls += [("lng", lng), ("lnb", lnb)]
            if l1:
                bq = self.sb("bq_t", [128, 8]); bo = self.sb("bo_t", [128, 8])
                b1 = self.sb("b1_t", [128, 32]); b2 = self.sb("b2_t", [128, 8])
                smalls += [("bq", bq), ("bo", bo), ("b1", b1), ("b2", b2)]

            self.pe1 = S(nc, es, "pe1"); self.pe2 = S(nc, es, "pe2")
            self.dv1 = S(nc, es, "dv1"); self.dv2 = S(nc, es, "dv2")
            self.ac1 = S(nc, es, "ac1")
            self.sy = S(nc, es, "sy"); self.gp = S(nc, es, "gp")
            self.cc = S(nc, es, "cc"); self.st = S(nc, es, "st")
            self.gpA = S(nc, es, "gpA"); self.gpB = S(nc, es, "gpB")
            self.syA = S(nc, es, "syA"); self.syB = S(nc, es, "syB")
            self.syC = S(nc, es, "syC"); self.syD = S(nc, es, "syD")
            self.gpC = S(nc, es, "gpC"); self.gpD = S(nc, es, "gpD")
            sy, gp, st = self.sy, self.gp, self.st

            Xv = [ACC[:, T * i:T * (i + 1)] for i in range(NCH)]
            XNv = [XN[:, T * i:T * (i + 1)] for i in range(NCH)]

            blk_cm = nc.Block()
            self.block = block = blk_cm.__enter__()
            try:
                def p0_sync(sync):
                    for j in range(4):
                        sy.inc(sync.dma_start(out=ACC[:, 4096 * j:4096 * (j + 1)],
                                              in_=dram["xT"][:, 4096 * j:4096 * (j + 1)]), 16)
                block.sync(p0_sync)

                def p0_gp(g):
                    for nm, t_ in smalls:
                        gp.inc(g.dma_start(out=t_[:], in_=dram[nm][:]), 16)
                block.gpsimd(p0_gp)

                def p0_v(v):
                    v.memset(self._ones[:], 1.0)
                    self.dv1.inc(v.memset(self._onesr[:], 1.0))
                block.vector(p0_v)

                bcp_np = np.zeros((2, 128), dtype=np.float32)
                bcp_np[0, 0:64] = 1.0
                bcp_np[1, 64:128] = 1.0
                bcp_const = nc.inline_tensor(bcp_np, name="bcp_const").ap()

                def p0_gp2(g):
                    gp.inc(g.dma_start(out=self._bcp2[:], in_=bcp_const[:]), 16)
                block.gpsimd(p0_gp2)

                self.barrier()

                if l0:
                    with ExitStack() as l0scope:
                        self._keep_stack = l0scope
                        self.ln_phase(Xv, XNv, fold=True)
                        self.barrier()
                        if self.dump == "p1":
                            self.finish(out, ACC, XN, dbg); return nc
                        self.l0_attn(Xv, XNv, dram, lng, lnb)
                        self.barrier()
                    if self.dump == "p3":
                        self.finish(out, ACC, XN, dbg); return nc
                    self.moe(Xv, XNv, dram)
                    self.barrier()
                if not l1 or self.dump == "p4":
                    self.finish(out, ACC, XN, dbg); return nc

                self.ternarize(dram, w1t_dram, w2t_dram)
                if self.dump == "p4b":
                    dW1 = nc.dram_tensor("dW1t", [128, 8 * 4096], BF16, kind="ExternalOutput").ap()
                    dW2 = nc.dram_tensor("dW2t", [128, 32 * 1024], BF16, kind="ExternalOutput").ap()
                    def f_tdump(sync):
                        for s_ in [self.pe1, self.pe2, self.dv1, self.dv2, self.ac1,
                                   self.sy, self.gp, self.cc,
                                   self.gpA, self.gpB, self.syA, self.syB, self.syC, self.syD]:
                            if s_.n > 0:
                                sync.wait_ge(s_.h, s_.n)
                        self.st.inc(sync.dma_start(out=dW1[:], in_=w1t_dram[:]), 16)
                        self.st.inc(sync.dma_start(out=dW2[:], in_=w2t_dram[:]), 16)
                        sync.wait_ge(self.st.h, self.st.n)
                    block.sync(f_tdump)
                    self.finish(out, ACC, XN, dbg); return nc
                self.ln_phase(Xv, XNv, fold=False)
                self.barrier()
                if self.dump == "p5":
                    self.finish(out, ACC, XN, dbg); return nc
                self.l1_attn(Xv, XNv, dram, bq, bo, cc_in, cc_out)
                self.barrier()
                if self.dump == "p67":
                    self.finish(out, ACC, XN, dbg); return nc
                self.l1_ffn(Xv, XNv, dram, b1, b2, w1t_dram, w2t_dram, out)
            finally:
                blk_cm.__exit__(None, None, None)
        return nc

    def finish(self, out, ACC, XN, dbg, tern=None):
        st = self.st
        sems = [self.pe1, self.pe2, self.dv1, self.dv2, self.ac1, self.sy, self.gp, self.cc,
                self.gpA, self.gpB, self.syA, self.syB, self.syC, self.syD,
                self.gpC, self.gpD]

        def f(sync):
            for s_ in sems:
                if s_.n > 0:
                    sync.wait_ge(s_.h, s_.n)
            for j in range(4):
                st.inc(sync.dma_start(out=out[:, 4096 * j:4096 * (j + 1)],
                                      in_=ACC[:, 4096 * j:4096 * (j + 1)]), 16)
            if dbg:
                st.inc(sync.dma_start(out=dbg["dXN"][:], in_=XN[:]), 16)
            sync.wait_ge(st.h, st.n)
        self.block.sync(f)
        if tern is not None and dbg:
            pass

    def ln_phase(self, Xv, XNv, fold):
        block = self.block
        pe1, pe2, dv1, ac1 = self.pe1, self.pe2, self.dv1, self.ac1
        ones, onesr = self._ones, self._onesr
        sfx = "_f" if fold else "_n"
        if fold:
            self._r_var = self.sb("r_var_keep", [1, T], stack=self._keep_stack)
            self._r_rstd = self.sb("r_rstd_keep", [1, T], stack=self._keep_stack)
        with ExitStack() as ph:
            r_mean = self.sb("r_mean" + sfx, [1, T], stack=ph)
            r_var = self._r_var if fold else self.sb("r_var" + sfx, [1, T], stack=ph)
            r_rstd = self._r_rstd if fold else self.sb("r_rstd" + sfx, [1, T], stack=ph)
            r_a = self.sb("r_a" + sfx, [1, T], stack=ph)
            r_tmp = self.sb("r_tmp" + sfx, [1, T], stack=ph)
            SQ = self.sb("SQ" + sfx, [128, 2 * 512], stack=ph)
            N1A = self.sb("N1A" + sfx, [128, 8 * 512], stack=ph)
            pb_sum = [self.psum("pb_sum%s%d" % (sfx, k), [1, 512], stack=ph) for k in range(2)]
            pb_sq = [self.psum("pb_sq%s%d" % (sfx, k), [1, 512], stack=ph) for k in range(2)]
            pb_r = [self.psum("pb_r%s%d" % (sfx, k), [128, 512], stack=ph) for k in range(2)]
            pb_a = [self.psum("pb_a%s%d" % (sfx, k), [128, 512], stack=ph) for k in range(2)]

            rows_done = {}; norm_done = {}; stats_done = {}
            for tt in range(4):
                sl = slice(512 * tt, 512 * (tt + 1))
                par = tt % 2
                ac_b = ac1.n; pe_b = pe1.n

                def f_act(sc, sl=sl, ac_b=ac_b, pe_b=pe_b, tt=tt):
                    for i in range(NCH):
                        if tt > 0 and i < 2:
                            sc.wait_ge(pe1.h, pe_b)
                        if i >= 2:
                            sc.wait_ge(pe1.h, pe_b + i - 1)
                        ac1.inc(sc.activation(out=SQ[:, 512 * (i % 2):512 * (i % 2 + 1)],
                                              in_=Xv[i][:, sl], func=AF.Square))
                block.scalar(f_act)

                def f_pe(t, sl=sl, par=par, ac_b=ac_b, tt=tt):
                    if tt >= 2:
                        t.wait_ge(dv1.h, rows_done[tt - 2])
                    for i in range(NCH):
                        t.matmul(pb_sum[par][:], ones[:], Xv[i][:, sl],
                                 start=(i == 0), stop=(i == NCH - 1))
                        t.wait_ge(ac1.h, ac_b + i + 1)
                        pe1.inc(t.matmul(pb_sq[par][:], ones[:],
                                         SQ[:, 512 * (i % 2):512 * (i % 2 + 1)],
                                         start=(i == 0), stop=(i == NCH - 1)))
                block.tensor(f_pe)
                stats_done[tt] = pe1.n

                def f_rows(v, sl=sl, par=par, pe_s=stats_done[tt]):
                    v.wait_ge(pe1.h, pe_s)
                    self.V(v, v.tensor_scalar(out=r_mean[:, sl], in0=pb_sum[par][:],
                                              scalar1=1.0 / 1024, scalar2=None, op0=ALU.mult))
                    self.V(v, v.tensor_scalar(out=r_var[:, sl], in0=pb_sq[par][:],
                                              scalar1=1.0 / 1024, scalar2=None, op0=ALU.mult))
                    self.V(v, v.tensor_tensor(out=r_tmp[:, sl], in0=r_mean[:, sl],
                                              in1=r_mean[:, sl], op=ALU.mult))
                    self.V(v, v.tensor_tensor(out=r_var[:, sl], in0=r_var[:, sl],
                                              in1=r_tmp[:, sl], op=ALU.subtract))
                    self.V(v, v.tensor_scalar(out=r_tmp[:, sl], in0=r_var[:, sl],
                                              scalar1=EPS0, scalar2=None, op0=ALU.add))
                block.vector(f_rows)
                rows_done[tt] = dv1.n

                def f_sqrt(sc, sl=sl, dv_r1=rows_done[tt]):
                    sc.wait_ge(dv1.h, dv_r1)
                    self.V(sc, sc.activation(out=r_tmp[:, sl], in_=r_tmp[:, sl], func=AF.Ln),
                           sem=ac1)
                    ac1.inc(sc.activation(out=r_rstd[:, sl], in_=r_tmp[:, sl], func=AF.Exp,
                                          scale=-0.5))
                block.scalar(f_sqrt)
                ac_r1 = ac1.n

                def f_rows2(v, sl=sl, ac_r1=ac_r1):
                    v.wait_ge(ac1.h, ac_r1)
                    self.V(v, v.scalar_tensor_tensor(out=r_a[:, sl], in0=r_mean[:, sl],
                                                     scalar=-1.0, in1=r_rstd[:, sl],
                                                     op0=ALU.mult, op1=ALU.mult))
                block.vector(f_rows2)
                dv_r2 = dv1.n

                def f_bc(t, sl=sl, par=par, dv_r2=dv_r2, tt=tt):
                    t.wait_ge(dv1.h, dv_r2)
                    if tt >= 2:
                        t.wait_ge(dv1.h, norm_done[tt - 2])
                    t.matmul(pb_r[par][:], onesr[:], r_rstd[:, sl], start=True, stop=True)
                    pe2.inc(t.matmul(pb_a[par][:], onesr[:], r_a[:, sl], start=True, stop=True))
                block.tensor(f_bc)
                pe2_r = pe2.n

                def f_norm(v, sl=sl, par=par, pe2_r=pe2_r):
                    v.wait_ge(pe2.h, pe2_r)
                    for i in range(NCH):
                        inst = v.tensor_tensor(out=N1A[:, 512 * i:512 * (i + 1)],
                                               in0=Xv[i][:, sl], in1=pb_r[par][:], op=ALU.mult)
                        if i == NCH - 1:
                            dv1.inc(inst); v.wait_ge(dv1.h, dv1.n)
                    for i in range(NCH):
                        inst = v.tensor_tensor(out=XNv[i][:, sl], in0=N1A[:, 512 * i:512 * (i + 1)],
                                               in1=pb_a[par][:], op=ALU.add)
                        if i == NCH - 1:
                            dv1.inc(inst); v.wait_ge(dv1.h, dv1.n)
                    if fold:
                        for i in range(NCH):
                            inst = v.tensor_tensor(out=Xv[i][:, sl], in0=Xv[i][:, sl],
                                                   in1=XNv[i][:, sl], op=ALU.add)
                            if i == NCH - 1:
                                dv1.inc(inst); v.wait_ge(dv1.h, dv1.n)
                block.vector(f_norm)
                norm_done[tt] = dv1.n

    def l0_attn(self, Xv, XNv, dram, lng, lnb):
        block = self.block
        pe1, pe2, dv1, dv2, ac1, gp = self.pe1, self.pe2, self.dv1, self.dv2, self.ac1, self.gp
        onesr = self._onesr
        with ExitStack() as ph:
            X2 = self.sb("X2", [128, NCH * T], BF16, stack=ph)
            X2v = [X2[:, T * i:T * (i + 1)] for i in range(NCH)]
            r_rstd2 = self.sb("r_rstd2", [1, T], stack=ph)
            r_t2 = self.sb("r_t2", [1, T], stack=ph)
            N1 = self.sb("N1b", [128, 512], stack=ph)
            kTt = self.sb("kTt", [128, 2 * 4096], BF16, stack=ph)
            Vt = self.sb("Vt", [128, 32 * 256], BF16, stack=ph)
            RB = self.sb("RB", [128, 4 * 512], stack=ph)
            PT = self.sb("PT", [128, 4 * 512], BF16, stack=ph)

            def f_load(g):
                gp.inc(g.dma_start(out=kTt[:], in_=dram["kT"][:]), 16)
                gp.inc(g.dma_start(out=Vt[:], in_=dram["V"][:]), 16)
            block.gpsimd(f_load)
            gp_w = gp.n

            def f_rows(v):
                self.V(v, v.scalar_tensor_tensor(out=r_t2[:], in0=self._r_rstd[:], scalar=1.0,
                                                 in1=self._r_rstd[:], op0=ALU.mult, op1=ALU.mult))
                self.V(v, v.tensor_tensor(out=r_t2[:], in0=self._r_var[:], in1=r_t2[:], op=ALU.mult))
                self.V(v, v.tensor_scalar(out=r_t2[:], in0=r_t2[:], scalar1=EPS0,
                                          scalar2=None, op0=ALU.add))
            block.vector(f_rows)
            dv_r = dv1.n

            def f_sq(sc, dv_r=dv_r):
                sc.wait_ge(dv1.h, dv_r)
                self.V(sc, sc.activation(out=r_t2[:], in_=r_t2[:], func=AF.Ln), sem=ac1)
                ac1.inc(sc.activation(out=r_rstd2[:], in_=r_t2[:], func=AF.Exp, scale=-0.5))
            block.scalar(f_sq)
            ac_r = ac1.n
            dvn0 = dv1.n

            x2scope = ExitStack()
            pb_r = self.psum("pb_r2", [128, 512], stack=x2scope)
            for tt in range(4):
                sl = slice(512 * tt, 512 * (tt + 1))
                dvn = dv1.n
                def f_bc(t, sl=sl, dvn=dvn, ac_r=ac_r):
                    t.wait_ge(dv1.h, dvn)
                    t.wait_ge(ac1.h, ac_r)
                    pe2.inc(t.matmul(pb_r[:], onesr[:], r_rstd2[:, sl], start=True, stop=True))
                block.tensor(f_bc)
                pe2_r = pe2.n
                def f_norm(v, sl=sl, pe2_r=pe2_r):
                    v.wait_ge(pe2.h, pe2_r)
                    for i in range(NCH):
                        self.V(v, v.tensor_tensor(out=N1[:], in0=XNv[i][:, sl], in1=pb_r[:], op=ALU.mult))
                        self.V(v, v.tensor_scalar(out=X2v[i][:, sl], in0=N1[:],
                                                  scalar1=lng[:, i:i + 1], scalar2=lnb[:, i:i + 1],
                                                  op0=ALU.mult, op1=ALU.add))
                block.vector(f_norm)

            x2scope.close()
            ps_s = [self.psum("ps_s%d" % k, [128, 512], stack=ph) for k in range(4)]
            ps_o = [self.psum("ps_o%d" % k, [128, 512], stack=ph) for k in range(4)]

            dv2_0 = dv2.n
            it = 0
            for h in range(4):
                for tt in range(4):
                    sl = slice(512 * tt, 512 * (tt + 1))
                    oset = (it % 2) * 2
                    po = (ps_o[oset], ps_o[oset + 1])
                    pe_b = pe1.n; pe2_b = pe2.n; dv_b = dv1.n; ac_b = ac1.n

                    def f_pe(t, h=h, sl=sl, po=po, dv_b=dv_b, ac_b=ac_b, it=it):
                        if it == 0:
                            t.wait_ge(gp.h, gp_w)
                            t.wait_ge(dv1.h, dv_b)
                        elif it >= 2:
                            t.wait_ge(dv2.h, dv2_0 + it - 1)
                        for s in range(32):
                            if s >= 4:
                                t.wait_ge(dv1.h, dv_b + s - 3)
                            t.matmul(ps_s[s % 4][:], kTt[:, 128 * s:128 * (s + 1)],
                                     X2v[2 * h][:, sl], start=True, stop=False)
                            pe1.inc(t.matmul(ps_s[s % 4][:],
                                             kTt[:, 4096 + 128 * s:4096 + 128 * (s + 1)],
                                             X2v[2 * h + 1][:, sl], start=False, stop=True))
                            if s >= 3:
                                sp = s - 3
                                t.wait_ge(ac1.h, ac_b + sp + 1)
                                t.matmul(po[0][:], Vt[:, 256 * sp:256 * sp + 128],
                                         PT[:, 512 * (sp % 4):512 * (sp % 4 + 1)],
                                         start=(sp == 0), stop=False)
                                pe2.inc(t.matmul(po[1][:], Vt[:, 256 * sp + 128:256 * (sp + 1)],
                                                 PT[:, 512 * (sp % 4):512 * (sp % 4 + 1)],
                                                 start=(sp == 0), stop=False))
                        for sp in (29, 30, 31):
                            t.wait_ge(ac1.h, ac_b + sp + 1)
                            t.matmul(po[0][:], Vt[:, 256 * sp:256 * sp + 128],
                                     PT[:, 512 * (sp % 4):512 * (sp % 4 + 1)],
                                     start=False, stop=(sp == 31))
                            pe2.inc(t.matmul(po[1][:], Vt[:, 256 * sp + 128:256 * (sp + 1)],
                                             PT[:, 512 * (sp % 4):512 * (sp % 4 + 1)],
                                             start=False, stop=(sp == 31)))
                    block.tensor(f_pe)

                    def f_dv(v, pe_b=pe_b, ac_b=ac_b):
                        for s in range(32):
                            v.wait_ge(pe1.h, pe_b + s + 1)
                            if s >= 4:
                                v.wait_ge(ac1.h, ac_b + s - 3)
                            dv1.inc(v.tensor_scalar(out=RB[:, 512 * (s % 4):512 * (s % 4 + 1)],
                                                    in0=ps_s[s % 4][:], scalar1=0.0,
                                                    scalar2=None, op0=ALU.max))
                    block.vector(f_dv)

                    def f_ac(sc, dv_b=dv_b, pe2_b=pe2_b):
                        for s in range(32):
                            sc.wait_ge(dv1.h, dv_b + s + 1)
                            if s >= 4:
                                sc.wait_ge(pe2.h, pe2_b + s - 3)
                            ac1.inc(sc.activation(out=PT[:, 512 * (s % 4):512 * (s % 4 + 1)],
                                                  in_=RB[:, 512 * (s % 4):512 * (s % 4 + 1)],
                                                  func=AF.Square))
                    block.scalar(f_ac)

                    def f_evac(v, h=h, sl=sl, po=po, pe2_b=pe2_b):
                        v.wait_ge(pe2.h, pe2_b + 32)
                        v.tensor_tensor(out=Xv[2 * h][:, sl], in0=Xv[2 * h][:, sl],
                                        in1=po[0][:], op=ALU.add)
                        dv2.inc(v.tensor_tensor(out=Xv[2 * h + 1][:, sl], in0=Xv[2 * h + 1][:, sl],
                                                in1=po[1][:], op=ALU.add))
                    block.vector(f_evac)
                    it += 1

    def moe(self, Xv, XNv, dram):
        block = self.block
        pe1, pe2, dv1, dv2, ac1, gp = self.pe1, self.pe2, self.dv1, self.dv2, self.ac1, self.gp
        onesr = self._onesr
        with ExitStack() as ph:
            W1At = self.sb("W1At", [128, 8 * 512], BF16, stack=ph)
            W1Bt = [self.sb("W1Bt%d" % k, [64, 4096], BF16, stack=ph) for k in range(2)]
            W2Bt = self.sb("W2Bt", [128, 4 * 1024], BF16, stack=ph)
            W2At = [self.sb("W2At%d" % k, [128, 32 * 64], BF16, stack=ph) for k in range(2)]
            Zt = self.sb("Zt", [64, 8 * 1024], BF16, stack=ph)
            Ht = self.sb("Ht", [128, 4 * 512], BF16, stack=ph)
            Gt = self.sb("Gt", [128, 4 * 1024], BF16, stack=ph)
            GS = self.sb("GS", [64, 1024], stack=ph)
            ewt = self.sb("ewt", [1, 2048], stack=ph)
            eit = self.sb("eit", [1, 2048], stack=ph)
            cfm = self.sb("cfm", [1, 2048], stack=ph)
            cfr = self.sb("cfr", [1, 1024], stack=ph)

            gpA, gpB = self.gpA, self.gpB

            def f_load(g):
                gp.inc(g.dma_start(out=W1At[:], in_=dram["W1A"][:]), 16)
                gp.inc(g.dma_start(out=W2Bt[:], in_=dram["W2B"][:]), 16)
                gp.inc(g.dma_start(out=ewt[:], in_=dram["ew"][:]), 16)
                gp.inc(g.dma_start(out=eit[:], in_=dram["ei"][:]), 16)
                gpA.inc(g.dma_start(out=W1Bt[0][:], in_=dram["W1B"][:, 0:4096]), 16)
                gpA.inc(g.dma_start(out=W2At[0][:], in_=dram["W2A"][:, 0:2048]), 16)
            block.gpsimd(f_load)
            gp_b = gp.n
            gpw_ready = {0: (gpA, gpA.n)}

            with ExitStack() as zp:
                pz = [self.psum("pz%d" % k, [128, 512], stack=zp) for k in range(2)]
                dvz0 = dv2.n
                for n, (rc, th) in enumerate([(rc, th) for rc in range(4) for th in range(2)]):
                    buf = n % 2
                    def f_pe(t, rc=rc, th=th, buf=buf, n=n):
                        if n == 0:
                            t.wait_ge(gp.h, gp_b)
                        elif n >= 2:
                            t.wait_ge(dv2.h, dvz0 + n - 1)
                        for i in range(NCH):
                            mm = t.matmul(pz[buf][:],
                                          W1At[:, 512 * i + 128 * rc:512 * i + 128 * (rc + 1)],
                                          XNv[i][:, 512 * th:512 * (th + 1)],
                                          start=(i == 0), stop=(i == NCH - 1))
                        pe1.inc(mm)
                    block.tensor(f_pe)
                    pe_e = pe1.n
                    def f_ev(v, rc=rc, th=th, buf=buf, pe_e=pe_e):
                        v.wait_ge(pe1.h, pe_e)
                        c0 = 1024 * (2 * rc) + 512 * th
                        c1 = 1024 * (2 * rc + 1) + 512 * th
                        v.tensor_copy(out=Zt[:, c0:c0 + 512], in_=pz[buf][0:64, :])
                        dv2.inc(v.tensor_copy(out=Zt[:, c1:c1 + 512], in_=pz[buf][64:128, :]))
                    block.vector(f_ev)

            with ExitStack() as ep:
                ph_ = [self.psum("ph%d" % k, [128, 512], stack=ep) for k in range(4)]
                pg = [self.psum("pg%d" % k, [64, 512], stack=ep) for k in range(2)]
                pc = [self.psum("pc%d" % k, [64, 512], stack=ep) for k in range(2)]
                for e in range(8):
                    if e < 7:
                        dv2_s = dv2.n
                        par = gpB if (e + 1) % 2 else gpA
                        def f_pre(g, e=e, dv2_s=dv2_s, par=par):
                            g.wait_ge(dv2.h, dv2_s)
                            par.inc(g.dma_start(out=W1Bt[(e + 1) % 2][:],
                                                in_=dram["W1B"][:, 4096 * (e + 1):4096 * (e + 2)]), 16)
                            par.inc(g.dma_start(out=W2At[(e + 1) % 2][:],
                                                in_=dram["W2A"][:, 2048 * (e + 1):2048 * (e + 2)]), 16)
                        block.gpsimd(f_pre)
                        gpw_ready[e + 1] = (par, par.n)

                    def f_coef(v, e=e):
                        self.V(v, v.tensor_scalar(out=cfm[:], in0=eit[:], scalar1=float(e),
                                                  scalar2=None, op0=ALU.is_equal))
                        self.V(v, v.tensor_tensor(out=cfm[:], in0=cfm[:], in1=ewt[:], op=ALU.mult))
                        cv = cfm.rearrange("a (t k) -> a t k", k=2)
                        cfrv = cfr.rearrange("a (t k) -> a t k", k=1)
                        self.V(v, v.tensor_tensor(out=cfrv[:, :, :], in0=cv[:, :, 0:1],
                                                  in1=cv[:, :, 1:2], op=ALU.add))
                    block.vector(f_coef)
                    dv_cf = dv1.n

                    gpw_sem, gpw_n = gpw_ready[e]
                    dv2_e = dv2.n
                    for th in range(2):
                        pe_b = pe1.n; ac_b = ac1.n; pe2_s = pe2.n

                        def f_pe(t, e=e, th=th, ac_b=ac_b, gpw_sem=gpw_sem, gpw_n=gpw_n, dv2_e=dv2_e):
                            if th == 0:
                                t.wait_ge(gpw_sem.h, gpw_n)
                                t.wait_ge(dv2.h, dv2_e)

                            def g_mm(f):
                                t.wait_ge(ac1.h, ac_b + f + 1)
                                pe2.inc(t.matmul(pg[th][:],
                                                 W2At[e % 2][:, 64 * f:64 * (f + 1)],
                                                 Ht[:, 512 * (f % 4):512 * (f % 4 + 1)],
                                                 start=(f == 0), stop=(f == 31)))
                            for f in range(32):
                                t.wait_ge(ac1.h, ac_b + max(0, f - 3))
                                pe1.inc(t.matmul(ph_[f % 4][:],
                                                 W1Bt[e % 2][:, 128 * f:128 * (f + 1)],
                                                 Zt[:, 1024 * e + 512 * th:1024 * e + 512 * (th + 1)],
                                                 start=True, stop=True))
                                if f >= 3:
                                    g_mm(f - 3)
                            g_mm(29); g_mm(30); g_mm(31)
                        block.tensor(f_pe)

                        def f_gelu(sc, pe_b=pe_b, pe2_s=pe2_s):
                            for f in range(32):
                                sc.wait_ge(pe1.h, pe_b + f + 1)
                                sc.wait_ge(pe2.h, pe2_s + max(0, f - 3))
                                ac1.inc(sc.activation(out=Ht[:, 512 * (f % 4):512 * (f % 4 + 1)],
                                                      in_=ph_[f % 4][:], func=AF.Gelu))
                        block.scalar(f_gelu)

                    pe2_gdone = pe2.n
                    def f_bc(t, dv_cf=dv_cf, dv2_e=dv2_e):
                        t.wait_ge(dv1.h, dv_cf)
                        t.wait_ge(dv2.h, dv2_e)
                        for th in range(2):
                            pe1.inc(t.matmul(pc[th][:], onesr[:, 0:64],
                                             cfr[:, 512 * th:512 * (th + 1)],
                                             start=True, stop=True))
                    block.tensor(f_bc)
                    pe_bc = pe1.n

                    def f_gs(sc, pe2_gdone=pe2_gdone):
                        sc.wait_ge(pe2.h, pe2_gdone)
                        for th in range(2):
                            ac1.inc(sc.activation(out=GS[:, 512 * th:512 * (th + 1)],
                                                  in_=pg[th][:], func=AF.Copy))
                    block.scalar(f_gs)
                    ac_gs = ac1.n

                    def f_g(v, e=e, ac_gs=ac_gs, pe_bc=pe_bc):
                        v.wait_ge(ac1.h, ac_gs)
                        v.wait_ge(pe1.h, pe_bc)
                        c = e // 2
                        r0 = 64 * (e % 2)
                        for th in range(2):
                            dv2.inc(v.tensor_tensor(
                                out=Gt[r0:r0 + 64, 1024 * c + 512 * th:1024 * c + 512 * (th + 1)],
                                in0=GS[:, 512 * th:512 * (th + 1)], in1=pc[th][:], op=ALU.mult))
                    block.vector(f_g)

            with ExitStack() as yp:
                py = [self.psum("py%d" % k, [128, 512], stack=yp) for k in range(2)]
                dvg = dv2.n
                for n, (i, th) in enumerate([(i, th) for i in range(NCH) for th in range(2)]):
                    buf = n % 2
                    def f_pe(t, i=i, th=th, buf=buf, n=n, dvg=dvg):
                        if n == 0:
                            t.wait_ge(dv2.h, dvg)
                        elif n >= 2:
                            t.wait_ge(dv2.h, dvg + n - 1)
                        for rc in range(4):
                            mm = t.matmul(py[buf][:],
                                          W2Bt[:, 1024 * rc + 128 * i:1024 * rc + 128 * (i + 1)],
                                          Gt[:, 1024 * rc + 512 * th:1024 * rc + 512 * (th + 1)],
                                          start=(rc == 0), stop=(rc == 3))
                        pe1.inc(mm)
                    block.tensor(f_pe)
                    pe_y = pe1.n
                    def f_ev(v, i=i, th=th, buf=buf, pe_y=pe_y):
                        v.wait_ge(pe1.h, pe_y)
                        dv2.inc(v.tensor_tensor(out=Xv[i][:, 512 * th:512 * (th + 1)],
                                                in0=Xv[i][:, 512 * th:512 * (th + 1)],
                                                in1=py[buf][:], op=ALU.add))
                    block.vector(f_ev)

    def ternarize(self, dram, w1t_dram, w2t_dram):
        block = self.block
        pe1, dv1, ac1, sy, gp = self.pe1, self.dv1, self.ac1, self.sy, self.gp
        ones, onesr = self._ones, self._onesr
        with ExitStack() as ph:
            WS = [self.sb("WS%d" % k, [128, 4096], stack=ph) for k in range(4)]
            T1 = self.sb("T1", [128, 4096], stack=ph)
            TB = [self.sb("TB%d" % k, [128, 4096], BF16, stack=ph) for k in range(2)]
            parts = self.sb("parts", [128, 8], stack=ph)
            accp = self.sb("accp", [128, 1], stack=ph)
            sc1 = self.sb("sc1", [1, 1], stack=ph)
            rS = self.sb("rS", [128, 1], stack=ph)
            p1 = self.psum("pt1", [1, 512], stack=ph)
            p2 = self.psum("pt2", [128, 512], stack=ph)

            syp = [self.syA, self.syB, self.syC, self.syD]
            gpp = [self.gpA, self.gpB]
            for (src, dst) in ((dram["W1f"], w1t_dram), (dram["W2f"], w2t_dram)):
                nch = 8
                dv_b = dv1.n
                ld_n = {}
                def f_ld(s, src=src, dv_b=dv_b, ld_n=ld_n):
                    for c in range(nch):
                        if c >= 4:
                            s.wait_ge(dv1.h, dv_b + c - 3)
                        syp[c % 4].inc(s.dma_start(out=WS[c % 4][:],
                                                   in_=src[:, 4096 * c:4096 * (c + 1)]), 16)
                        ld_n[c] = syp[c % 4].n
                block.sync(f_ld)

                def f_red(v, ld_n=ld_n):
                    for c in range(nch):
                        v.wait_ge(syp[c % 4].h, ld_n[c])
                        dv1.inc(v.tensor_reduce(out=parts[:, c:c + 1], in_=WS[c % 4][:],
                                                axis=AXX, op=ALU.add, apply_absolute_value=True))
                    v.wait_ge(dv1.h, dv1.n)
                    self.V(v, v.tensor_reduce(out=accp[:], in_=parts[:], axis=AXX, op=ALU.add))
                block.vector(f_red)
                dv_red = dv1.n

                def f_pe(t, dv_red=dv_red):
                    t.wait_ge(dv1.h, dv_red)
                    pe1.inc(t.matmul(p1[:, 0:1], accp[:], ones[:], start=True, stop=True))
                block.tensor(f_pe)
                pe_s = pe1.n

                def f_sc(v, pe_s=pe_s):
                    v.wait_ge(pe1.h, pe_s)
                    self.V(v, v.tensor_scalar(out=sc1[:], in0=p1[0:1, 0:1], scalar1=1.0 / (nch * 4096 * 128),
                                              scalar2=1e-8, op0=ALU.mult, op1=ALU.max))
                    self.V(v, v.reciprocal(out=sc1[:], in_=sc1[:]))
                block.vector(f_sc)
                dv_sc = dv1.n

                def f_bc(t, dv_sc=dv_sc):
                    t.wait_ge(dv1.h, dv_sc)
                    pe1.inc(t.matmul(p2[:, 0:1], onesr[:], sc1[:], start=True, stop=True))
                block.tensor(f_bc)
                pe_s2 = pe1.n

                def f_cp(v, pe_s2=pe_s2):
                    v.wait_ge(pe1.h, pe_s2)
                    self.V(v, v.tensor_copy(out=rS[:], in_=p2[:, 0:1]))
                block.vector(f_cp)
                dv_cp = dv1.n

                ac_b = ac1.n; dv_b5 = dv1.n
                ld2_n = {}
                def f_ld2(s, src=src, ac_b=ac_b, dv_cp=dv_cp, ld2_n=ld2_n):
                    for c in range(nch):
                        if c < 4:
                            s.wait_ge(dv1.h, dv_cp)
                        else:
                            s.wait_ge(ac1.h, ac_b + c - 3)
                        syp[c % 4].inc(s.dma_start(out=WS[c % 4][:],
                                                   in_=src[:, 4096 * c:4096 * (c + 1)]), 16)
                        ld2_n[c] = syp[c % 4].n
                block.sync(f_ld2)

                def f_act(sc, dv_cp=dv_cp, dv_b5=dv_b5, ld2_n=ld2_n):
                    sc.wait_ge(dv1.h, dv_cp)
                    for c in range(nch):
                        sc.wait_ge(syp[c % 4].h, ld2_n[c])
                        if c >= 1:
                            sc.wait_ge(dv1.h, dv_b5 + 2 * c)
                        ac1.inc(sc.activation(out=T1[:], in_=WS[c % 4][:], func=AF.Copy,
                                              scale=rS[:, 0:1], bias=MAGIC))
                block.scalar(f_act)

                st_n = {}
                gpp_base = (gpp[0].n, gpp[1].n)
                def f_tern(v, ac_b=ac_b, gpp_base=gpp_base):
                    for c in range(nch):
                        v.wait_ge(ac1.h, ac_b + c + 1)
                        if c >= 2:
                            v.wait_ge(gpp[c % 2].h, gpp_base[c % 2] + 16 * (c // 2))
                        self.V(v, v.tensor_scalar(out=T1[:], in0=T1[:], scalar1=MAGIC, scalar2=-1.0,
                                                  op0=ALU.subtract, op1=ALU.max))
                        self.V(v, v.tensor_scalar(out=TB[c % 2][:], in0=T1[:], scalar1=1.0,
                                                  scalar2=None, op0=ALU.min))
                block.vector(f_tern)

                def f_st(g, dst=dst, dv_b5=dv_b5):
                    for c in range(nch):
                        g.wait_ge(dv1.h, dv_b5 + 2 * (c + 1))
                        gpp[c % 2].inc(g.dma_start(out=dst[:, 4096 * c:4096 * (c + 1)],
                                                   in_=TB[c % 2][:]), 16)
                block.gpsimd(f_st)
                self.barrier()

    def l1_attn(self, Xv, XNv, dram, bq, bo, cc_in, cc_out):
        block = self.block
        pe1, pe2, dv1, dv2, ac1, gp, cc, sy = (self.pe1, self.pe2, self.dv1, self.dv2,
                                               self.ac1, self.gp, self.cc, self.sy)
        onesr, bcp2 = self._onesr, self._bcp2
        with ExitStack() as ph:
            wkB = self.sb("wkB", [128, 8 * 1024], BF16, stack=ph)
            wvB = self.sb("wvB", [128, 8 * 1024], BF16, stack=ph)
            bkB = self.sb("bkB", [128, 1024], stack=ph)
            bvB = self.sb("bvB", [128, 1024], stack=ph)
            bkr = self.sb("bkr_t", [1, 1024], stack=ph)
            bvr = self.sb("bvr_t", [1, 1024], stack=ph)
            KS1 = self.sb("KS1", [128, 2 * 512], stack=ph)
            KS2 = self.sb("KS2", [128, 2 * 512], stack=ph)
            KS3 = self.sb("KS3", [128, 2 * 512], stack=ph)
            KVbuf = self.sb("KVbuf", [64, 16 * 130], stack=ph)
            KFr = self.sb("KFr", [128, 3 * 1024], BF16, stack=ph)
            VPr = self.sb("VPr", [128, 3 * 16 * 65], BF16, stack=ph)
            vpv = VPr.rearrange("p (r h d) -> p r h d", r=3, h=16)

            def f_load(g):
                gp.inc(g.dma_start(out=wkB[:], in_=dram["wkT"][:]), 16)
                gp.inc(g.dma_start(out=wvB[:], in_=dram["wvT"][:]), 16)
            block.gpsimd(f_load)
            gp_w = gp.n

            def f_load2(s):
                sy.inc(s.dma_start(out=bkr[:], in_=dram["bkr"][:]), 16)
                sy.inc(s.dma_start(out=bvr[:], in_=dram["bvr"][:]), 16)
            block.sync(f_load2)
            sy_w = sy.n

            with ExitStack() as pp:
                pbb = self.psum("pbb", [128, 512], stack=pp)
                pe_b = pe1.n; dv_b = dv1.n

                def f_pebc(t, sy_w=sy_w, dv_b=dv_b):
                    t.wait_ge(sy.h, sy_w)
                    for j in range(4):
                        src = bkr if j < 2 else bvr
                        if j >= 1:
                            t.wait_ge(dv1.h, dv_b + j)
                        pe1.inc(t.matmul(pbb[:], onesr[:], src[:, 512 * (j % 2):512 * (j % 2 + 1)],
                                         start=True, stop=True))
                block.tensor(f_pebc)

                def f_dvbc(v, pe_b=pe_b):
                    for j in range(4):
                        dstt = bkB if j < 2 else bvB
                        v.wait_ge(pe1.h, pe_b + j + 1)
                        dv1.inc(v.tensor_copy(out=dstt[:, 512 * (j % 2):512 * (j % 2 + 1)],
                                              in_=pbb[:]))
                block.vector(f_dvbc)

                def f_ones(v):
                    dv1.inc(v.memset(vpv[:, :, :, 64:65], 1.0))
                block.vector(f_ones)

            with ExitStack() as kp:
                pk = [self.psum("pk%d" % k, [128, 512], stack=kp) for k in range(2)]
                kvp = [self.psum("kvp%d" % k, [64, 390 if k < 5 else 130], stack=kp)
                       for k in range(6)]
                pe_mom = {}
                ks1_done = {}
                vev_done = {}
                chain_done = {}

                def emit_moments(tc):
                    r = tc % 3
                    dv_tc = chain_done[tc]
                    def f_mom(t, tc=tc, r=r, dv_tc=dv_tc):
                        t.wait_ge(dv1.h, dv_tc)
                        slot = tc // 8
                        for h in range(16):
                            bank, off = h // 3, 130 * (h % 3)
                            mm = t.matmul(kvp[bank][:, off + 65 * slot:off + 65 * (slot + 1)],
                                          KFr[:, 1024 * r + 64 * h:1024 * r + 64 * (h + 1)],
                                          VPr[:, 1040 * r + 65 * h:1040 * r + 65 * (h + 1)],
                                          start=(tc == 0 and h % 3 == 0),
                                          stop=(tc == 15 and h % 3 == (2 if bank < 5 else 0)),
                                          skip_group_check=True)
                            if h == 15:
                                pe1.inc(mm)
                    block.tensor(f_mom)
                    pe_mom[tc] = pe1.n

                evac_done = {}
                for tc in range(16):
                    r = tc % 3
                    pe_war = pe_mom.get(tc - 3, 0)
                    for jh in range(2):
                        j = 4 * tc + jh
                        def f_pe(t, tc=tc, jh=jh, j=j, first=(tc == 0 and jh == 0)):
                            if first:
                                t.wait_ge(gp.h, gp_w)
                                t.wait_ge(dv1.h, dv1.n)
                            prev = evac_done.get(j - 2, 0)
                            if prev:
                                t.wait_ge(dv1.h, prev)
                            for i in range(NCH):
                                mm = t.matmul(pk[j % 2][:],
                                              XNv[i][:, 128 * tc:128 * (tc + 1)],
                                              wkB[:, 1024 * i + 512 * jh:1024 * i + 512 * (jh + 1)],
                                              start=(i == 0), stop=(i == NCH - 1))
                            pe1.inc(mm)
                        block.tensor(f_pe)
                        pe_kv = pe1.n
                        def f_ev(v, jh=jh, j=j, pe_kv=pe_kv):
                            v.wait_ge(pe1.h, pe_kv)
                            self.V(v, v.tensor_tensor(out=KS1[:, 512 * jh:512 * (jh + 1)],
                                                      in0=pk[j % 2][:],
                                                      in1=bkB[:, 512 * jh:512 * (jh + 1)], op=ALU.add))
                            self.V(v, v.tensor_scalar(out=KS2[:, 512 * jh:512 * (jh + 1)],
                                                      in0=KS1[:, 512 * jh:512 * (jh + 1)],
                                                      scalar1=0.0, scalar2=None, op0=ALU.min))
                        block.vector(f_ev)
                        evac_done[j] = dv1.n - 1
                        dv_k1 = dv1.n
                        def f_exp(sc, jh=jh, dv_k1=dv_k1):
                            sc.wait_ge(dv1.h, dv_k1)
                            ac1.inc(sc.activation(out=KS3[:, 512 * jh:512 * (jh + 1)],
                                                  in_=KS2[:, 512 * jh:512 * (jh + 1)], func=AF.Exp))
                        block.scalar(f_exp)
                        ac_k = ac1.n
                        def f_ev2(v, r=r, jh=jh, ac_k=ac_k, pe_war=pe_war):
                            v.wait_ge(ac1.h, ac_k)
                            if pe_war:
                                v.wait_ge(pe1.h, pe_war)
                            self.V(v, v.tensor_scalar(out=KS2[:, 512 * jh:512 * (jh + 1)],
                                                      in0=KS1[:, 512 * jh:512 * (jh + 1)],
                                                      scalar1=0.0, scalar2=None, op0=ALU.max))
                            c0 = 1024 * r + 512 * jh
                            self.V(v, v.tensor_tensor(out=KFr[:, c0:c0 + 512],
                                                      in0=KS3[:, 512 * jh:512 * (jh + 1)],
                                                      in1=KS2[:, 512 * jh:512 * (jh + 1)], op=ALU.add))
                        block.vector(f_ev2)
                    for jh in range(2):
                        j = 4 * tc + 2 + jh
                        def f_pe(t, tc=tc, jh=jh, j=j):
                            prev = evac_done.get(j - 2, 0)
                            if prev:
                                t.wait_ge(dv1.h, prev)
                            for i in range(NCH):
                                mm = t.matmul(pk[j % 2][:],
                                              XNv[i][:, 128 * tc:128 * (tc + 1)],
                                              wvB[:, 1024 * i + 512 * jh:1024 * i + 512 * (jh + 1)],
                                              start=(i == 0), stop=(i == NCH - 1))
                            pe1.inc(mm)
                        block.tensor(f_pe)
                        pe_kv = pe1.n
                        def f_ev(v, r=r, jh=jh, j=j, pe_kv=pe_kv, pe_war=pe_war):
                            v.wait_ge(pe1.h, max(pe_kv, pe_war))
                            pv = pk[j % 2][:].rearrange("p (one h d) -> p one h d", one=1, h=8)
                            bv = bvB[:, 512 * jh:512 * (jh + 1)].rearrange(
                                "p (one h d) -> p one h d", one=1, h=8)
                            dv1.inc(v.tensor_tensor(
                                out=vpv[:, r:r + 1, 8 * jh:8 * (jh + 1), 0:64],
                                in0=pv, in1=bv, op=ALU.add))
                        block.vector(f_ev)
                        evac_done[j] = dv1.n
                    chain_done[tc] = dv1.n
                    if tc >= 1:
                        emit_moments(tc - 1)
                emit_moments(15)

                pe_all = pe1.n
                def f_evm(v, pe_all=pe_all):
                    v.wait_ge(pe1.h, pe_all)
                    for b in range(6):
                        n_h = 3 if b < 5 else 1
                        dv1.inc(v.tensor_copy(out=KVbuf[:, 390 * b:390 * b + 130 * n_h],
                                              in_=kvp[b][:]))
                block.vector(f_evm)

            dv_kv = dv1.n
            def f_cc(g, dv_kv=dv_kv):
                g.wait_ge(dv1.h, dv_kv)
                gp.inc(g.dma_start(out=cc_in[:], in_=KVbuf[:]), 16)
                g.wait_ge(gp.h, gp.n)
                cc.inc(g.collective_compute("AllReduce", ALU.add,
                                            [[0, 1, 2, 3], [4, 5, 6, 7]],
                                            [cc_in.opt()], [cc_out.opt()]))
            block.gpsimd(f_cc)

        with ExitStack() as ph:
            woB = self.sb("woB", [128, 8 * 1024], BF16, stack=ph)
            QF = self.sb("QF", [128, 8 * 2048], BF16, stack=ph)
            arena = self.sb("arena", [128, 16 * 130], BF16, stack=ph)
            BD = self.sb("BD", [128, 16 * 128], BF16, stack=ph)
            KS = self.sb("KSs", [128, 16 * 2], BF16, stack=ph)
            ar5 = arena.rearrange("p (pair slot half j) -> p pair slot half j",
                                  pair=8, slot=2, half=2)

            def f_load(g):
                gp.inc(g.dma_start(out=woB[:], in_=dram["woT"][:]), 16)
            block.gpsimd(f_load)

            def f_ze(v):
                dv1.inc(v.memset(arena[:], 0.0))
            block.vector(f_ze)
            dv_ze = dv1.n
            cc_n = cc.n

            def f_ar(g, dv_ze=dv_ze, cc_n=cc_n):
                g.wait_ge(cc.h, cc_n)
                g.wait_ge(dv1.h, dv_ze)
                src5 = cc_out.rearrange("d (h2 par slot j) -> d h2 par slot j",
                                        h2=8, par=2, slot=2)
                self.gpB.inc(g.dma_start(out=ar5[0:64, :, :, 0:1, :].opt(),
                                         in_=src5[:, :, 0:1, :, :].opt()), 16)
                self.gpB.inc(g.dma_start(out=ar5[64:128, :, :, 1:2, :].opt(),
                                         in_=src5[:, :, 1:2, :, :].opt()), 16)
            block.gpsimd(f_ar)
            gp_ar = self.gpB.n

            with ExitStack() as qp:
                wqB = self.sb("wqB", [128, 8 * 1024], BF16, stack=qp)
                QS1 = self.sb("QS1", [128, 2 * 512], stack=qp)
                QS2 = self.sb("QS2", [128, 2 * 512], stack=qp)
                QS3 = self.sb("QS3", [128, 2 * 512], stack=qp)
                def f_loadq(g):
                    gp.inc(g.dma_start(out=wqB[:], in_=dram["wqT"][:]), 16)
                block.gpsimd(f_loadq)
                gp_w2 = gp.n
                pq = [self.psum("pq%d" % k, [128, 512], stack=qp) for k in range(2)]
                qs_done = {}; exp_done = {}; qf_done = {}
                for n, (ic, tt) in enumerate([(ic, tt) for ic in range(NCH) for tt in range(4)]):
                    b = n % 2
                    bs = slice(512 * b, 512 * (b + 1))
                    def f_pe(t, ic=ic, tt=tt, b=b, n=n):
                        if n == 0:
                            t.wait_ge(gp.h, gp_w2)
                        elif n >= 2:
                            t.wait_ge(dv1.h, qs_done[n - 2])
                        for i in range(NCH):
                            mm = t.matmul(pq[b][:],
                                          wqB[:, 1024 * i + 128 * ic:1024 * i + 128 * (ic + 1)],
                                          XNv[i][:, 512 * tt:512 * (tt + 1)],
                                          start=(i == 0), stop=(i == NCH - 1))
                        pe1.inc(mm)
                    block.tensor(f_pe)
                    pe_q = pe1.n
                    def f_ev(v, ic=ic, b=b, bs=bs, n=n, pe_q=pe_q):
                        v.wait_ge(pe1.h, pe_q)
                        if n >= 2:
                            v.wait_ge(ac1.h, exp_done[n - 2])
                            v.wait_ge(dv1.h, qf_done[n - 2])
                        v.tensor_scalar(out=QS1[:, bs], in0=pq[b][:], scalar1=bq[:, ic:ic + 1],
                                        scalar2=0.0, op0=ALU.add, op1=ALU.min)
                        dv1.inc(v.tensor_scalar(out=QS2[:, bs], in0=pq[b][:], scalar1=bq[:, ic:ic + 1],
                                                scalar2=0.0, op0=ALU.add, op1=ALU.max))
                    block.vector(f_ev)
                    qs_done[n] = dv1.n
                    def f_exp(sc, b=b, bs=bs, n=n, dv_q=qs_done[n]):
                        sc.wait_ge(dv1.h, dv_q)
                        if n >= 2:
                            sc.wait_ge(dv1.h, qf_done[n - 2])
                        ac1.inc(sc.activation(out=QS3[:, bs], in_=QS1[:, bs], func=AF.Exp))
                    block.scalar(f_exp)
                    exp_done[n] = ac1.n
                    def f_ev2(v, ic=ic, tt=tt, bs=bs, ac_q=exp_done[n]):
                        v.wait_ge(ac1.h, ac_q)
                        c0 = 2048 * ic + 512 * tt
                        dv1.inc(v.tensor_tensor(out=QF[:, c0:c0 + 512],
                                                in0=QS3[:, bs], in1=QS2[:, bs], op=ALU.add))
                    block.vector(f_ev2)
                    qf_done[n] = dv1.n

            ATT = self.sb("ATT", [128, 8 * 2048], BF16, stack=ph)

            dv_bd0 = dv1.n
            def f_bd(v, gp_ar=gp_ar):
                v.wait_ge(self.gpB.h, gp_ar)
                for p in range(8):
                    for slot in range(2):
                        idx = 2 * p + slot
                        src = ar5[:, p:p + 1, slot:slot + 1, :, :]
                        v.tensor_copy(
                            out=BD[:, 128 * idx:128 * (idx + 1)].rearrange(
                                "p (one one2 half j) -> p one one2 half j", one=1, one2=1, half=2),
                            in_=src[:, :, :, :, 0:64])
                        dv1.inc(v.tensor_copy(
                            out=KS[:, 2 * idx:2 * (idx + 1)].rearrange(
                                "p (one one2 half j) -> p one one2 half j", one=1, one2=1, half=2),
                            in_=src[:, :, :, :, 64:65]))
            block.vector(f_bd)

            with ExitStack() as ap:
                patt = [self.psum("patt%d" % k, [128, 512], stack=ap) for k in range(2)]
                pden = [self.psum("pden%d" % k, [2, 512], stack=ap) for k in range(2)]
                pdb = [self.psum("pdb%d" % k, [128, 512], stack=ap) for k in range(2)]
                dn2 = self.sb("dn2", [2, 2 * 512], stack=ap)
                AS2 = self.sb("AS2", [128, 2 * 512], stack=ap)
                att_done = {}
                ln_done = {}
                db_done = {}
                dv1_mx = {}
                for n, (p, slot, tt2) in enumerate([(p, s_, t_) for p in range(8)
                                                    for s_ in range(2) for t_ in range(2)]):
                    idx = 2 * p + slot
                    col = 1024 * slot + 512 * tt2
                    b = n % 2
                    dv_now = dv1.n
                    def f_pe(t, idx=idx, p=p, col=col, n=n, b=b, dv_now=dv_now):
                        if n == 0:
                            t.wait_ge(dv1.h, dv_now)
                        if n >= 2:
                            t.wait_ge(dv2.h, att_done[n - 2])
                        t.matmul(patt[b][:], BD[:, 128 * idx:128 * (idx + 1)],
                                 QF[:, 2048 * p + col:2048 * p + col + 512],
                                 start=True, stop=True)
                        pe1.inc(t.matmul(pden[b][:], KS[:, 2 * idx:2 * (idx + 1)],
                                         QF[:, 2048 * p + col:2048 * p + col + 512],
                                         start=True, stop=True))
                    block.tensor(f_pe)
                    pe_a = pe1.n
                    def f_dn(v, pe_a=pe_a, b=b, n=n):
                        v.wait_ge(pe1.h, pe_a)
                        if n >= 2:
                            v.wait_ge(ac1.h, ln_done[n - 2])
                            v.wait_ge(pe2.h, db_done[n - 2])
                        dv1.inc(v.tensor_scalar(out=dn2[:, 512 * b:512 * (b + 1)],
                                                in0=pden[b][:], scalar1=1e-6,
                                                scalar2=None, op0=ALU.max))
                    block.vector(f_dn)
                    dv1_mx[n] = dv1.n
                    def f_lnx(sc, b=b, n=n, dvw=dv1_mx[n]):
                        sc.wait_ge(dv1.h, dvw)
                        self.V(sc, sc.activation(out=dn2[:, 512 * b:512 * (b + 1)],
                                                 in_=dn2[:, 512 * b:512 * (b + 1)], func=AF.Ln),
                               sem=ac1)
                        ac1.inc(sc.activation(out=dn2[:, 512 * b:512 * (b + 1)],
                                              in_=dn2[:, 512 * b:512 * (b + 1)], func=AF.Exp,
                                              scale=-1.0))
                    block.scalar(f_lnx)
                    ln_done[n] = ac1.n
                    def f_db(t, b=b, acw=ln_done[n]):
                        t.wait_ge(ac1.h, acw)
                        pe2.inc(t.matmul(pdb[b][:], bcp2[:], dn2[:, 512 * b:512 * (b + 1)],
                                         start=True, stop=True))
                    block.tensor(f_db)
                    pe2_db = pe2.n
                    db_done[n] = pe2.n
                    def f_as(sc, pe_a=pe_a, b=b, n=n):
                        sc.wait_ge(pe1.h, pe_a)
                        if n >= 2:
                            sc.wait_ge(dv2.h, att_done[n - 2])
                        ac1.inc(sc.activation(out=AS2[:, 512 * b:512 * (b + 1)],
                                              in_=patt[b][:], func=AF.Copy))
                    block.scalar(f_as)
                    ac_as = ac1.n
                    def f_att(v, p=p, col=col, b=b, ac_as=ac_as, pe2_db=pe2_db):
                        v.wait_ge(ac1.h, ac_as)
                        v.wait_ge(pe2.h, pe2_db)
                        dv2.inc(v.tensor_tensor(out=ATT[:, 2048 * p + col:2048 * p + col + 512],
                                                in0=AS2[:, 512 * b:512 * (b + 1)],
                                                in1=pdb[b][:], op=ALU.mult))
                    block.vector(f_att)
                    att_done[n] = dv2.n

                po = [self.psum("po%d" % k, [128, 512], stack=ap) for k in range(2)]
                dv2_q = dv2.n
                ev_done = {}
                for n, (ic, tt) in enumerate([(ic, tt) for ic in range(NCH) for tt in range(4)]):
                    buf = n % 2
                    def f_pe(t, ic=ic, tt=tt, buf=buf, n=n):
                        if n == 0:
                            t.wait_ge(dv2.h, dv2_q)
                        elif n >= 2:
                            t.wait_ge(dv2.h, ev_done[n - 2])
                        for j in range(NCH):
                            mm = t.matmul(po[buf][:],
                                          woB[:, 1024 * j + 128 * ic:1024 * j + 128 * (ic + 1)],
                                          ATT[:, 2048 * j + 512 * tt:2048 * j + 512 * (tt + 1)],
                                          start=(j == 0), stop=(j == NCH - 1))
                        pe1.inc(mm)
                    block.tensor(f_pe)
                    pe_o = pe1.n
                    def f_ev(v, ic=ic, tt=tt, buf=buf, pe_o=pe_o):
                        v.wait_ge(pe1.h, pe_o)
                        dv2.inc(v.scalar_tensor_tensor(
                            out=Xv[ic][:, 512 * tt:512 * (tt + 1)],
                            in0=po[buf][:], scalar=bo[:, ic:ic + 1],
                            in1=Xv[ic][:, 512 * tt:512 * (tt + 1)],
                            op0=ALU.add, op1=ALU.add))
                    block.vector(f_ev)
                    ev_done[n] = dv2.n

    def l1_ffn(self, Xv, XNv, dram, b1, b2, w1t_dram, w2t_dram, out):
        block = self.block
        pe1, pe2, dv1, dv2, ac1, gp, sy, st = (self.pe1, self.pe2, self.dv1, self.dv2,
                                               self.ac1, self.gp, self.sy, self.st)
        with ExitStack() as ph:
            W1tB = self.sb("W1tB", [128, 8 * 4096], BF16, stack=ph)
            HT = self.sb("HT", [128, 32 * 512], BF16, stack=ph)
            W2tB = [self.sb("W2tB%d" % k, [128, 1024], BF16, stack=ph) for k in range(4)]

            def f_ld(g):
                for j in range(4):
                    gp.inc(g.dma_start(out=W1tB[:, 8192 * j:8192 * (j + 1)],
                                       in_=w1t_dram[:, 8192 * j:8192 * (j + 1)]), 16)
            block.gpsimd(f_ld)
            gp_w1 = gp.n

            for tt in range(4):
                with ExitStack() as tp:
                    phh = [self.psum("phh%d_%d" % (tt, k), [128, 512], stack=tp) for k in range(2)]
                    ac_b = ac1.n; pe_b = pe1.n; dv2_prev = dv2.n
                    def f_pe(t, tt=tt, ac_b=ac_b, dv2_prev=dv2_prev, first=(tt == 0)):
                        if first:
                            t.wait_ge(gp.h, gp_w1)
                        else:
                            t.wait_ge(dv2.h, dv2_prev)
                        for f in range(32):
                            buf = f % 2
                            if f >= 2:
                                t.wait_ge(ac1.h, ac_b + f - 1)
                            for i in range(NCH):
                                mm = t.matmul(phh[buf][:],
                                              W1tB[:, 4096 * i + 128 * f:4096 * i + 128 * (f + 1)],
                                              XNv[i][:, 512 * tt:512 * (tt + 1)],
                                              start=(i == 0), stop=(i == NCH - 1))
                            pe1.inc(mm)
                    block.tensor(f_pe)
                    def f_gelu(sc, pe_b=pe_b):
                        for f in range(32):
                            buf = f % 2
                            sc.wait_ge(pe1.h, pe_b + f + 1)
                            ac1.inc(sc.activation(out=HT[:, 512 * f:512 * (f + 1)],
                                                  in_=phh[buf][:], func=AF.Gelu,
                                                  bias=b1[:, f:f + 1]))
                    block.scalar(f_gelu)
                    ac_done = ac1.n

                with ExitStack() as tp:
                    py8 = [self.psum("py8_%d_%d" % (tt, k), [128, 512], stack=tp) for k in range(8)]
                    gpp = [self.gpA, self.gpB, self.gpC, self.gpD]
                    pe2_b = pe2.n
                    w2_n = {}
                    def f_w2(g, pe2_b=pe2_b, w2_n=w2_n):
                        for f in range(32):
                            if f >= 4:
                                g.wait_ge(pe2.h, pe2_b + f - 3)
                            else:
                                g.wait_ge(pe2.h, pe2_b)
                            gpp[f % 4].inc(g.dma_start(out=W2tB[f % 4][:],
                                                       in_=w2t_dram[:, 1024 * f:1024 * (f + 1)]), 16)
                            w2_n[f] = gpp[f % 4].n
                    block.gpsimd(f_w2)

                    def f_pe2(t, ac_done=ac_done, w2_n=w2_n):
                        t.wait_ge(ac1.h, ac_done)
                        for f in range(32):
                            t.wait_ge(gpp[f % 4].h, w2_n[f])
                            for ic in range(NCH):
                                mm = t.matmul(py8[ic][:], W2tB[f % 4][:, 128 * ic:128 * (ic + 1)],
                                              HT[:, 512 * f:512 * (f + 1)],
                                              start=(f == 0), stop=(f == 31))
                            pe2.inc(mm)
                    block.tensor(f_pe2)
                    pe2_done = pe2.n

                    def f_ev(v, tt=tt, pe2_done=pe2_done):
                        v.wait_ge(pe2.h, pe2_done)
                        for ic in range(NCH):
                            dv2.inc(v.scalar_tensor_tensor(
                                out=Xv[ic][:, 512 * tt:512 * (tt + 1)],
                                in0=py8[ic][:], scalar=b2[:, ic:ic + 1],
                                in1=Xv[ic][:, 512 * tt:512 * (tt + 1)],
                                op0=ALU.add, op1=ALU.add))
                    block.vector(f_ev)
                    dv2_done = dv2.n

                    def f_st(s, tt=tt, dv2_done=dv2_done):
                        s.wait_ge(dv2.h, dv2_done)
                        for ic in range(NCH):
                            st.inc(s.dma_start(out=out[:, T * ic + 512 * tt:T * ic + 512 * (tt + 1)],
                                               in_=Xv[ic][:, 512 * tt:512 * (tt + 1)]), 16)
                    block.sync(f_st)

            def f_fin(s):
                s.wait_ge(st.h, st.n)
            block.sync(f_fin)



def _pack(w):
    R, C = w.shape
    k = R // 128
    return np.ascontiguousarray(w.reshape(k, 128, C).transpose(1, 0, 2).reshape(128, k * C))


def _col_tile(vec):
    k = vec.shape[0] // 128
    return np.ascontiguousarray(vec.reshape(k, 128).T)


def _host_inputs(inputs):
    x = np.ascontiguousarray(inputs["L0_attn_residual"], dtype=np.float32)
    xflat = x.reshape(16384, 1024)
    sh = {}
    sh["kT"] = _pack(np.ascontiguousarray(inputs["L0_attn_w_fc"].reshape(4096, 256).T))
    sh["V"] = _pack(np.ascontiguousarray(inputs["L0_attn_w_proj"].T.reshape(4096, 256)))
    sh["lng"] = _col_tile(inputs["L0_attn_ln_g"])
    sh["lnb"] = _col_tile(inputs["L0_attn_ln_b"])
    sh["W1A"] = _pack(np.ascontiguousarray(inputs["L0_ffn_w1_A"].T))
    sh["W1B"] = np.ascontiguousarray(
        inputs["L0_ffn_w1_B"].reshape(8, 64, 4096).transpose(1, 0, 2).reshape(64, 8 * 4096))
    sh["W2A"] = np.ascontiguousarray(
        inputs["L0_ffn_w2_A"].reshape(8, 64, 32, 128).transpose(3, 0, 2, 1).reshape(128, 8 * 32 * 64))
    sh["W2B"] = _pack(np.ascontiguousarray(inputs["L0_ffn_w2_B"]))
    sh["wqT"] = _pack(np.ascontiguousarray(inputs["L1_attn_wq"].T))
    sh["wkT"] = _pack(np.ascontiguousarray(inputs["L1_attn_wk"].T))
    sh["wvT"] = _pack(np.ascontiguousarray(inputs["L1_attn_wv"].T))
    sh["woT"] = _pack(np.ascontiguousarray(inputs["L1_attn_wo"].T))
    sh["bq"] = _col_tile(inputs["L1_attn_bq"])
    sh["bo"] = _col_tile(inputs["L1_attn_bo"])
    sh["bkr"] = np.ascontiguousarray(inputs["L1_attn_bk"].reshape(1, 1024))
    sh["bvr"] = np.ascontiguousarray(inputs["L1_attn_bv"].reshape(1, 1024))
    sh["W1f"] = _pack(np.ascontiguousarray(inputs["L1_ffn_W1"].T))
    sh["W2f"] = _pack(np.ascontiguousarray(inputs["L1_ffn_W2"].T))
    sh["b1"] = _col_tile(inputs["L1_ffn_b1"])
    sh["b2"] = _col_tile(inputs["L1_ffn_b2"])
    ew = np.ascontiguousarray(inputs["L0_ffn_expert_weights"], dtype=np.float32)
    ei = np.ascontiguousarray(inputs["L0_ffn_expert_indices"], dtype=np.int32)
    in_maps = []
    for c in range(NCORE):
        m = dict(sh)
        idx = np.r_[1024 * c:1024 * (c + 1), 8192 + 1024 * c:8192 + 1024 * (c + 1)]
        m["xT"] = _pack(np.ascontiguousarray(xflat[idx].T))
        m["ew"] = np.ascontiguousarray(ew[1024 * c:1024 * (c + 1)].reshape(1, 2048))
        m["ei"] = np.ascontiguousarray(ei[1024 * c:1024 * (c + 1)].reshape(1, 2048))
        in_maps.append(m)
    return in_maps


def _unpack_out(res, key="outT"):
    o = np.empty((16384, 1024), dtype=np.float32)
    for c in range(NCORE):
        arr = np.asarray(res[c][key])
        outT = arr.reshape(128, 8, 2048).transpose(1, 0, 2).reshape(1024, 2048)
        idx = np.r_[1024 * c:1024 * (c + 1), 8192 + 1024 * c:8192 + 1024 * (c + 1)]
        o[idx] = outT.T
    return o.reshape(4, 4096, 1024)


_KEYS = {
    (0, 1): None,
    (0,): ["xT", "kT", "V", "lng", "lnb", "W1A", "W1B", "W2A", "W2B", "ew", "ei"],
    (1,): ["xT", "wqT", "wkT", "wvT", "woT", "bq", "bo", "bkr", "bvr",
           "W1f", "W2f", "b1", "b2"],
}


def run_layers(in_maps, layers=(0, 1), dump=None, trace=False):
    nc = Build(layers=layers, dump=dump).build()
    keep = _KEYS[tuple(layers)]
    maps = [{k: v for k, v in m.items() if keep is None or k in keep} for m in in_maps]
    return run_bass_kernel_spmd(nc, maps, core_ids=list(range(NCORE)), trace=trace)


def kernel(**inputs):
    in_maps = _host_inputs(inputs)
    res = run_layers(in_maps, layers=(0, 1))
    return _unpack_out(res.results)
